# version 76
# baseline (speedup 1.0000x reference)
import numpy as np
import ml_dtypes
from contextlib import ExitStack

import concourse.bass as bass
import concourse.mybir as mybir
from concourse.bass_utils import run_bass_kernel_spmd

F32 = mybir.dt.float32
BF16 = mybir.dt.bfloat16
AF = mybir.ActivationFunctionType
ALU = mybir.AluOpType
AX = mybir.AxisListType

TP = 128
DLOW = 64
NE, NR = 8, 16
ERK = NE * NR
K2 = DLOW + 1
EPS = 1e-5
E_HALF = float(np.exp(-0.5))
CC_ONE, CC_ZERO, CC_HALF = 0, 1, 2
DEBUG_LOG = None


class _Stream:
    def __init__(self, name, sem, dma):
        self.name, self.sem, self.dma = name, sem, dma
        self.count = 0


class Buf:
    __slots__ = ("ap", "writer", "readers")

    def __init__(self, ap):
        self.ap, self.writer, self.readers = ap, None, []


class _Op:
    __slots__ = ("idx", "st", "fns", "deps", "est", "prio", "start", "fin",
                 "tok")

    def __init__(self, idx, st, fns, deps, est):
        self.idx, self.st, self.fns, self.deps, self.est = (
            idx, st, fns, deps, est)
        self.prio = 0.0
        self.start = self.fin = 0.0
        self.tok = None


SCHED_SEED = 911


class Prog:

    SEQ_OF = {"v": "v", "s": "s", "t": "t", "g": "g", "gd": "g", "gd2": "g",
              "gd3": "g",
              "syc": "sy", "sf0": "sy", "sf1": "sy", "sx0": "sy", "sx1": "sy",
              "sst": "sy"}
    XLAT = 340.0
    OPFIX = 90.0

    def __init__(self, nc, stack):
        self.nc = nc
        self.streams = {}
        for nm, dma in [("v", False), ("s", False), ("t", False), ("g", False),
                        ("gd", True), ("gd2", True), ("gd3", True),
                        ("syc", True),
                        ("sf0", True), ("sf1", True), ("sx0", True),
                        ("sx1", True), ("sst", True)]:
            sem = stack.enter_context(nc.semaphore(name="sem_" + nm))
            self.streams[nm] = _Stream(nm, sem, dma)
        self.ops = []
        self.final_stream = None

    def emit(self, st_name, fn, reads=(), writes=(), est=400.0):
        fns = tuple(fn) if isinstance(fn, (list, tuple)) else (fn,)
        deps = set()
        for b in reads:
            if b.writer is not None:
                deps.add(b.writer)
        for b in writes:
            if b.writer is not None:
                deps.add(b.writer)
            deps.update(b.readers)
        op = _Op(len(self.ops), st_name, fns, deps, float(est))
        self.ops.append(op)
        for b in reads:
            b.readers.append(op.idx)
        for b in writes:
            b.writer = op.idx
            b.readers = []
        return op

    def schedule(self):
        ops = self.ops
        n = len(ops)
        barrier_members = {}
        for op in ops:
            if op.st in ("gd", "gd2", "gd3", "syc"):
                barrier_members.setdefault(op.st, set()).add(op.idx)
        for op in ops:
            ext = None
            for d in op.deps:
                dst = ops[d].st
                if dst in barrier_members and op.st != dst:
                    ext = (ext or set()) | barrier_members[dst]
            if ext:
                op.deps = (op.deps | ext) - {op.idx}
        succs = [[] for _ in range(n)]
        for op in ops:
            for d in op.deps:
                succs[d].append(op.idx)
        if SCHED_SEED:
            rng = np.random.default_rng(SCHED_SEED)
            jit = rng.uniform(0.92, 1.08, n)
        else:
            jit = None
        for i in range(n - 1, -1, -1):
            p = ops[i].est
            if jit is not None:
                p *= jit[i]
            m = 0.0
            for s in succs[i]:
                if ops[s].prio > m:
                    m = ops[s].prio
            ops[i].prio = p + m
        for op in ops:
            if op.st == "gd3":
                op.prio = 3e12
            elif op.st == "gd":
                op.prio = 2e12
            elif op.st == "gd2":
                op.prio = 1e12
        SLACK = 400.0
        seqs = sorted(set(self.SEQ_OF.values()))
        seq_time = {q: 0.0 for q in seqs}
        ndeps = [len(op.deps) for op in ops]
        ready = {q: set() for q in seqs}
        est_ready = [0.0] * n
        for op in ops:
            if not op.deps:
                ready[self.SEQ_OF[op.st]].add(op.idx)
        seq_lists = {q: [] for q in seqs}
        scheduled = 0
        while scheduled < n:
            best = None
            for q in seqs:
                rq = ready[q]
                if not rq:
                    continue
                emin = min(est_ready[i] for i in rq)
                t0 = max(seq_time[q], emin)
                cand, cprio = None, -1.0
                for i in rq:
                    if est_ready[i] <= t0 + SLACK and ops[i].prio > cprio:
                        cand, cprio = i, ops[i].prio
                start = max(seq_time[q], est_ready[cand])
                if (best is None or start < best[0] - 1e-9 or
                        (abs(start - best[0]) < 1e-9 and cprio > best[3])):
                    best = (start, q, cand, cprio)
            start, q, i, _ = best
            op = ops[i]
            ready[q].discard(i)
            op.start = start
            op.fin = start + op.est + self.OPFIX
            seq_time[q] = op.fin
            seq_lists[q].append(i)
            scheduled += 1
            for s in succs[i]:
                ndeps[s] -= 1
                if ndeps[s] == 0:
                    est_ready[s] = max(
                        [0.0] + [ops[d].fin + self.XLAT
                                 for d in ops[s].deps])
                    ready[self.SEQ_OF[ops[s].st]].add(s)
        assert scheduled == n, (scheduled, n)
        import os
        if os.environ.get("SCHED_REPORT"):
            busy = {}
            for op in ops:
                busy[self.SEQ_OF[op.st]] = busy.get(self.SEQ_OF[op.st], 0) + op.est
            print("SCHED makespan %.1f us; busy: %s" % (
                max(o.fin for o in ops) / 1e3,
                {k: round(v / 1e3, 1) for k, v in sorted(busy.items())}))
            import traceback as _tb
            last = max(ops, key=lambda o: o.fin)
            chain = []
            cur = last
            while cur is not None:
                chain.append(cur)
                best = None
                for d in cur.deps:
                    if abs(ops[d].fin + self.XLAT - cur.start) < 1e-6:
                        best = ops[d]
                        break
                if best is None:
                    q = self.SEQ_OF[cur.st]
                    cand = [o for o in ops
                            if self.SEQ_OF[o.st] == q and o.fin <= cur.start
                            and o is not cur]
                    if cand:
                        b = max(cand, key=lambda o: o.fin)
                        if abs(b.fin - cur.start) < 1e-6:
                            best = b
                cur = best
            chain.reverse()
            import collections
            agg = collections.Counter()
            for o in chain:
                agg[(o.st, round(o.est, -1))] += 1
            print("SCHED critical chain: %d ops" % len(chain))
            for (stn, e), cnt in agg.most_common(20):
                print("   %s est=%s x%d  (%.1f us)" % (stn, e, cnt,
                                                       e * cnt / 1e3))
        for q in seqs:
            for i in seq_lists[q]:
                op = ops[i]
                st = self.streams[op.st]
                st.count += 16 if st.dma else 1
                op.tok = (op.st, st.count)
        self.seq_ops = {q: [] for q in seqs}
        for q in seqs:
            waited = {}
            for i in seq_lists[q]:
                op = ops[i]
                need = {}
                for d in op.deps:
                    snm, val = ops[d].tok
                    if snm == op.st and self.streams[snm].dma:
                        continue
                    if snm in ("gd", "gd2", "gd3", "syc"):
                        val = self.streams[snm].count
                    if need.get(snm, 0) < val:
                        need[snm] = val
                waits = []
                for snm, val in sorted(need.items()):
                    if waited.get(snm, 0) < val:
                        waited[snm] = val
                        waits.append((self.streams[snm].sem, val))
                sem = self.streams[op.st].sem
                inc = 16 if self.streams[op.st].dma else 1

                def run(eng, _w=tuple(waits), _f=op.fns, _s=sem, _i=inc):
                    for s, v in _w:
                        eng.wait_ge(s, v)
                    last = None
                    for f in _f:
                        last = f(eng)
                    last.then_inc(_s, _i)

                self.seq_ops[q].append(run)
        fw = []
        for nm in ("sst", "sx0", "sx1", "syc", "sf0", "sf1"):
            st = self.streams[nm]
            if st.count:
                fw.append((st.sem, st.count))

        def run_final(eng, _w=tuple(fw)):
            for s, v in _w:
                eng.wait_ge(s, v)

        self.seq_ops["sy"].append(run_final)


def out_chunks(C):
    return [(i, min(i + 512, C)) for i in range(0, C, 512)]


def build_nc(C, HP, RP):
    NCH = C // 128
    H = C // 64
    NP = HP + RP
    ROWS = NP * TP
    NW1 = 4 * DLOW
    UGW = ERK + NE
    OC = out_chunks(C)

    nc = bass.Bass("TRN2", target_bir_lowering=False)
    dt = nc.dram_tensor

    x_in = dt("x", [ROWS, C], F32, kind="ExternalInput").ap()
    wtT_in = dt("wtT", [C, C], F32, kind="ExternalInput").ap()
    wr_in = dt("wr", [C, C], F32, kind="ExternalInput").ap()
    wk_in = dt("wk", [C, C], F32, kind="ExternalInput").ap()
    wv_in = dt("wv", [C, C], F32, kind="ExternalInput").ap()
    wo_in = dt("wo", [C, C], F32, kind="ExternalInput").ap()
    ug_in = dt("ug", [C, UGW], F32, kind="ExternalInput").ap()
    vcat_in = dt("vcat", [ERK, C], F32, kind="ExternalInput").ap()
    w1cat_in = dt("w1cat", [C, NW1], F32, kind="ExternalInput").ap()
    w2aug_in = dt("w2aug", [K2, H], F32, kind="ExternalInput").ap()
    a2aug_in = dt("a2aug", [K2, H], F32, kind="ExternalInput").ap()
    v2aug_in = dt("v2aug", [K2, C], F32, kind="ExternalInput").ap()
    g2z_in = dt("g2z", [K2, C], F32, kind="ExternalInput").ap()
    xmix_in = dt("xmix", [128, 6 * NCH], F32, kind="ExternalInput").ap()
    kkt_in = dt("kkt", [128, C], F32, kind="ExternalInput").ap()
    lnwc_in = dt("lnwc", [128, NCH], F32, kind="ExternalInput").ap()
    lnw_tile_in = dt("lnw_tile", [128, C], F32, kind="ExternalInput").ap()
    lnb_tile_in = dt("lnb_tile", [128, C], F32, kind="ExternalInput").ap()
    rk_tile_in = dt("rk_tile", [128, C], F32, kind="ExternalInput").ap()
    identb_in = dt("identb", [128, 128], BF16, kind="ExternalInput").ap()
    identf_in = dt("identf", [128, 128], F32, kind="ExternalInput").ap()
    eall_in = dt("eall", [128, C], F32, kind="ExternalInput").ap()
    cconst_in = dt("cconst", [128, 4], F32, kind="ExternalInput").ap()
    ones512_in = dt("ones512", [1, 4 * TP], BF16, kind="ExternalInput").ap()
    y_out = dt("y", [RP * TP, C], F32, kind="ExternalOutput").ap()

    stack = ExitStack()

    _nn = [0]

    def sb(shape, dty):
        _nn[0] += 1
        return Buf(stack.enter_context(
            nc.sbuf_tensor("sb%d" % _nn[0], shape, dty))[:])

    def psum(shape):
        _nn[0] += 1
        return Buf(stack.enter_context(
            nc.psum_tensor("ps%d" % _nn[0], shape, F32))[:])

    with stack:
        wt = sb([128, NCH * C], BF16)
        wr = sb([128, NCH * C], BF16)
        wk = sb([128, NCH * C], BF16)
        wv = sb([128, NCH * C], BF16)
        wo = sb([128, NCH * C], BF16)
        ug = sb([128, NCH * UGW], BF16)
        vcat = sb([ERK, C], BF16)
        w1cat = sb([128, NCH * NW1], BF16)
        w1x = sb([128, NCH * NW1], BF16)
        w2aug = sb([K2, H], BF16)
        a2aug = sb([K2, H], BF16)
        v2aug = sb([K2, C], BF16)
        g2z = sb([K2, C], BF16)
        xmix = sb([128, 6 * NCH], BF16)
        lnwc = sb([128, NCH], F32)
        lnwh = sb([128, NCH], F32)
        lblw = sb([128, C], BF16)
        rkt = sb([128, C], F32)
        idb = sb([128, 128], BF16)
        eall = sb([128, C], BF16)
        cc = sb([128, 4], F32)
        powc = sb([128, 16], F32)

        xin = [sb([128, C], F32) for _ in range(2)]
        x2 = [sb([128, C], F32) for _ in range(2)]
        x3 = sb([128, C], F32)
        xn_tm = sb([128, C], BF16)
        xc1 = sb([128, C], BF16)
        xnT = [sb([128, C], BF16) for _ in range(2)]
        xnT_ext = [sb([128, NCH * (TP + 2)], BF16) for _ in range(2)]
        sxT = [sb([128, C], BF16) for _ in range(2)]
        mixT = [sb([128, C], BF16) for _ in range(3)]
        th = sb([128, NW1], BF16)
        thT = [sb([K2, 4 * TP], BF16) for _ in range(2)]
        xg2 = sb([128, C], BF16)
        xg2T = sb([128, C], BF16)
        xn3_tm = xg2
        xn3T = xg2T
        tw = sb([128, ERK], BF16)
        twT = sb([ERK, TP], BF16)
        tw2 = sb([128, ERK], BF16)
        twT2 = sb([ERK, TP], BF16)
        tg_sb = sb([128, C], BF16)

        kpre = [sb([128, C], BF16) for _ in range(2)]
        kk8 = [sb([128, C], F32) for _ in range(2)]
        v_sb = [sb([128, C], BF16) for _ in range(2)]
        u2 = sb([128, C], BF16)
        u3 = u2

        states = [sb([128, C], F32) for _ in range(2)]
        r_sb = [sb([128, C], F32) for _ in range(2)]
        rkk = sb([128, C], F32)
        xwkv = sb([128, C], F32)
        xout = sb([128, C], F32)
        stgf = [x2[0], x2[1]]
        lnw_t = rkk
        sc1 = x3
        vgt = xg2
        kkt = x3
        kvp = mixT[2]
        stb = xg2T
        rkkr = xwkv
        u1 = xwkv
        bon = rkk

        ln12 = [sb([128, 12], F32) for _ in range(2)]
        lnmv = [sb([128, 2], F32) for _ in range(2)]
        lnst = [[dict(sum=sb([128, 1], F32), ssq=sb([128, 1], F32),
                      mean=sb([128, 1], F32), rstd=sb([128, 1], F32),
                      veps=sb([128, 1], F32), msq=sb([128, 1], F32),
                      rstdh=sb([128, 1], F32))
                 for _ in range(2)] for _ in range(3)]
        gexp = sb([128, NE], F32)
        gsum = sb([128, 1], F32)
        grec = sb([128, 1], F32)
        gate = sb([128, NE], F32)
        gexp2 = sb([128, NE], F32)
        gsum2 = sb([128, 1], F32)
        grec2 = sb([128, 1], F32)
        gate2 = sb([128, NE], F32)
        w_fm = [sb([128, TP], BF16) for _ in range(2)]
        a_t = sb([128, H], F32)
        a8 = sb([128, H], F32)
        ssh = sb([128, H], F32)
        rna = sb([128, H], F32)
        sshe = sb([128, H], F32)
        rn = sb([128, H], F32)
        dot = sshe
        gs1 = sb([128, H], F32)
        gd2 = sb([128, H], F32)
        gm = ssh
        gaM = rna
        grs = rn
        gbe = gd2

        psT = psum([128, 1024])
        psB = psum([128, 1024])
        psW = psum([128, 1024])
        psC = psum([128, 1024])
        quarters = lambda ps: [Buf(ps.ap[:, i * 256:(i + 1) * 256])
                               for i in range(4)]
        psT_r = quarters(psT)
        psB_r = quarters(psB)
        psW_r = quarters(psW)
        psC_r = quarters(psC)

        def bq(ps_r, lo, hi):
            return ps_r[lo // 256:(hi + 255) // 256]

        P = Prog(nc, stack)
        E = P.emit

        one_c = cc.ap[:, CC_ONE:CC_ONE + 1]
        zero_c = cc.ap[:, CC_ZERO:CC_ZERO + 1]
        half_c = cc.ap[:, CC_HALF:CC_HALF + 1]
        zero16 = cc.ap[0:H, CC_ZERO:CC_ZERO + 1]
        half16 = cc.ap[0:H, CC_HALF:CC_HALF + 1]

        def bc3(small_ap, nh, wide):
            return small_ap.broadcast_to([128, nh, wide])

        def re3(ap_, nh):
            return ap_.rearrange("p (h d) -> p h d", h=nh)

        xe3v = [xnT_ext[i].ap.rearrange("p (k c) -> p k c", k=NCH)
                for i in range(2)]

        E("sx0", lambda e: e.dma_start(out=xin[0].ap, in_=x_in[0:TP, :]),
          writes=[xin[0]])
        E("sx1", lambda e: e.dma_start(out=xin[1].ap, in_=x_in[TP:2 * TP, :]),
          writes=[xin[1]])
        lnb_stg = xout
        for buf, src in [(lnb_stg, lnb_tile_in), (lnw_t, lnw_tile_in),
                         (rkt, rk_tile_in),
                         (cc, cconst_in), (kkt, kkt_in),
                         (lnwc, lnwc_in)]:
            E("syc", lambda e, b=buf, s=src: e.dma_start(out=b.ap, in_=s),
              writes=[buf])
        E("syc", lambda e: e.dma_start(out=thT[0].ap[DLOW:K2, :],
                                       in_=ones512_in), writes=[thT[0]])
        E("syc", lambda e: e.dma_start(out=thT[1].ap[DLOW:K2, :],
                                       in_=ones512_in), writes=[thT[1]])

        def cload(stream, group, buf, src, chunked=False):
            s = src.rearrange("(k p) c -> p k c", p=128) if chunked else src
            o = (buf.ap.rearrange("p (k c) -> p k c", k=NCH) if chunked
                 else buf.ap)
            E(stream, lambda e, _o=o, _s=s: e.dma_start(out=_o, in_=_s),
              writes=[buf])
            group.append(buf)

        grpA, grpB = [], []
        cload("gd", grpA, ug, ug_in, True)
        cload("gd", grpA, vcat, vcat_in)
        cload("gd", grpA, idb, identb_in)
        cload("gd", grpA, xmix, xmix_in)
        cload("gd", grpA, eall, eall_in)
        cload("gd2", grpB, w1cat, w1cat_in, True)
        cload("gd2", grpB, w2aug, w2aug_in)
        cload("gd2", grpB, a2aug, a2aug_in)
        cload("gd2", grpB, v2aug, v2aug_in)
        cload("gd2", grpB, g2z, g2z_in)
        cload("gd2", grpB, wr, wr_in, True)
        cload("gd2", grpB, wv, wv_in, True)

        E("v", lambda e: e.memset(powc.ap, -0.5), writes=[powc])
        pw1 = powc.ap[:, 0:1]
        pwH = powc.ap[:, 0:H]
        E("v", lambda e: e.tensor_scalar(lnwh.ap, lnwc.ap, 0.5, None,
                                         ALU.mult),
          reads=[lnwc], writes=[lnwh])
        E("v", lambda e: e.reciprocal(lnw_t.ap, lnw_t.ap),
          reads=[lnw_t], writes=[lnw_t])
        E("v", lambda e: e.tensor_tensor(lblw.ap, lnb_stg.ap, lnw_t.ap,
                                         ALU.mult),
          reads=[lnb_stg, lnw_t], writes=[lblw])
        E("v", lambda e: e.tensor_scalar(rkt.ap, rkt.ap, 0.125, None,
                                         ALU.mult),
          reads=[rkt], writes=[rkt])
        for i in range(2):
            E("v", lambda e, _b=xnT_ext[i]: e.memset(_b.ap, 0.0),
              writes=[xnT_ext[i]])
            E("v", lambda e, _b=w_fm[i]: e.memset(_b.ap, 0.0),
              writes=[w_fm[i]])
        mix_bc = (xmix.ap[:, 2 * NCH:6 * NCH]
                  .rearrange("p (m k) -> p k m", m=4)[:, :, :, None]
                  .broadcast_to([128, NCH, 4, DLOW]))
        E("v", lambda e: e.tensor_tensor(
            w1x.ap.rearrange("p (k m d) -> p k m d", k=NCH, m=4),
            w1cat.ap.rearrange("p (k m d) -> p k m d", k=NCH, m=4),
            mix_bc, ALU.mult),
          reads=[w1cat, xmix], writes=[w1x])

        def folded_load(dst, src, fold):
            for k in range(NCH):
                s = stgf[k % 2]
                E("sf%d" % (k % 2), lambda e, _s=s, _k=k, _src=src:
                  e.dma_start(out=_s.ap,
                              in_=_src[_k * 128:(_k + 1) * 128, :]),
                  writes=[s])
                fold(k, s, dst)

        for k in range(NCH):
            E("gd3", lambda e, _k=k: e.dma_start(
                out=wt.ap[:, _k * C:(_k + 1) * C],
                in_=wtT_in[_k * 128:(_k + 1) * 128, :]),
              writes=[wt])
        folded_load(wk, wk_in, lambda k, s, d: E(
            "g", lambda e, _k=k, _s=s, _d=d: e.tensor_tensor(
                _d.ap[:, _k * C:(_k + 1) * C], _s.ap, kkt.ap, ALU.mult),
            reads=[s, kkt], writes=[d]))
        folded_load(wo, wo_in, lambda k, s, d: E(
            "s", lambda e, _k=k, _s=s, _d=d: e.activation(
                _d.ap[:, _k * C:(_k + 1) * C], _s.ap, AF.Copy,
                bias=0.0, scale=lnwh.ap[:, _k:_k + 1]),
            reads=[s, lnwh, cc], writes=[d]))

        def stats_from_accum(st):
            E("v", [lambda e: e.tensor_scalar(st["mean"].ap, st["sum"].ap,
                                              -1.0 / C, None, ALU.mult),
                    lambda e: e.tensor_scalar(st["veps"].ap, st["ssq"].ap,
                                              1.0 / C, EPS, ALU.mult,
                                              ALU.add)],
              reads=[st["sum"], st["ssq"]],
              writes=[st["mean"], st["veps"]], est=250)
            E("v", lambda e: e.tensor_tensor(st["msq"].ap, st["mean"].ap,
                                             st["mean"].ap, ALU.mult),
              reads=[st["mean"]], writes=[st["msq"]], est=100)
            E("v", lambda e: e.tensor_tensor(st["veps"].ap, st["veps"].ap,
                                             st["msq"].ap, ALU.subtract),
              reads=[st["veps"], st["msq"]], writes=[st["veps"]], est=100)
            E("g", lambda e: e.tensor_tensor(st["rstd"].ap, st["veps"].ap,
                                             pw1, ALU.pow),
              reads=[st["veps"], powc], writes=[st["rstd"]], est=1500)
            E("v", lambda e: e.tensor_scalar(st["rstdh"].ap, st["rstd"].ap,
                                             0.5, None, ALU.mult),
              reads=[st["rstd"]], writes=[st["rstdh"]], est=100)

        def stats_from_bn(src_buf, st, par):
            wdt = C // 2
            l12, lmv = ln12[par], lnmv[par]
            E("v", [lambda e: e.bn_stats(l12.ap[:, 0:6],
                                         src_buf.ap[:, 0:wdt]),
                    lambda e: e.bn_stats(l12.ap[:, 6:12],
                                         src_buf.ap[:, wdt:C])],
              reads=[src_buf], writes=[l12])
            E("v", lambda e: e.bn_aggr(lmv.ap, l12.ap),
              reads=[l12], writes=[lmv])
            E("v", lambda e: e.tensor_scalar(st["mean"].ap, lmv.ap[:, 0:1],
                                             -1.0, None, ALU.mult),
              reads=[lmv], writes=[st["mean"]])
            E("v", lambda e: e.tensor_scalar(st["veps"].ap, lmv.ap[:, 1:2],
                                             EPS, None, ALU.add),
              reads=[lmv], writes=[st["veps"]])
            E("g", lambda e: e.tensor_tensor(st["rstd"].ap, st["veps"].ap,
                                             pw1, ALU.pow),
              reads=[st["veps"], powc], writes=[st["rstd"]], est=1500)
            E("v", lambda e: e.tensor_scalar(st["rstdh"].ap, st["rstd"].ap,
                                             0.5, None, ALU.mult),
              reads=[st["rstd"]], writes=[st["rstdh"]], est=100)

        def transpose_blocks(src_ap, n_blk, ps, col0=0):
            fns = []
            for k in range(n_blk):
                fns.append(lambda e,
                           _s=src_ap[:, k * 128:(k + 1) * 128],
                           _d=ps.ap[:, col0 + k * 128:col0 + (k + 1) * 128]:
                           e.matmul(_d, _s, idb.ap, start=True, stop=True))
            return fns

        def big_mm_group(lhs_ap_of, w_buf, ps_buf, close=True):
            fns = []
            for k in range(NCH):
                lhs = lhs_ap_of(k)
                for ci, (lo, hi) in enumerate(OC):
                    fns.append(lambda e, _l=lhs,
                               _r=w_buf.ap[:, k * C + lo:k * C + hi],
                               _d=ps_buf.ap[:, lo:hi],
                               _st=(k == 0),
                               _sp=(close and k == NCH - 1):
                               e.matmul(_d, _l, _r, start=_st, stop=_sp))
            return fns

        def ffn_fn(x_buf, out_buf, st, fb, bps, bps_r, aps, aps_r,
                   compact_aux=False):
            E("v", lambda e: e.tensor_scalar(fb["xc"].ap, x_buf.ap,
                                             st["mean"].ap, None,
                                             ALU.add),
              reads=[x_buf, st["mean"]], writes=[fb["xc"]])
            if compact_aux:
                nh = NCH // 2
                for r in range(2):
                    fns = []
                    for k in range(r * nh, (r + 1) * nh):
                        fns.append(lambda e,
                                   _s=fb["xc"].ap[:, k * 128:(k + 1) * 128],
                                   _d=aps.ap[:, (k % nh) * 128:
                                             (k % nh + 1) * 128]:
                                   e.matmul(_d, _s, idb.ap, start=True,
                                            stop=True))
                    E("t", fns, reads=[fb["xc"], idb], writes=aps_r[0:2],
                      est=nh * 95)
                    E("s", lambda e, _r=r, _w=nh * 128: e.copy(
                        fb["xT"].ap[:, _r * _w:(_r + 1) * _w],
                        aps.ap[:, 0:_w]),
                      reads=aps_r[0:2], writes=[fb["xT"]], est=500)
            else:
                E("t", transpose_blocks(fb["xc"].ap, NCH, aps),
                  reads=[fb["xc"], idb], writes=aps_r, est=NCH * 95)
                E("s", lambda e: e.copy(fb["xT"].ap, aps.ap[:, 0:C]),
                  reads=aps_r, writes=[fb["xT"]], est=950)
            twc = 384 if compact_aux else 512
            twq = aps_r[1] if compact_aux else aps_r[2]
            fns = []
            for k in range(NCH):
                fns.append(lambda e,
                           _l=fb["xT"].ap[:, k * 128:(k + 1) * 128],
                           _r=ug.ap[:, k * UGW:(k + 1) * UGW],
                           _st=(k == 0), _sp=(k == NCH - 1):
                           e.matmul(aps.ap[:, 0:UGW], _l, _r,
                                    start=_st, stop=_sp))
            E("t", fns, reads=[fb["xT"], ug], writes=[aps_r[0]], est=800)
            for lo, hi in OC:
                fns = []
                for k in range(NCH):
                    fns.append(lambda e,
                               _l=fb["xT"].ap[:, k * 128:(k + 1) * 128],
                               _r=wt.ap[:, k * C + lo:k * C + hi],
                               _d=bps.ap[:, lo:hi], _st=(k == 0):
                               e.matmul(_d, _l, _r, start=_st, stop=False))
                E("t", fns, reads=[fb["xT"], wt],
                  writes=bq(bps_r, lo, hi), est=NCH * (hi - lo) * 0.42 + 150)
            E("s", lambda e: e.activation(fb["gexp"].ap, aps.ap[:, ERK:UGW],
                                          AF.Exp, bias=zero_c,
                                          scale=st["rstd"].ap,
                                          accum_out=fb["gsum"].ap),
              reads=[aps_r[0], st["rstd"], cc],
              writes=[fb["gexp"], fb["gsum"]])
            E("v", lambda e: e.reciprocal(fb["grec"].ap, fb["gsum"].ap),
              reads=[fb["gsum"]], writes=[fb["grec"]])
            E("v", lambda e: e.tensor_scalar(fb["gate"].ap, fb["gexp"].ap,
                                             fb["grec"].ap, 1.0 / NR,
                                             ALU.mult, ALU.mult),
              reads=[fb["gexp"], fb["grec"]], writes=[fb["gate"]])
            E("v", lambda e: e.tensor_tensor(
                re3(fb["tw"].ap, NE), re3(aps.ap[:, 0:ERK], NE),
                bc3(fb["gate"].ap, NE, NR), ALU.mult),
              reads=[aps_r[0], fb["gate"]], writes=[fb["tw"]])
            E("t", lambda e: e.matmul(aps.ap[:, twc:twc + 128], fb["tw"].ap,
                                      idb.ap, start=True, stop=True),
              reads=[fb["tw"], idb], writes=[twq], est=200)
            E("s", lambda e: e.copy(fb["twT"].ap, aps.ap[:, twc:twc + 128]),
              reads=[twq], writes=[fb["twT"]], est=300)
            for lo, hi in OC:
                E("t", lambda e, _r=vcat.ap[:, lo:hi], _d=bps.ap[:, lo:hi]:
                  e.matmul(_d, fb["twT"].ap, _r, start=False, stop=True),
                  reads=[fb["twT"], vcat], writes=bq(bps_r, lo, hi),
                  est=900)
            E("v", lambda e: e.scalar_tensor_tensor(
                out_buf.ap, bps.ap[:, 0:C], st["rstdh"].ap, x_buf.ap,
                ALU.mult, ALU.add),
              reads=bps_r + [x_buf, st["rstdh"]], writes=[out_buf], est=1300)

        def mix_stream(mslot, mixcol, par, eng="v"):
            mb = mixT[mslot]
            xe3_ = xe3v[par]
            m3 = mb.ap.rearrange("p (k c) -> p k c", k=NCH)
            mc = (xmix.ap[:, mixcol * NCH:(mixcol + 1) * NCH]
                  [:, :, None].broadcast_to([128, NCH, 128]))
            est = 1150 if eng == "v" else 2500
            E(eng, lambda e, _sx=sxT[par]: e.tensor_tensor(
                m3, _sx.ap.rearrange("p (k c) -> p k c", k=NCH), mc,
                ALU.mult),
              reads=[sxT[par], xmix], writes=[mb], est=est)
            E(eng, lambda e: e.tensor_tensor(m3, m3, xe3_[:, :, 2:TP + 2],
                                             ALU.add),
              reads=[mb, xnT_ext[par]], writes=[mb], est=est)
            return mb

        FB1 = [dict(xc=xc1, xT=xnT[i], tw=tw, twT=twT, gexp=gexp,
                    gsum=gsum, grec=grec, gate=gate) for i in range(2)]
        FB2 = dict(xc=xn3_tm, xT=xn3T, tw=tw2, twT=twT2, gexp=gexp2,
                   gsum=gsum2, grec=grec2, gate=gate2)


        def proj_stream(mslot, wbuf, dst, est_mm):
            for lo, hi in OC:
                w = hi - lo
                qs = psT_r[2:2 + (w + 255) // 256]
                fns = []
                for k in range(NCH):
                    fns.append(lambda e,
                               _l=mixT[mslot].ap[:, k * 128:(k + 1) * 128],
                               _r=wbuf.ap[:, k * C + lo:k * C + hi],
                               _d=psT.ap[:, 512:512 + w],
                               _st=(k == 0), _sp=(k == NCH - 1):
                               e.matmul(_d, _l, _r, start=_st, stop=_sp))
                E("t", fns, reads=[mixT[mslot], wbuf],
                  writes=qs, est=est_mm)
                E("s", lambda e, _lo=lo, _hi=hi, _w=w, _d=dst: e.copy(
                    _d.ap[:, _lo:_hi], psT.ap[:, 512:512 + _w]),
                  reads=qs, writes=[dst], est=550)

        def stage_a(p):
            real = p >= HP
            par = p % 2
            xb, x2b = xin[par], x2[par]
            kk8b, vsb, rsb, thTb = (kk8[par], v_sb[par], r_sb[par],
                                    thT[par])
            scur, sprev = states[par], states[1 - par]
            kpreb = kpre[par]
            wfmb = w_fm[par]
            sxTb = sxT[par]
            xe, xep = xe3v[par], xe3v[1 - par]
            lnA, lnB = lnst[0][par], lnst[1][par]
            E("s", lambda e: e.activation(xc1.ap, xb.ap, AF.Copy,
                                          bias=0.0, scale=one_c,
                                          accum_out=lnA["sum"].ap),
              reads=[xb, cc], writes=[xc1, lnA["sum"]], est=1150)
            E("s", lambda e: e.activation(xn_tm.ap, xb.ap, AF.Square,
                                          bias=zero_c, scale=one_c,
                                          accum_out=lnA["ssq"].ap),
              reads=[xb, cc], writes=[xn_tm, lnA["ssq"]], est=1300)
            stats_from_accum(lnA)
            ffn_fn(xb, x2b, lnA, FB1[par], psB, psB_r, psT, psT_r,
                   compact_aux=True)
            stats_from_bn(x2b, lnB, par)
            if p + 2 < NP:
                E("sx%d" % par, lambda e, _r=(p + 2) * TP, _b=xb:
                  e.dma_start(out=_b.ap, in_=x_in[_r:_r + TP, :]),
                  writes=[xb])
            E("v", lambda e: e.tensor_tensor(lnB["msq"].ap,
                                             lnB["mean"].ap,
                                             lnB["rstd"].ap, ALU.mult),
              reads=[lnB["mean"], lnB["rstd"]], writes=[lnB["msq"]],
              est=100)
            E("s", lambda e: e.activation(xn_tm.ap, x2b.ap, AF.Identity,
                                          bias=lnB["msq"].ap,
                                          scale=lnB["rstd"].ap),
              reads=[x2b, lnB["msq"], lnB["rstd"]], writes=[xn_tm],
              est=1150)
            if p > 0:
                E("v", lambda e: e.tensor_copy(xe[:, :, 1:2],
                                               xep[:, :, TP + 1:TP + 2]),
                  reads=[xnT_ext[1 - par]], writes=[xnT_ext[par]])
            nh2 = NCH // 2
            for r in range(2):
                fns = []
                for k in range(r * nh2, (r + 1) * nh2):
                    fns.append(lambda e,
                               _s=xn_tm.ap[:, k * 128:(k + 1) * 128],
                               _d=psT.ap[:, (k % nh2) * 128:
                                         (k % nh2 + 1) * 128]:
                               e.matmul(_d, _s, idb.ap, start=True,
                                        stop=True))
                E("t", fns, reads=[xn_tm, idb], writes=psT_r[0:2],
                  est=nh2 * 95)
                E("s", lambda e, _r=r, _n=nh2: e.copy(
                    xe[:, _r * _n:(_r + 1) * _n, 2:TP + 2],
                    psT.ap[:, 0:_n * 128].rearrange("p (k c) -> p k c",
                                                    k=_n)),
                  reads=psT_r[0:2], writes=[xnT_ext[par]], est=500)
            E("v", lambda e: e.tensor_tensor(
                sxTb.ap.rearrange("p (k c) -> p k c", k=NCH),
                xe[:, :, 2:TP + 2], xe[:, :, 1:TP + 1], ALU.subtract),
              reads=[xnT_ext[par]], writes=[sxTb], est=1250)
            fns = []
            for k in range(NCH):
                fns.append(lambda e, _l=xe[:, k, 2:TP + 2],
                           _r=w1cat.ap[:, k * NW1:(k + 1) * NW1],
                           _st=(k == 0):
                           e.matmul(psT.ap[:, 0:NW1], _l, _r, start=_st,
                                    stop=False))
                fns.append(lambda e, _l=sxTb.ap[:, k * 128:(k + 1) * 128],
                           _r=w1x.ap[:, k * NW1:(k + 1) * NW1],
                           _sp=(k == NCH - 1):
                           e.matmul(psT.ap[:, 0:NW1], _l, _r, start=False,
                                    stop=_sp))
            E("t", fns, reads=[xnT_ext[par], sxTb, w1cat, w1x],
              writes=[psT_r[0]], est=1300)
            E("s", lambda e: e.activation(th.ap, psT.ap[:, 0:NW1], AF.Tanh,
                                          bias=zero_c, scale=one_c),
              reads=[psT_r[0], cc], writes=[th], est=350)
            fns = []
            for m in range(4):
                fns.append(lambda e, _s=th.ap[:, m * DLOW:(m + 1) * DLOW],
                           _d=psT.ap[0:DLOW, m * TP:(m + 1) * TP]:
                           e.matmul(_d, _s, idb.ap, start=True, stop=True))
            E("t", fns, reads=[th, idb], writes=psT_r[0:2], est=450)
            E("s", lambda e, _t=thTb: e.copy(
                _t.ap[0:DLOW, :], psT.ap[0:DLOW, 0:4 * TP]),
              reads=psT_r[0:2], writes=[thTb], est=400)
            E("t", lambda e, _t=thTb: e.matmul(
                psT.ap[0:H, 0:TP], w2aug.ap, _t.ap[:, TP:2 * TP],
                start=True, stop=True),
              reads=[w2aug, thTb], writes=[psT_r[0]], est=200)
            E("s", lambda e, _w=wfmb: e.activation(_w.ap[0:H, :],
                                                   psT.ap[0:H, 0:TP],
                                                   AF.Tanh, bias=zero16,
                                                   scale=half16),
              reads=[psT_r[0], cc], writes=[wfmb], est=250)
            E("v", lambda e, _w=wfmb: e.tensor_scalar(_w.ap[0:H, :],
                                                      _w.ap[0:H, :],
                                                      1.0, 0.5 * E_HALF,
                                                      ALU.add, ALU.mult),
              reads=[wfmb], writes=[wfmb], est=200)
            E("t", lambda e, _t=thTb: e.matmul(
                psT.ap[:, 256:256 + H], _t.ap[:, 2 * TP:3 * TP], a2aug.ap,
                start=True, stop=True),
              reads=[thTb, a2aug], writes=[psT_r[1]], est=200)
            E("s", lambda e: e.activation(a_t.ap, psT.ap[:, 256:256 + H],
                                          AF.Tanh, bias=zero_c,
                                          scale=half_c),
              reads=[psT_r[1], cc], writes=[a_t], est=250)
            E("v", lambda e: e.tensor_scalar(a8.ap, a_t.ap, 1.0 / 16,
                                             1.0 / 16, ALU.mult, ALU.add),
              reads=[a_t], writes=[a8], est=170)
            mix_stream(0, 1, par)
            proj_stream(0, wk, kpreb, NCH * 512 * 0.42 + 150)
            mix_stream(1, 2, par)
            proj_stream(1, wv, vsb, NCH * 512 * 0.42 + 150)
            if real:
                mix_stream(2, 0, par)
                proj_stream(2, wr, rsb, NCH * 512 * 0.42 + 150)
            E("s", lambda e, _kk=kk8b: e.activation(_kk.ap, kpreb.ap,
                                                    AF.Square,
                                                    bias=zero_c,
                                                    scale=one_c),
              reads=[kpreb, cc], writes=[kk8b], est=1300)
            E("v", lambda e, _kk=kk8b: e.tensor_reduce(ssh.ap,
                                                       re3(_kk.ap, H), AX.X,
                                                       ALU.add),
              reads=[kk8b], writes=[ssh], est=1300)
            E("v", lambda e: e.tensor_scalar(sshe.ap, ssh.ap, 1e-24,
                                             1.0 / 64, ALU.max, ALU.mult),
              reads=[ssh], writes=[sshe], est=170)
            E("g", lambda e: e.tensor_tensor(rn.ap, sshe.ap, pwH, ALU.pow),
              reads=[sshe, powc], writes=[rn], est=3600)
            E("v", lambda e, _kk=kk8b, _kp=kpreb: e.tensor_tensor(
                re3(_kk.ap, H), re3(_kp.ap, H), bc3(rn.ap, H, 64),
                ALU.mult),
              reads=[kpreb, rn], writes=[kk8b], est=1250)
            E("v", lambda e: e.tensor_tensor(rna.ap, rn.ap, a8.ap,
                                             ALU.mult),
              reads=[rn, a8], writes=[rna], est=170)
            E("g", lambda e, _kp=kpreb: e.tensor_tensor(
                re3(kvp.ap, H), re3(_kp.ap, H), bc3(rna.ap, H, 64),
                ALU.mult),
              reads=[kpreb, rna], writes=[kvp], est=2500)
            E("v", lambda e, _v=vsb, _kv=kpreb: e.tensor_tensor(
                _kv.ap, kvp.ap, _v.ap, ALU.mult),
              reads=[kvp, vsb], writes=[kpreb], est=700)
            E("t", transpose_blocks(kpreb.ap, NCH, psC),
              reads=[kpreb, idb], writes=psC_r, est=NCH * 110)
            for lo, hi in OC:
                E("s", lambda e, _lo=lo, _hi=hi, _d=kpreb: e.copy(
                    _d.ap[:, _lo:_hi], psC.ap[:, _lo:_hi]),
                  reads=bq(psC_r, lo, hi), writes=[kpreb], est=550)
            for lo, hi in OC:
                fns = []
                for k in range(lo // 128, hi // 128):
                    fns.append(lambda e,
                               _l=eall.ap[:, k * 128:(k + 1) * 128],
                               _d=psC.ap[:, k * 128:(k + 1) * 128],
                               _w=wfmb:
                               e.matmul(_d, _l, _w.ap, start=True,
                                        stop=True))
                E("t", fns, reads=[eall, wfmb],
                  writes=bq(psC_r, lo, hi), est=(hi - lo) // 128 * 240)
            for k in range(NCH):
                lo = k * 128
                init = 0.0 if p == 0 else sprev.ap[:, lo + TP - 1:lo + TP]
                E("v", lambda e, _lo=lo, _i=init, _sc=scur, _kf=kpreb:
                  e.tensor_tensor_scan(
                    _sc.ap[:, _lo:_lo + TP], psC.ap[:, _lo:_lo + TP],
                    _kf.ap[:, _lo:_lo + TP], _i, ALU.mult, ALU.add),
                  reads=[psC_r[k * 128 // 256], kpreb] +
                        ([] if p == 0 else [sprev]),
                  writes=[scur], est=475)

        def stage_b(p):
            par = p % 2
            x2b = x2[par]
            kk8b, vsb, rsb, thTb = (kk8[par], v_sb[par], r_sb[par],
                                    thT[par])
            scur = states[par]
            lnC = lnst[2][par]
            fns = []
            for ci, (lo, hi) in enumerate(OC):
                fns.append(lambda e, _t=thTb, _r=g2z.ap[:, lo:hi],
                           _d=psW.ap[:, lo:hi]:
                           e.matmul(_d, _t.ap[:, 3 * TP:4 * TP], _r,
                                    start=True, stop=True))
            E("t", fns, reads=[thTb, g2z], writes=psW_r, est=500)
            E("s", lambda e: e.activation(tg_sb.ap, psW.ap[:, 0:C],
                                          AF.Tanh, bias=zero_c,
                                          scale=half_c),
              reads=psW_r + [cc], writes=[tg_sb], est=1100)
            fns = []
            for ci, (lo, hi) in enumerate(OC):
                fns.append(lambda e, _t=thTb, _r=v2aug.ap[:, lo:hi],
                           _d=psW.ap[:, lo:hi]:
                           e.matmul(_d, _t.ap[:, 0:TP], _r,
                                    start=True, stop=True))
            E("t", fns, reads=[thTb, v2aug], writes=psW_r, est=500)
            E("s", lambda e: e.activation(vgt.ap, psW.ap[:, 0:C], AF.Tanh,
                                          bias=zero_c, scale=half_c),
              reads=psW_r + [cc], writes=[vgt], est=1100)
            E("s", lambda e, _s=scur: e.copy(stb.ap, _s.ap),
              reads=[scur], writes=[stb], est=950)
            E("t", transpose_blocks(stb.ap, NCH, psC),
              reads=[stb, idb], writes=psC_r, est=NCH * 110)
            E("v", lambda e, _r=rsb: e.tensor_tensor(
                sc1.ap, psC.ap[:, 0:C], _r.ap, ALU.mult),
              reads=psC_r + [rsb], writes=[sc1], est=1350)
            E("g", lambda e, _r=rsb, _kk=kk8b: e.tensor_tensor(
                rkk.ap, _r.ap, _kk.ap, ALU.mult),
              reads=[rsb, kk8b], writes=[rkk], est=2300)
            E("g", lambda e: e.tensor_tensor(rkkr.ap, rkk.ap, rkt.ap,
                                             ALU.mult),
              reads=[rkk, rkt], writes=[rkkr], est=2300)
            E("v", lambda e: e.tensor_reduce(dot.ap, re3(rkkr.ap, H), AX.X,
                                             ALU.add),
              reads=[rkkr], writes=[dot], est=1300)
            E("g", lambda e, _v=vsb: e.tensor_tensor(
                re3(bon.ap, H), re3(_v.ap, H), bc3(dot.ap, H, 64), ALU.mult),
              reads=[vsb, dot], writes=[bon], est=2500)
            E("v", lambda e: e.tensor_tensor(xwkv.ap, sc1.ap, bon.ap,
                                             ALU.add),
              reads=[sc1, bon], writes=[xwkv], est=1250)
            x3v = re3(xwkv.ap, H)
            E("s", lambda e, _kk=kk8b: e.activation(_kk.ap, xwkv.ap,
                                                    AF.Square, bias=zero_c,
                                                    scale=one_c),
              reads=[xwkv, cc], writes=[kk8b], est=1300)
            E("v", lambda e: e.tensor_reduce(gs1.ap, x3v, AX.X, ALU.add),
              reads=[xwkv], writes=[gs1], est=1300)
            E("v", lambda e, _kk=kk8b: e.tensor_reduce(
                gaM.ap, re3(_kk.ap, H), AX.X, ALU.add),
              reads=[kk8b], writes=[gaM], est=1300)
            E("v", lambda e: e.tensor_scalar(gm.ap, gs1.ap, 1.0 / 64, None,
                                             ALU.mult),
              reads=[gs1], writes=[gm], est=170)
            E("v", lambda e: e.tensor_tensor(gd2.ap, gm.ap, gm.ap,
                                             ALU.mult),
              reads=[gm], writes=[gd2], est=170)
            E("v", lambda e: e.scalar_tensor_tensor(gbe.ap, gaM.ap,
                                                    1.0 / 64, gd2.ap,
                                                    ALU.mult, ALU.subtract),
              reads=[gaM, gd2], writes=[gbe], est=170)
            E("v", lambda e: e.tensor_scalar(gbe.ap, gbe.ap, EPS, None,
                                             ALU.add),
              reads=[gbe], writes=[gbe], est=170)
            E("g", lambda e: e.tensor_tensor(grs.ap, gbe.ap, pwH, ALU.pow),
              reads=[gbe, powc], writes=[grs], est=3600)
            E("v", lambda e: e.tensor_tensor(re3(u1.ap, H), x3v,
                                             bc3(gm.ap, H, 64),
                                             ALU.subtract),
              reads=[xwkv, gm], writes=[u1], est=1250)
            E("g", lambda e: e.tensor_tensor(re3(u2.ap, H), re3(u1.ap, H),
                                             bc3(grs.ap, H, 64), ALU.mult),
              reads=[u1, grs], writes=[u2], est=2500)
            E("v", lambda e: e.tensor_tensor(u3.ap, u2.ap, lblw.ap, ALU.add),
              reads=[u2, lblw], writes=[u3], est=700)
            E("v", lambda e: e.scalar_tensor_tensor(xg2.ap, tg_sb.ap, 1.0,
                                                    u3.ap, ALU.add,
                                                    ALU.mult),
              reads=[tg_sb, u3], writes=[xg2], est=900)
            E("t", transpose_blocks(xg2.ap, NCH, psC),
              reads=[xg2, idb], writes=psC_r, est=NCH * 95)
            E("s", lambda e: e.copy(xg2T.ap, psC.ap[:, 0:C]),
              reads=psC_r, writes=[xg2T], est=620)
            for lo, hi in OC:
                fns = []
                for k in range(NCH):
                    fns.append(lambda e,
                               _l=xg2T.ap[:, k * 128:(k + 1) * 128],
                               _r=wo.ap[:, k * C + lo:k * C + hi],
                               _d=psW.ap[:, lo:hi],
                               _st=(k == 0), _sp=(k == NCH - 1):
                               e.matmul(_d, _l, _r, start=_st, stop=_sp))
                E("t", fns, reads=[xg2T, wo], writes=bq(psW_r, lo, hi),
                  est=NCH * (hi - lo) * 0.42 + 150)
            E("v", lambda e, _x2=x2b: e.scalar_tensor_tensor(
                x3.ap, psW.ap[:, 0:C], 1.0, _x2.ap, ALU.mult, ALU.add),
              reads=psW_r + [x2b], writes=[x3], est=1300)
            stats_from_bn(x3, lnC, par)
            ffn_fn(x3, xout, lnC, FB2, psW, psW_r, psC, psC_r)
            E("sst", lambda e, _r=(p - HP) * TP, _x=xout:
              e.dma_start(out=y_out[_r:_r + TP, :], in_=_x.ap),
              reads=[xout], est=2500)

        for p in range(NP):
            stage_a(p)
            if p >= HP:
                stage_b(p)

        P.schedule()
        with nc.Block() as block:
            @block.vector
            def _(v):
                for op in P.seq_ops["v"]:
                    op(v)

            @block.scalar
            def _(s):
                for op in P.seq_ops["s"]:
                    op(s)

            @block.tensor
            def _(t):
                for op in P.seq_ops["t"]:
                    op(t)

            @block.gpsimd
            def _(g):
                for op in P.seq_ops["g"]:
                    op(g)

            @block.sync
            def _(sy):
                for op in P.seq_ops["sy"]:
                    op(sy)

    return nc


def prep_shared(inp, C, pre="L0_attn_", fpre="L0_ffn_"):
    NCH = C // 128
    H = C // 64
    a = {k: np.asarray(v) for k, v in inp.items()}
    f = np.float64 if any(v.dtype == np.float64 for v in a.values()) \
        else np.float32
    w0 = a[pre + "w0"].reshape(1, H)
    a0 = a[pre + "a0"].reshape(1, H)
    v0 = a[pre + "v0"].reshape(1, C)
    colsplit = lambda v: np.ascontiguousarray(v.reshape(NCH, 128).T)
    tile = lambda v: np.ascontiguousarray(np.tile(v.reshape(1, C), (128, 1)))
    shared = {
        "wtT": np.ascontiguousarray(a[fpre + "weight"].T),
        "wr": np.ascontiguousarray(a[pre + "Wr"]),
        "wk": np.ascontiguousarray(a[pre + "Wk"]),
        "wv": np.ascontiguousarray(a[pre + "Wv"]),
        "wo": np.ascontiguousarray(a[pre + "Wo"]),
        "ug": np.ascontiguousarray(np.concatenate(
            [a[fpre + "lora_U"].reshape(ERK, C).T, a[fpre + "gate_w"].T],
            axis=1)),
        "vcat": np.ascontiguousarray(
            a[fpre + "lora_V"].transpose(0, 2, 1).reshape(ERK, C)),
        "w1cat": np.ascontiguousarray(np.concatenate(
            [a[pre + "v1"], a[pre + "w1"], a[pre + "a1"], a[pre + "g1"]],
            axis=1)),
        "w2aug": np.ascontiguousarray(np.concatenate([a[pre + "w2"], w0], 0)),
        "a2aug": np.ascontiguousarray(np.concatenate([a[pre + "a2"], a0], 0)),
        "v2aug": np.ascontiguousarray(np.concatenate([a[pre + "v2"], v0], 0)),
        "g2z": np.ascontiguousarray(np.concatenate(
            [a[pre + "g2"], np.zeros((1, C), f)], 0)),
        "xmix": np.ascontiguousarray(np.concatenate(
            [colsplit(a[pre + "x_" + s]) for s in "rkvwag"], axis=1)),
        "kkt": tile(a[pre + "k_k"]),
        "lnwc": colsplit(a[pre + "ln_w"]),
        "lnw_tile": tile(a[pre + "ln_w"]),
        "lnb_tile": tile(a[pre + "ln_b"]),
        "rk_tile": tile(a[pre + "r_k"]),
        "identb": np.eye(128, dtype=ml_dtypes.bfloat16),
        "identf": np.eye(128, dtype=f),
        "ones512": np.ones((1, 4 * TP), dtype=ml_dtypes.bfloat16),
    }
    eall = np.zeros((128, C), f)
    for k in range(NCH):
        for pp in range(128):
            eall[2 * k + pp // 64, k * 128 + pp] = 1.0
    shared["eall"] = eall
    cconst = np.zeros((128, 4), f)
    cconst[:, CC_ONE] = 1.0
    cconst[:, CC_ZERO] = 0.0
    cconst[:, CC_HALF] = 0.5
    shared["cconst"] = cconst
    return shared


def make_in_maps(x, shared, C, HP, RP, n_seq_shards=4):
    B, T, _ = x.shape
    NP = HP + RP
    per = RP * TP
    halo = HP * TP
    in_maps = []
    for core in range(B * n_seq_shards):
        b, q = core // n_seq_shards, core % n_seq_shards
        start = q * per - halo
        rows = np.zeros((NP * TP, C), x.dtype)
        lo = max(start, 0)
        rows[lo - start:, :] = x[b, lo:q * per + per, :]
        m = dict(shared)
        m["x"] = rows
        in_maps.append(m)
    return in_maps


_NC_CACHE = {}


def kernel(**inputs):
    C, HP, RP = 1024, 1, 16
    x = np.asarray(inputs["L0_attn_x"])
    B, T, _ = x.shape
    per = RP * TP
    shared = prep_shared(inputs, C)
    key = (C, HP, RP)
    if key not in _NC_CACHE:
        _NC_CACHE[key] = build_nc(C, HP, RP)
    nc = _NC_CACHE[key]
    in_maps = make_in_maps(x, shared, C, HP, RP)
    res = run_bass_kernel_spmd(nc, in_maps, core_ids=list(range(8)))
    odt = np.asarray(res.results[0]["y"]).dtype
    out = np.empty((B, T, C), np.float32 if odt == np.float32 else odt)
    for core in range(8):
        b, q = core // 4, core % 4
        out[b, q * per:(q + 1) * per, :] = res.results[core]["y"]
    return out


# version 77
# speedup vs baseline: 1.0023x; 1.0023x over previous
import numpy as np
import ml_dtypes
from contextlib import ExitStack

import concourse.bass as bass
import concourse.mybir as mybir
from concourse.bass_utils import run_bass_kernel_spmd

F32 = mybir.dt.float32
BF16 = mybir.dt.bfloat16
AF = mybir.ActivationFunctionType
ALU = mybir.AluOpType
AX = mybir.AxisListType

TP = 128
DLOW = 64
NE, NR = 8, 16
ERK = NE * NR
K2 = DLOW + 1
EPS = 1e-5
E_HALF = float(np.exp(-0.5))
CC_ONE, CC_ZERO, CC_HALF = 0, 1, 2
DEBUG_LOG = None


class _Stream:
    def __init__(self, name, sem, dma):
        self.name, self.sem, self.dma = name, sem, dma
        self.count = 0


class Buf:
    __slots__ = ("ap", "writer", "readers")

    def __init__(self, ap):
        self.ap, self.writer, self.readers = ap, None, []


class _Op:
    __slots__ = ("idx", "st", "fns", "deps", "est", "prio", "start", "fin",
                 "tok")

    def __init__(self, idx, st, fns, deps, est):
        self.idx, self.st, self.fns, self.deps, self.est = (
            idx, st, fns, deps, est)
        self.prio = 0.0
        self.start = self.fin = 0.0
        self.tok = None


SCHED_SEED = 1111


class Prog:

    SEQ_OF = {"v": "v", "s": "s", "t": "t", "g": "g", "gd": "g", "gd2": "g",
              "gd3": "g",
              "syc": "sy", "sf0": "sy", "sf1": "sy", "sx0": "sy", "sx1": "sy",
              "sst": "sy"}
    XLAT = 340.0
    OPFIX = 90.0

    def __init__(self, nc, stack):
        self.nc = nc
        self.streams = {}
        for nm, dma in [("v", False), ("s", False), ("t", False), ("g", False),
                        ("gd", True), ("gd2", True), ("gd3", True),
                        ("syc", True),
                        ("sf0", True), ("sf1", True), ("sx0", True),
                        ("sx1", True), ("sst", True)]:
            sem = stack.enter_context(nc.semaphore(name="sem_" + nm))
            self.streams[nm] = _Stream(nm, sem, dma)
        self.ops = []
        self.final_stream = None

    def emit(self, st_name, fn, reads=(), writes=(), est=400.0):
        fns = tuple(fn) if isinstance(fn, (list, tuple)) else (fn,)
        deps = set()
        for b in reads:
            if b.writer is not None:
                deps.add(b.writer)
        for b in writes:
            if b.writer is not None:
                deps.add(b.writer)
            deps.update(b.readers)
        op = _Op(len(self.ops), st_name, fns, deps, float(est))
        self.ops.append(op)
        for b in reads:
            b.readers.append(op.idx)
        for b in writes:
            b.writer = op.idx
            b.readers = []
        return op

    def schedule(self):
        ops = self.ops
        n = len(ops)
        barrier_members = {}
        for op in ops:
            if op.st in ("gd", "gd2", "gd3", "syc"):
                barrier_members.setdefault(op.st, set()).add(op.idx)
        for op in ops:
            ext = None
            for d in op.deps:
                dst = ops[d].st
                if dst in barrier_members and op.st != dst:
                    ext = (ext or set()) | barrier_members[dst]
            if ext:
                op.deps = (op.deps | ext) - {op.idx}
        succs = [[] for _ in range(n)]
        for op in ops:
            for d in op.deps:
                succs[d].append(op.idx)
        if SCHED_SEED:
            rng = np.random.default_rng(SCHED_SEED)
            jit = rng.uniform(0.92, 1.08, n)
        else:
            jit = None
        for i in range(n - 1, -1, -1):
            p = ops[i].est
            if jit is not None:
                p *= jit[i]
            m = 0.0
            for s in succs[i]:
                if ops[s].prio > m:
                    m = ops[s].prio
            ops[i].prio = p + m
        for op in ops:
            if op.st == "gd3":
                op.prio = 3e12
            elif op.st == "gd":
                op.prio = 2e12
            elif op.st == "gd2":
                op.prio = 1e12
        SLACK = 400.0
        seqs = sorted(set(self.SEQ_OF.values()))
        seq_time = {q: 0.0 for q in seqs}
        ndeps = [len(op.deps) for op in ops]
        ready = {q: set() for q in seqs}
        est_ready = [0.0] * n
        for op in ops:
            if not op.deps:
                ready[self.SEQ_OF[op.st]].add(op.idx)
        seq_lists = {q: [] for q in seqs}
        scheduled = 0
        while scheduled < n:
            best = None
            for q in seqs:
                rq = ready[q]
                if not rq:
                    continue
                emin = min(est_ready[i] for i in rq)
                t0 = max(seq_time[q], emin)
                cand, cprio = None, -1.0
                for i in rq:
                    if est_ready[i] <= t0 + SLACK and ops[i].prio > cprio:
                        cand, cprio = i, ops[i].prio
                start = max(seq_time[q], est_ready[cand])
                if (best is None or start < best[0] - 1e-9 or
                        (abs(start - best[0]) < 1e-9 and cprio > best[3])):
                    best = (start, q, cand, cprio)
            start, q, i, _ = best
            op = ops[i]
            ready[q].discard(i)
            op.start = start
            op.fin = start + op.est + self.OPFIX
            seq_time[q] = op.fin
            seq_lists[q].append(i)
            scheduled += 1
            for s in succs[i]:
                ndeps[s] -= 1
                if ndeps[s] == 0:
                    est_ready[s] = max(
                        [0.0] + [ops[d].fin + self.XLAT
                                 for d in ops[s].deps])
                    ready[self.SEQ_OF[ops[s].st]].add(s)
        assert scheduled == n, (scheduled, n)
        import os
        if os.environ.get("SCHED_REPORT"):
            busy = {}
            for op in ops:
                busy[self.SEQ_OF[op.st]] = busy.get(self.SEQ_OF[op.st], 0) + op.est
            print("SCHED makespan %.1f us; busy: %s" % (
                max(o.fin for o in ops) / 1e3,
                {k: round(v / 1e3, 1) for k, v in sorted(busy.items())}))
            import traceback as _tb
            last = max(ops, key=lambda o: o.fin)
            chain = []
            cur = last
            while cur is not None:
                chain.append(cur)
                best = None
                for d in cur.deps:
                    if abs(ops[d].fin + self.XLAT - cur.start) < 1e-6:
                        best = ops[d]
                        break
                if best is None:
                    q = self.SEQ_OF[cur.st]
                    cand = [o for o in ops
                            if self.SEQ_OF[o.st] == q and o.fin <= cur.start
                            and o is not cur]
                    if cand:
                        b = max(cand, key=lambda o: o.fin)
                        if abs(b.fin - cur.start) < 1e-6:
                            best = b
                cur = best
            chain.reverse()
            import collections
            agg = collections.Counter()
            for o in chain:
                agg[(o.st, round(o.est, -1))] += 1
            print("SCHED critical chain: %d ops" % len(chain))
            for (stn, e), cnt in agg.most_common(20):
                print("   %s est=%s x%d  (%.1f us)" % (stn, e, cnt,
                                                       e * cnt / 1e3))
        for q in seqs:
            for i in seq_lists[q]:
                op = ops[i]
                st = self.streams[op.st]
                st.count += 16 if st.dma else 1
                op.tok = (op.st, st.count)
        self.seq_ops = {q: [] for q in seqs}
        for q in seqs:
            waited = {}
            for i in seq_lists[q]:
                op = ops[i]
                need = {}
                for d in op.deps:
                    snm, val = ops[d].tok
                    if snm == op.st and self.streams[snm].dma:
                        continue
                    if snm in ("gd", "gd2", "gd3", "syc"):
                        val = self.streams[snm].count
                    if need.get(snm, 0) < val:
                        need[snm] = val
                waits = []
                for snm, val in sorted(need.items()):
                    if waited.get(snm, 0) < val:
                        waited[snm] = val
                        waits.append((self.streams[snm].sem, val))
                sem = self.streams[op.st].sem
                inc = 16 if self.streams[op.st].dma else 1

                def run(eng, _w=tuple(waits), _f=op.fns, _s=sem, _i=inc):
                    for s, v in _w:
                        eng.wait_ge(s, v)
                    last = None
                    for f in _f:
                        last = f(eng)
                    last.then_inc(_s, _i)

                self.seq_ops[q].append(run)
        fw = []
        for nm in ("sst", "sx0", "sx1", "syc", "sf0", "sf1"):
            st = self.streams[nm]
            if st.count:
                fw.append((st.sem, st.count))

        def run_final(eng, _w=tuple(fw)):
            for s, v in _w:
                eng.wait_ge(s, v)

        self.seq_ops["sy"].append(run_final)


def out_chunks(C):
    return [(i, min(i + 512, C)) for i in range(0, C, 512)]


def build_nc(C, HP, RP):
    NCH = C // 128
    H = C // 64
    NP = HP + RP
    ROWS = NP * TP
    NW1 = 4 * DLOW
    UGW = ERK + NE
    OC = out_chunks(C)

    nc = bass.Bass("TRN2", target_bir_lowering=False)
    dt = nc.dram_tensor

    x_in = dt("x", [ROWS, C], F32, kind="ExternalInput").ap()
    wtT_in = dt("wtT", [C, C], F32, kind="ExternalInput").ap()
    wr_in = dt("wr", [C, C], F32, kind="ExternalInput").ap()
    wk_in = dt("wk", [C, C], F32, kind="ExternalInput").ap()
    wv_in = dt("wv", [C, C], F32, kind="ExternalInput").ap()
    wo_in = dt("wo", [C, C], F32, kind="ExternalInput").ap()
    ug_in = dt("ug", [C, UGW], F32, kind="ExternalInput").ap()
    vcat_in = dt("vcat", [ERK, C], F32, kind="ExternalInput").ap()
    w1cat_in = dt("w1cat", [C, NW1], F32, kind="ExternalInput").ap()
    w2aug_in = dt("w2aug", [K2, H], F32, kind="ExternalInput").ap()
    a2aug_in = dt("a2aug", [K2, H], F32, kind="ExternalInput").ap()
    v2aug_in = dt("v2aug", [K2, C], F32, kind="ExternalInput").ap()
    g2z_in = dt("g2z", [K2, C], F32, kind="ExternalInput").ap()
    xmix_in = dt("xmix", [128, 6 * NCH], F32, kind="ExternalInput").ap()
    kkt_in = dt("kkt", [128, C], F32, kind="ExternalInput").ap()
    lnwc_in = dt("lnwc", [128, NCH], F32, kind="ExternalInput").ap()
    lnw_tile_in = dt("lnw_tile", [128, C], F32, kind="ExternalInput").ap()
    lnb_tile_in = dt("lnb_tile", [128, C], F32, kind="ExternalInput").ap()
    rk_tile_in = dt("rk_tile", [128, C], F32, kind="ExternalInput").ap()
    identb_in = dt("identb", [128, 128], BF16, kind="ExternalInput").ap()
    identf_in = dt("identf", [128, 128], F32, kind="ExternalInput").ap()
    eall_in = dt("eall", [128, C], F32, kind="ExternalInput").ap()
    cconst_in = dt("cconst", [128, 4], F32, kind="ExternalInput").ap()
    ones512_in = dt("ones512", [1, 4 * TP], BF16, kind="ExternalInput").ap()
    y_out = dt("y", [RP * TP, C], F32, kind="ExternalOutput").ap()

    stack = ExitStack()

    _nn = [0]

    def sb(shape, dty):
        _nn[0] += 1
        return Buf(stack.enter_context(
            nc.sbuf_tensor("sb%d" % _nn[0], shape, dty))[:])

    def psum(shape):
        _nn[0] += 1
        return Buf(stack.enter_context(
            nc.psum_tensor("ps%d" % _nn[0], shape, F32))[:])

    with stack:
        wt = sb([128, NCH * C], BF16)
        wr = sb([128, NCH * C], BF16)
        wk = sb([128, NCH * C], BF16)
        wv = sb([128, NCH * C], BF16)
        wo = sb([128, NCH * C], BF16)
        ug = sb([128, NCH * UGW], BF16)
        vcat = sb([ERK, C], BF16)
        w1cat = sb([128, NCH * NW1], BF16)
        w1x = sb([128, NCH * NW1], BF16)
        w2aug = sb([K2, H], BF16)
        a2aug = sb([K2, H], BF16)
        v2aug = sb([K2, C], BF16)
        g2z = sb([K2, C], BF16)
        xmix = sb([128, 6 * NCH], BF16)
        lnwc = sb([128, NCH], F32)
        lnwh = sb([128, NCH], F32)
        lblw = sb([128, C], BF16)
        rkt = sb([128, C], F32)
        idb = sb([128, 128], BF16)
        eall = sb([128, C], BF16)
        cc = sb([128, 4], F32)
        powc = sb([128, 16], F32)

        xin = [sb([128, C], F32) for _ in range(2)]
        x2 = [sb([128, C], F32) for _ in range(2)]
        x3 = sb([128, C], F32)
        xn_tm = sb([128, C], BF16)
        xc1 = sb([128, C], BF16)
        xnT = [sb([128, C], BF16) for _ in range(2)]
        xnT_ext = [sb([128, NCH * (TP + 2)], BF16) for _ in range(2)]
        sxT = [sb([128, C], BF16) for _ in range(2)]
        mixT = [sb([128, C], BF16) for _ in range(3)]
        th = sb([128, NW1], BF16)
        thT = [sb([K2, 4 * TP], BF16) for _ in range(2)]
        xg2 = sb([128, C], BF16)
        xg2T = sb([128, C], BF16)
        xn3_tm = xg2
        xn3T = xg2T
        tw = sb([128, ERK], BF16)
        twT = sb([ERK, TP], BF16)
        tw2 = sb([128, ERK], BF16)
        twT2 = sb([ERK, TP], BF16)
        tg_sb = sb([128, C], BF16)

        kpre = [sb([128, C], BF16) for _ in range(2)]
        kk8 = [sb([128, C], F32) for _ in range(2)]
        v_sb = [sb([128, C], BF16) for _ in range(2)]
        u2 = sb([128, C], BF16)
        u3 = u2

        states = [sb([128, C], F32) for _ in range(2)]
        r_sb = [sb([128, C], F32) for _ in range(2)]
        rkk = sb([128, C], F32)
        xwkv = sb([128, C], F32)
        xout = sb([128, C], F32)
        stgf = [x2[0], x2[1]]
        lnw_t = rkk
        sc1 = x3
        vgt = xg2
        kkt = x3
        kvp = mixT[2]
        stb = xg2T
        rkkr = xwkv
        u1 = xwkv
        bon = rkk

        ln12 = [sb([128, 12], F32) for _ in range(2)]
        lnmv = [sb([128, 2], F32) for _ in range(2)]
        lnst = [[dict(sum=sb([128, 1], F32), ssq=sb([128, 1], F32),
                      mean=sb([128, 1], F32), rstd=sb([128, 1], F32),
                      veps=sb([128, 1], F32), msq=sb([128, 1], F32),
                      rstdh=sb([128, 1], F32))
                 for _ in range(2)] for _ in range(3)]
        gexp = sb([128, NE], F32)
        gsum = sb([128, 1], F32)
        grec = sb([128, 1], F32)
        gate = sb([128, NE], F32)
        gexp2 = sb([128, NE], F32)
        gsum2 = sb([128, 1], F32)
        grec2 = sb([128, 1], F32)
        gate2 = sb([128, NE], F32)
        w_fm = [sb([128, TP], BF16) for _ in range(2)]
        a_t = sb([128, H], F32)
        a8 = sb([128, H], F32)
        ssh = sb([128, H], F32)
        rna = sb([128, H], F32)
        sshe = sb([128, H], F32)
        rn = sb([128, H], F32)
        dot = sshe
        gs1 = sb([128, H], F32)
        gd2 = sb([128, H], F32)
        gm = ssh
        gaM = rna
        grs = rn
        gbe = gd2

        psT = psum([128, 1024])
        psB = psum([128, 1024])
        psW = psum([128, 1024])
        psC = psum([128, 1024])
        quarters = lambda ps: [Buf(ps.ap[:, i * 256:(i + 1) * 256])
                               for i in range(4)]
        psT_r = quarters(psT)
        psB_r = quarters(psB)
        psW_r = quarters(psW)
        psC_r = quarters(psC)

        def bq(ps_r, lo, hi):
            return ps_r[lo // 256:(hi + 255) // 256]

        P = Prog(nc, stack)
        E = P.emit

        one_c = cc.ap[:, CC_ONE:CC_ONE + 1]
        zero_c = cc.ap[:, CC_ZERO:CC_ZERO + 1]
        half_c = cc.ap[:, CC_HALF:CC_HALF + 1]
        zero16 = cc.ap[0:H, CC_ZERO:CC_ZERO + 1]
        half16 = cc.ap[0:H, CC_HALF:CC_HALF + 1]

        def bc3(small_ap, nh, wide):
            return small_ap.broadcast_to([128, nh, wide])

        def re3(ap_, nh):
            return ap_.rearrange("p (h d) -> p h d", h=nh)

        xe3v = [xnT_ext[i].ap.rearrange("p (k c) -> p k c", k=NCH)
                for i in range(2)]

        E("sx0", lambda e: e.dma_start(out=xin[0].ap, in_=x_in[0:TP, :]),
          writes=[xin[0]])
        E("sx1", lambda e: e.dma_start(out=xin[1].ap, in_=x_in[TP:2 * TP, :]),
          writes=[xin[1]])
        lnb_stg = xout
        for buf, src in [(lnb_stg, lnb_tile_in), (lnw_t, lnw_tile_in),
                         (rkt, rk_tile_in),
                         (cc, cconst_in), (kkt, kkt_in),
                         (lnwc, lnwc_in)]:
            E("syc", lambda e, b=buf, s=src: e.dma_start(out=b.ap, in_=s),
              writes=[buf])
        E("syc", lambda e: e.dma_start(out=thT[0].ap[DLOW:K2, :],
                                       in_=ones512_in), writes=[thT[0]])
        E("syc", lambda e: e.dma_start(out=thT[1].ap[DLOW:K2, :],
                                       in_=ones512_in), writes=[thT[1]])

        def cload(stream, group, buf, src, chunked=False):
            s = src.rearrange("(k p) c -> p k c", p=128) if chunked else src
            o = (buf.ap.rearrange("p (k c) -> p k c", k=NCH) if chunked
                 else buf.ap)
            E(stream, lambda e, _o=o, _s=s: e.dma_start(out=_o, in_=_s),
              writes=[buf])
            group.append(buf)

        grpA, grpB = [], []
        cload("gd", grpA, ug, ug_in, True)
        cload("gd", grpA, vcat, vcat_in)
        cload("gd", grpA, idb, identb_in)
        cload("gd", grpA, xmix, xmix_in)
        cload("gd", grpA, eall, eall_in)
        cload("gd2", grpB, w1cat, w1cat_in, True)
        cload("gd2", grpB, w2aug, w2aug_in)
        cload("gd2", grpB, a2aug, a2aug_in)
        cload("gd2", grpB, v2aug, v2aug_in)
        cload("gd2", grpB, g2z, g2z_in)
        cload("gd2", grpB, wr, wr_in, True)
        cload("gd2", grpB, wv, wv_in, True)

        E("v", lambda e: e.memset(powc.ap, -0.5), writes=[powc])
        pw1 = powc.ap[:, 0:1]
        pwH = powc.ap[:, 0:H]
        E("v", lambda e: e.tensor_scalar(lnwh.ap, lnwc.ap, 0.5, None,
                                         ALU.mult),
          reads=[lnwc], writes=[lnwh])
        E("v", lambda e: e.reciprocal(lnw_t.ap, lnw_t.ap),
          reads=[lnw_t], writes=[lnw_t])
        E("v", lambda e: e.tensor_tensor(lblw.ap, lnb_stg.ap, lnw_t.ap,
                                         ALU.mult),
          reads=[lnb_stg, lnw_t], writes=[lblw])
        E("v", lambda e: e.tensor_scalar(rkt.ap, rkt.ap, 0.125, None,
                                         ALU.mult),
          reads=[rkt], writes=[rkt])
        for i in range(2):
            E("v", lambda e, _b=xnT_ext[i]: e.memset(_b.ap, 0.0),
              writes=[xnT_ext[i]])
            E("v", lambda e, _b=w_fm[i]: e.memset(_b.ap, 0.0),
              writes=[w_fm[i]])
        mix_bc = (xmix.ap[:, 2 * NCH:6 * NCH]
                  .rearrange("p (m k) -> p k m", m=4)[:, :, :, None]
                  .broadcast_to([128, NCH, 4, DLOW]))
        E("v", lambda e: e.tensor_tensor(
            w1x.ap.rearrange("p (k m d) -> p k m d", k=NCH, m=4),
            w1cat.ap.rearrange("p (k m d) -> p k m d", k=NCH, m=4),
            mix_bc, ALU.mult),
          reads=[w1cat, xmix], writes=[w1x])

        def folded_load(dst, src, fold):
            for k in range(NCH):
                s = stgf[k % 2]
                E("sf%d" % (k % 2), lambda e, _s=s, _k=k, _src=src:
                  e.dma_start(out=_s.ap,
                              in_=_src[_k * 128:(_k + 1) * 128, :]),
                  writes=[s])
                fold(k, s, dst)

        for k in range(NCH):
            E("gd3", lambda e, _k=k: e.dma_start(
                out=wt.ap[:, _k * C:(_k + 1) * C],
                in_=wtT_in[_k * 128:(_k + 1) * 128, :]),
              writes=[wt])
        folded_load(wk, wk_in, lambda k, s, d: E(
            "g", lambda e, _k=k, _s=s, _d=d: e.tensor_tensor(
                _d.ap[:, _k * C:(_k + 1) * C], _s.ap, kkt.ap, ALU.mult),
            reads=[s, kkt], writes=[d]))
        folded_load(wo, wo_in, lambda k, s, d: E(
            "s", lambda e, _k=k, _s=s, _d=d: e.activation(
                _d.ap[:, _k * C:(_k + 1) * C], _s.ap, AF.Copy,
                bias=0.0, scale=lnwh.ap[:, _k:_k + 1]),
            reads=[s, lnwh, cc], writes=[d]))

        def stats_from_accum(st):
            E("v", [lambda e: e.tensor_scalar(st["mean"].ap, st["sum"].ap,
                                              -1.0 / C, None, ALU.mult),
                    lambda e: e.tensor_scalar(st["veps"].ap, st["ssq"].ap,
                                              1.0 / C, EPS, ALU.mult,
                                              ALU.add)],
              reads=[st["sum"], st["ssq"]],
              writes=[st["mean"], st["veps"]], est=250)
            E("v", lambda e: e.tensor_tensor(st["msq"].ap, st["mean"].ap,
                                             st["mean"].ap, ALU.mult),
              reads=[st["mean"]], writes=[st["msq"]], est=100)
            E("v", lambda e: e.tensor_tensor(st["veps"].ap, st["veps"].ap,
                                             st["msq"].ap, ALU.subtract),
              reads=[st["veps"], st["msq"]], writes=[st["veps"]], est=100)
            E("g", lambda e: e.tensor_tensor(st["rstd"].ap, st["veps"].ap,
                                             pw1, ALU.pow),
              reads=[st["veps"], powc], writes=[st["rstd"]], est=1500)
            E("v", lambda e: e.tensor_scalar(st["rstdh"].ap, st["rstd"].ap,
                                             0.5, None, ALU.mult),
              reads=[st["rstd"]], writes=[st["rstdh"]], est=100)

        def stats_from_bn(src_buf, st, par):
            wdt = C // 2
            l12, lmv = ln12[par], lnmv[par]
            E("v", [lambda e: e.bn_stats(l12.ap[:, 0:6],
                                         src_buf.ap[:, 0:wdt]),
                    lambda e: e.bn_stats(l12.ap[:, 6:12],
                                         src_buf.ap[:, wdt:C])],
              reads=[src_buf], writes=[l12])
            E("v", lambda e: e.bn_aggr(lmv.ap, l12.ap),
              reads=[l12], writes=[lmv])
            E("v", lambda e: e.tensor_scalar(st["mean"].ap, lmv.ap[:, 0:1],
                                             -1.0, None, ALU.mult),
              reads=[lmv], writes=[st["mean"]])
            E("v", lambda e: e.tensor_scalar(st["veps"].ap, lmv.ap[:, 1:2],
                                             EPS, None, ALU.add),
              reads=[lmv], writes=[st["veps"]])
            E("g", lambda e: e.tensor_tensor(st["rstd"].ap, st["veps"].ap,
                                             pw1, ALU.pow),
              reads=[st["veps"], powc], writes=[st["rstd"]], est=1500)
            E("v", lambda e: e.tensor_scalar(st["rstdh"].ap, st["rstd"].ap,
                                             0.5, None, ALU.mult),
              reads=[st["rstd"]], writes=[st["rstdh"]], est=100)

        def transpose_blocks(src_ap, n_blk, ps, col0=0):
            fns = []
            for k in range(n_blk):
                fns.append(lambda e,
                           _s=src_ap[:, k * 128:(k + 1) * 128],
                           _d=ps.ap[:, col0 + k * 128:col0 + (k + 1) * 128]:
                           e.matmul(_d, _s, idb.ap, start=True, stop=True))
            return fns

        def big_mm_group(lhs_ap_of, w_buf, ps_buf, close=True):
            fns = []
            for k in range(NCH):
                lhs = lhs_ap_of(k)
                for ci, (lo, hi) in enumerate(OC):
                    fns.append(lambda e, _l=lhs,
                               _r=w_buf.ap[:, k * C + lo:k * C + hi],
                               _d=ps_buf.ap[:, lo:hi],
                               _st=(k == 0),
                               _sp=(close and k == NCH - 1):
                               e.matmul(_d, _l, _r, start=_st, stop=_sp))
            return fns

        def ffn_fn(x_buf, out_buf, st, fb, bps, bps_r, aps, aps_r,
                   compact_aux=False):
            E("v", lambda e: e.tensor_scalar(fb["xc"].ap, x_buf.ap,
                                             st["mean"].ap, None,
                                             ALU.add),
              reads=[x_buf, st["mean"]], writes=[fb["xc"]])
            if compact_aux:
                nh = NCH // 2
                for r in range(2):
                    fns = []
                    for k in range(r * nh, (r + 1) * nh):
                        fns.append(lambda e,
                                   _s=fb["xc"].ap[:, k * 128:(k + 1) * 128],
                                   _d=aps.ap[:, (k % nh) * 128:
                                             (k % nh + 1) * 128]:
                                   e.matmul(_d, _s, idb.ap, start=True,
                                            stop=True))
                    E("t", fns, reads=[fb["xc"], idb], writes=aps_r[0:2],
                      est=nh * 95)
                    E("s", lambda e, _r=r, _w=nh * 128: e.copy(
                        fb["xT"].ap[:, _r * _w:(_r + 1) * _w],
                        aps.ap[:, 0:_w]),
                      reads=aps_r[0:2], writes=[fb["xT"]], est=500)
            else:
                E("t", transpose_blocks(fb["xc"].ap, NCH, aps),
                  reads=[fb["xc"], idb], writes=aps_r, est=NCH * 95)
                E("s", lambda e: e.copy(fb["xT"].ap, aps.ap[:, 0:C]),
                  reads=aps_r, writes=[fb["xT"]], est=950)
            twc = 384 if compact_aux else 512
            twq = aps_r[1] if compact_aux else aps_r[2]
            fns = []
            for k in range(NCH):
                fns.append(lambda e,
                           _l=fb["xT"].ap[:, k * 128:(k + 1) * 128],
                           _r=ug.ap[:, k * UGW:(k + 1) * UGW],
                           _st=(k == 0), _sp=(k == NCH - 1):
                           e.matmul(aps.ap[:, 0:UGW], _l, _r,
                                    start=_st, stop=_sp))
            E("t", fns, reads=[fb["xT"], ug], writes=[aps_r[0]], est=800)
            for lo, hi in OC:
                fns = []
                for k in range(NCH):
                    fns.append(lambda e,
                               _l=fb["xT"].ap[:, k * 128:(k + 1) * 128],
                               _r=wt.ap[:, k * C + lo:k * C + hi],
                               _d=bps.ap[:, lo:hi], _st=(k == 0):
                               e.matmul(_d, _l, _r, start=_st, stop=False))
                E("t", fns, reads=[fb["xT"], wt],
                  writes=bq(bps_r, lo, hi), est=NCH * (hi - lo) * 0.42 + 150)
            E("s", lambda e: e.activation(fb["gexp"].ap, aps.ap[:, ERK:UGW],
                                          AF.Exp, bias=zero_c,
                                          scale=st["rstd"].ap,
                                          accum_out=fb["gsum"].ap),
              reads=[aps_r[0], st["rstd"], cc],
              writes=[fb["gexp"], fb["gsum"]])
            E("v", lambda e: e.reciprocal(fb["grec"].ap, fb["gsum"].ap),
              reads=[fb["gsum"]], writes=[fb["grec"]])
            E("v", lambda e: e.tensor_scalar(fb["gate"].ap, fb["gexp"].ap,
                                             fb["grec"].ap, 1.0 / NR,
                                             ALU.mult, ALU.mult),
              reads=[fb["gexp"], fb["grec"]], writes=[fb["gate"]])
            E("v", lambda e: e.tensor_tensor(
                re3(fb["tw"].ap, NE), re3(aps.ap[:, 0:ERK], NE),
                bc3(fb["gate"].ap, NE, NR), ALU.mult),
              reads=[aps_r[0], fb["gate"]], writes=[fb["tw"]])
            E("t", lambda e: e.matmul(aps.ap[:, twc:twc + 128], fb["tw"].ap,
                                      idb.ap, start=True, stop=True),
              reads=[fb["tw"], idb], writes=[twq], est=200)
            E("s", lambda e: e.copy(fb["twT"].ap, aps.ap[:, twc:twc + 128]),
              reads=[twq], writes=[fb["twT"]], est=300)
            for lo, hi in OC:
                E("t", lambda e, _r=vcat.ap[:, lo:hi], _d=bps.ap[:, lo:hi]:
                  e.matmul(_d, fb["twT"].ap, _r, start=False, stop=True),
                  reads=[fb["twT"], vcat], writes=bq(bps_r, lo, hi),
                  est=900)
            E("v", lambda e: e.scalar_tensor_tensor(
                out_buf.ap, bps.ap[:, 0:C], st["rstdh"].ap, x_buf.ap,
                ALU.mult, ALU.add),
              reads=bps_r + [x_buf, st["rstdh"]], writes=[out_buf], est=1300)

        def mix_stream(mslot, mixcol, par, eng="v"):
            mb = mixT[mslot]
            xe3_ = xe3v[par]
            m3 = mb.ap.rearrange("p (k c) -> p k c", k=NCH)
            mc = (xmix.ap[:, mixcol * NCH:(mixcol + 1) * NCH]
                  [:, :, None].broadcast_to([128, NCH, 128]))
            est = 1150 if eng == "v" else 2500
            E(eng, lambda e, _sx=sxT[par]: e.tensor_tensor(
                m3, _sx.ap.rearrange("p (k c) -> p k c", k=NCH), mc,
                ALU.mult),
              reads=[sxT[par], xmix], writes=[mb], est=est)
            E(eng, lambda e: e.tensor_tensor(m3, m3, xe3_[:, :, 2:TP + 2],
                                             ALU.add),
              reads=[mb, xnT_ext[par]], writes=[mb], est=est)
            return mb

        FB1 = [dict(xc=xc1, xT=xnT[i], tw=tw, twT=twT, gexp=gexp,
                    gsum=gsum, grec=grec, gate=gate) for i in range(2)]
        FB2 = dict(xc=xn3_tm, xT=xn3T, tw=tw2, twT=twT2, gexp=gexp2,
                   gsum=gsum2, grec=grec2, gate=gate2)


        def proj_stream(mslot, wbuf, dst, est_mm):
            for lo, hi in OC:
                w = hi - lo
                qs = psT_r[2:2 + (w + 255) // 256]
                fns = []
                for k in range(NCH):
                    fns.append(lambda e,
                               _l=mixT[mslot].ap[:, k * 128:(k + 1) * 128],
                               _r=wbuf.ap[:, k * C + lo:k * C + hi],
                               _d=psT.ap[:, 512:512 + w],
                               _st=(k == 0), _sp=(k == NCH - 1):
                               e.matmul(_d, _l, _r, start=_st, stop=_sp))
                E("t", fns, reads=[mixT[mslot], wbuf],
                  writes=qs, est=est_mm)
                E("s", lambda e, _lo=lo, _hi=hi, _w=w, _d=dst: e.copy(
                    _d.ap[:, _lo:_hi], psT.ap[:, 512:512 + _w]),
                  reads=qs, writes=[dst], est=550)

        def stage_a(p):
            real = p >= HP
            par = p % 2
            xb, x2b = xin[par], x2[par]
            kk8b, vsb, rsb, thTb = (kk8[par], v_sb[par], r_sb[par],
                                    thT[par])
            scur, sprev = states[par], states[1 - par]
            kpreb = kpre[par]
            wfmb = w_fm[par]
            sxTb = sxT[par]
            xe, xep = xe3v[par], xe3v[1 - par]
            lnA, lnB = lnst[0][par], lnst[1][par]
            E("s", lambda e: e.activation(xc1.ap, xb.ap, AF.Copy,
                                          bias=0.0, scale=one_c,
                                          accum_out=lnA["sum"].ap),
              reads=[xb, cc], writes=[xc1, lnA["sum"]], est=1150)
            E("s", lambda e: e.activation(xn_tm.ap, xb.ap, AF.Square,
                                          bias=zero_c, scale=one_c,
                                          accum_out=lnA["ssq"].ap),
              reads=[xb, cc], writes=[xn_tm, lnA["ssq"]], est=1300)
            stats_from_accum(lnA)
            ffn_fn(xb, x2b, lnA, FB1[par], psB, psB_r, psT, psT_r,
                   compact_aux=True)
            stats_from_bn(x2b, lnB, par)
            if p + 2 < NP:
                E("sx%d" % par, lambda e, _r=(p + 2) * TP, _b=xb:
                  e.dma_start(out=_b.ap, in_=x_in[_r:_r + TP, :]),
                  writes=[xb])
            E("v", lambda e: e.tensor_tensor(lnB["msq"].ap,
                                             lnB["mean"].ap,
                                             lnB["rstd"].ap, ALU.mult),
              reads=[lnB["mean"], lnB["rstd"]], writes=[lnB["msq"]],
              est=100)
            E("s", lambda e: e.activation(xn_tm.ap, x2b.ap, AF.Identity,
                                          bias=lnB["msq"].ap,
                                          scale=lnB["rstd"].ap),
              reads=[x2b, lnB["msq"], lnB["rstd"]], writes=[xn_tm],
              est=1150)
            if p > 0:
                E("v", lambda e: e.tensor_copy(xe[:, :, 1:2],
                                               xep[:, :, TP + 1:TP + 2]),
                  reads=[xnT_ext[1 - par]], writes=[xnT_ext[par]])
            nh2 = NCH // 2
            for r in range(2):
                fns = []
                for k in range(r * nh2, (r + 1) * nh2):
                    fns.append(lambda e,
                               _s=xn_tm.ap[:, k * 128:(k + 1) * 128],
                               _d=psT.ap[:, (k % nh2) * 128:
                                         (k % nh2 + 1) * 128]:
                               e.matmul(_d, _s, idb.ap, start=True,
                                        stop=True))
                E("t", fns, reads=[xn_tm, idb], writes=psT_r[0:2],
                  est=nh2 * 95)
                E("s", lambda e, _r=r, _n=nh2: e.copy(
                    xe[:, _r * _n:(_r + 1) * _n, 2:TP + 2],
                    psT.ap[:, 0:_n * 128].rearrange("p (k c) -> p k c",
                                                    k=_n)),
                  reads=psT_r[0:2], writes=[xnT_ext[par]], est=500)
            E("v", lambda e: e.tensor_tensor(
                sxTb.ap.rearrange("p (k c) -> p k c", k=NCH),
                xe[:, :, 2:TP + 2], xe[:, :, 1:TP + 1], ALU.subtract),
              reads=[xnT_ext[par]], writes=[sxTb], est=1250)
            fns = []
            for k in range(NCH):
                fns.append(lambda e, _l=xe[:, k, 2:TP + 2],
                           _r=w1cat.ap[:, k * NW1:(k + 1) * NW1],
                           _st=(k == 0):
                           e.matmul(psT.ap[:, 0:NW1], _l, _r, start=_st,
                                    stop=False))
                fns.append(lambda e, _l=sxTb.ap[:, k * 128:(k + 1) * 128],
                           _r=w1x.ap[:, k * NW1:(k + 1) * NW1],
                           _sp=(k == NCH - 1):
                           e.matmul(psT.ap[:, 0:NW1], _l, _r, start=False,
                                    stop=_sp))
            E("t", fns, reads=[xnT_ext[par], sxTb, w1cat, w1x],
              writes=[psT_r[0]], est=1300)
            E("s", lambda e: e.activation(th.ap, psT.ap[:, 0:NW1], AF.Tanh,
                                          bias=zero_c, scale=one_c),
              reads=[psT_r[0], cc], writes=[th], est=350)
            fns = []
            for m in range(4):
                fns.append(lambda e, _s=th.ap[:, m * DLOW:(m + 1) * DLOW],
                           _d=psT.ap[0:DLOW, m * TP:(m + 1) * TP]:
                           e.matmul(_d, _s, idb.ap, start=True, stop=True))
            E("t", fns, reads=[th, idb], writes=psT_r[0:2], est=450)
            E("s", lambda e, _t=thTb: e.copy(
                _t.ap[0:DLOW, :], psT.ap[0:DLOW, 0:4 * TP]),
              reads=psT_r[0:2], writes=[thTb], est=400)
            E("t", lambda e, _t=thTb: e.matmul(
                psT.ap[0:H, 0:TP], w2aug.ap, _t.ap[:, TP:2 * TP],
                start=True, stop=True),
              reads=[w2aug, thTb], writes=[psT_r[0]], est=200)
            E("s", lambda e, _w=wfmb: e.activation(_w.ap[0:H, :],
                                                   psT.ap[0:H, 0:TP],
                                                   AF.Tanh, bias=zero16,
                                                   scale=half16),
              reads=[psT_r[0], cc], writes=[wfmb], est=250)
            E("v", lambda e, _w=wfmb: e.tensor_scalar(_w.ap[0:H, :],
                                                      _w.ap[0:H, :],
                                                      1.0, 0.5 * E_HALF,
                                                      ALU.add, ALU.mult),
              reads=[wfmb], writes=[wfmb], est=200)
            E("t", lambda e, _t=thTb: e.matmul(
                psT.ap[:, 256:256 + H], _t.ap[:, 2 * TP:3 * TP], a2aug.ap,
                start=True, stop=True),
              reads=[thTb, a2aug], writes=[psT_r[1]], est=200)
            E("s", lambda e: e.activation(a_t.ap, psT.ap[:, 256:256 + H],
                                          AF.Tanh, bias=zero_c,
                                          scale=half_c),
              reads=[psT_r[1], cc], writes=[a_t], est=250)
            E("v", lambda e: e.tensor_scalar(a8.ap, a_t.ap, 1.0 / 16,
                                             1.0 / 16, ALU.mult, ALU.add),
              reads=[a_t], writes=[a8], est=170)
            mix_stream(0, 1, par)
            proj_stream(0, wk, kpreb, NCH * 512 * 0.42 + 150)
            mix_stream(1, 2, par)
            proj_stream(1, wv, vsb, NCH * 512 * 0.42 + 150)
            if real:
                mix_stream(2, 0, par)
                proj_stream(2, wr, rsb, NCH * 512 * 0.42 + 150)
            E("s", lambda e, _kk=kk8b: e.activation(_kk.ap, kpreb.ap,
                                                    AF.Square,
                                                    bias=zero_c,
                                                    scale=one_c),
              reads=[kpreb, cc], writes=[kk8b], est=1300)
            E("v", lambda e, _kk=kk8b: e.tensor_reduce(ssh.ap,
                                                       re3(_kk.ap, H), AX.X,
                                                       ALU.add),
              reads=[kk8b], writes=[ssh], est=1300)
            E("v", lambda e: e.tensor_scalar(sshe.ap, ssh.ap, 1e-24,
                                             1.0 / 64, ALU.max, ALU.mult),
              reads=[ssh], writes=[sshe], est=170)
            E("g", lambda e: e.tensor_tensor(rn.ap, sshe.ap, pwH, ALU.pow),
              reads=[sshe, powc], writes=[rn], est=3600)
            E("v", lambda e, _kk=kk8b, _kp=kpreb: e.tensor_tensor(
                re3(_kk.ap, H), re3(_kp.ap, H), bc3(rn.ap, H, 64),
                ALU.mult),
              reads=[kpreb, rn], writes=[kk8b], est=1250)
            E("v", lambda e: e.tensor_tensor(rna.ap, rn.ap, a8.ap,
                                             ALU.mult),
              reads=[rn, a8], writes=[rna], est=170)
            E("g", lambda e, _kp=kpreb: e.tensor_tensor(
                re3(kvp.ap, H), re3(_kp.ap, H), bc3(rna.ap, H, 64),
                ALU.mult),
              reads=[kpreb, rna], writes=[kvp], est=2500)
            E("v", lambda e, _v=vsb, _kv=kpreb: e.tensor_tensor(
                _kv.ap, kvp.ap, _v.ap, ALU.mult),
              reads=[kvp, vsb], writes=[kpreb], est=700)
            E("t", transpose_blocks(kpreb.ap, NCH, psC),
              reads=[kpreb, idb], writes=psC_r, est=NCH * 110)
            for lo, hi in OC:
                E("s", lambda e, _lo=lo, _hi=hi, _d=kpreb: e.copy(
                    _d.ap[:, _lo:_hi], psC.ap[:, _lo:_hi]),
                  reads=bq(psC_r, lo, hi), writes=[kpreb], est=550)
            for lo, hi in OC:
                fns = []
                for k in range(lo // 128, hi // 128):
                    fns.append(lambda e,
                               _l=eall.ap[:, k * 128:(k + 1) * 128],
                               _d=psC.ap[:, k * 128:(k + 1) * 128],
                               _w=wfmb:
                               e.matmul(_d, _l, _w.ap, start=True,
                                        stop=True))
                E("t", fns, reads=[eall, wfmb],
                  writes=bq(psC_r, lo, hi), est=(hi - lo) // 128 * 240)
            for k in range(NCH):
                lo = k * 128
                init = 0.0 if p == 0 else sprev.ap[:, lo + TP - 1:lo + TP]
                E("v", lambda e, _lo=lo, _i=init, _sc=scur, _kf=kpreb:
                  e.tensor_tensor_scan(
                    _sc.ap[:, _lo:_lo + TP], psC.ap[:, _lo:_lo + TP],
                    _kf.ap[:, _lo:_lo + TP], _i, ALU.mult, ALU.add),
                  reads=[psC_r[k * 128 // 256], kpreb] +
                        ([] if p == 0 else [sprev]),
                  writes=[scur], est=475)

        def stage_b(p):
            par = p % 2
            x2b = x2[par]
            kk8b, vsb, rsb, thTb = (kk8[par], v_sb[par], r_sb[par],
                                    thT[par])
            scur = states[par]
            lnC = lnst[2][par]
            fns = []
            for ci, (lo, hi) in enumerate(OC):
                fns.append(lambda e, _t=thTb, _r=g2z.ap[:, lo:hi],
                           _d=psW.ap[:, lo:hi]:
                           e.matmul(_d, _t.ap[:, 3 * TP:4 * TP], _r,
                                    start=True, stop=True))
            E("t", fns, reads=[thTb, g2z], writes=psW_r, est=500)
            E("s", lambda e: e.activation(tg_sb.ap, psW.ap[:, 0:C],
                                          AF.Tanh, bias=zero_c,
                                          scale=half_c),
              reads=psW_r + [cc], writes=[tg_sb], est=1100)
            fns = []
            for ci, (lo, hi) in enumerate(OC):
                fns.append(lambda e, _t=thTb, _r=v2aug.ap[:, lo:hi],
                           _d=psW.ap[:, lo:hi]:
                           e.matmul(_d, _t.ap[:, 0:TP], _r,
                                    start=True, stop=True))
            E("t", fns, reads=[thTb, v2aug], writes=psW_r, est=500)
            E("s", lambda e: e.activation(vgt.ap, psW.ap[:, 0:C], AF.Tanh,
                                          bias=zero_c, scale=half_c),
              reads=psW_r + [cc], writes=[vgt], est=1100)
            E("s", lambda e, _s=scur: e.copy(stb.ap, _s.ap),
              reads=[scur], writes=[stb], est=950)
            E("t", transpose_blocks(stb.ap, NCH, psC),
              reads=[stb, idb], writes=psC_r, est=NCH * 110)
            E("v", lambda e, _r=rsb: e.tensor_tensor(
                sc1.ap, psC.ap[:, 0:C], _r.ap, ALU.mult),
              reads=psC_r + [rsb], writes=[sc1], est=1350)
            E("g", lambda e, _r=rsb, _kk=kk8b: e.tensor_tensor(
                rkk.ap, _r.ap, _kk.ap, ALU.mult),
              reads=[rsb, kk8b], writes=[rkk], est=2300)
            E("g", lambda e: e.tensor_tensor(rkkr.ap, rkk.ap, rkt.ap,
                                             ALU.mult),
              reads=[rkk, rkt], writes=[rkkr], est=2300)
            E("v", lambda e: e.tensor_reduce(dot.ap, re3(rkkr.ap, H), AX.X,
                                             ALU.add),
              reads=[rkkr], writes=[dot], est=1300)
            E("g", lambda e, _v=vsb: e.tensor_tensor(
                re3(bon.ap, H), re3(_v.ap, H), bc3(dot.ap, H, 64), ALU.mult),
              reads=[vsb, dot], writes=[bon], est=2500)
            E("v", lambda e: e.tensor_tensor(xwkv.ap, sc1.ap, bon.ap,
                                             ALU.add),
              reads=[sc1, bon], writes=[xwkv], est=1250)
            x3v = re3(xwkv.ap, H)
            E("s", lambda e, _kk=kk8b: e.activation(_kk.ap, xwkv.ap,
                                                    AF.Square, bias=zero_c,
                                                    scale=one_c),
              reads=[xwkv, cc], writes=[kk8b], est=1300)
            E("v", lambda e: e.tensor_reduce(gs1.ap, x3v, AX.X, ALU.add),
              reads=[xwkv], writes=[gs1], est=1300)
            E("v", lambda e, _kk=kk8b: e.tensor_reduce(
                gaM.ap, re3(_kk.ap, H), AX.X, ALU.add),
              reads=[kk8b], writes=[gaM], est=1300)
            E("v", lambda e: e.tensor_scalar(gm.ap, gs1.ap, 1.0 / 64, None,
                                             ALU.mult),
              reads=[gs1], writes=[gm], est=170)
            E("v", lambda e: e.tensor_tensor(gd2.ap, gm.ap, gm.ap,
                                             ALU.mult),
              reads=[gm], writes=[gd2], est=170)
            E("v", lambda e: e.scalar_tensor_tensor(gbe.ap, gaM.ap,
                                                    1.0 / 64, gd2.ap,
                                                    ALU.mult, ALU.subtract),
              reads=[gaM, gd2], writes=[gbe], est=170)
            E("v", lambda e: e.tensor_scalar(gbe.ap, gbe.ap, EPS, None,
                                             ALU.add),
              reads=[gbe], writes=[gbe], est=170)
            E("g", lambda e: e.tensor_tensor(grs.ap, gbe.ap, pwH, ALU.pow),
              reads=[gbe, powc], writes=[grs], est=3600)
            E("v", lambda e: e.tensor_tensor(re3(u1.ap, H), x3v,
                                             bc3(gm.ap, H, 64),
                                             ALU.subtract),
              reads=[xwkv, gm], writes=[u1], est=1250)
            E("g", lambda e: e.tensor_tensor(re3(u2.ap, H), re3(u1.ap, H),
                                             bc3(grs.ap, H, 64), ALU.mult),
              reads=[u1, grs], writes=[u2], est=2500)
            E("v", lambda e: e.tensor_tensor(u3.ap, u2.ap, lblw.ap, ALU.add),
              reads=[u2, lblw], writes=[u3], est=700)
            E("v", lambda e: e.scalar_tensor_tensor(xg2.ap, tg_sb.ap, 1.0,
                                                    u3.ap, ALU.add,
                                                    ALU.mult),
              reads=[tg_sb, u3], writes=[xg2], est=900)
            E("t", transpose_blocks(xg2.ap, NCH, psC),
              reads=[xg2, idb], writes=psC_r, est=NCH * 95)
            E("s", lambda e: e.copy(xg2T.ap, psC.ap[:, 0:C]),
              reads=psC_r, writes=[xg2T], est=620)
            for lo, hi in OC:
                fns = []
                for k in range(NCH):
                    fns.append(lambda e,
                               _l=xg2T.ap[:, k * 128:(k + 1) * 128],
                               _r=wo.ap[:, k * C + lo:k * C + hi],
                               _d=psW.ap[:, lo:hi],
                               _st=(k == 0), _sp=(k == NCH - 1):
                               e.matmul(_d, _l, _r, start=_st, stop=_sp))
                E("t", fns, reads=[xg2T, wo], writes=bq(psW_r, lo, hi),
                  est=NCH * (hi - lo) * 0.42 + 150)
            E("v", lambda e, _x2=x2b: e.scalar_tensor_tensor(
                x3.ap, psW.ap[:, 0:C], 1.0, _x2.ap, ALU.mult, ALU.add),
              reads=psW_r + [x2b], writes=[x3], est=1300)
            stats_from_bn(x3, lnC, par)
            ffn_fn(x3, xout, lnC, FB2, psW, psW_r, psC, psC_r)
            E("sst", lambda e, _r=(p - HP) * TP, _x=xout:
              e.dma_start(out=y_out[_r:_r + TP, :], in_=_x.ap),
              reads=[xout], est=2500)

        for p in range(NP):
            stage_a(p)
            if p >= HP:
                stage_b(p)

        P.schedule()
        with nc.Block() as block:
            @block.vector
            def _(v):
                for op in P.seq_ops["v"]:
                    op(v)

            @block.scalar
            def _(s):
                for op in P.seq_ops["s"]:
                    op(s)

            @block.tensor
            def _(t):
                for op in P.seq_ops["t"]:
                    op(t)

            @block.gpsimd
            def _(g):
                for op in P.seq_ops["g"]:
                    op(g)

            @block.sync
            def _(sy):
                for op in P.seq_ops["sy"]:
                    op(sy)

    return nc


def prep_shared(inp, C, pre="L0_attn_", fpre="L0_ffn_"):
    NCH = C // 128
    H = C // 64
    a = {k: np.asarray(v) for k, v in inp.items()}
    f = np.float64 if any(v.dtype == np.float64 for v in a.values()) \
        else np.float32
    w0 = a[pre + "w0"].reshape(1, H)
    a0 = a[pre + "a0"].reshape(1, H)
    v0 = a[pre + "v0"].reshape(1, C)
    colsplit = lambda v: np.ascontiguousarray(v.reshape(NCH, 128).T)
    tile = lambda v: np.ascontiguousarray(np.tile(v.reshape(1, C), (128, 1)))
    shared = {
        "wtT": np.ascontiguousarray(a[fpre + "weight"].T),
        "wr": np.ascontiguousarray(a[pre + "Wr"]),
        "wk": np.ascontiguousarray(a[pre + "Wk"]),
        "wv": np.ascontiguousarray(a[pre + "Wv"]),
        "wo": np.ascontiguousarray(a[pre + "Wo"]),
        "ug": np.ascontiguousarray(np.concatenate(
            [a[fpre + "lora_U"].reshape(ERK, C).T, a[fpre + "gate_w"].T],
            axis=1)),
        "vcat": np.ascontiguousarray(
            a[fpre + "lora_V"].transpose(0, 2, 1).reshape(ERK, C)),
        "w1cat": np.ascontiguousarray(np.concatenate(
            [a[pre + "v1"], a[pre + "w1"], a[pre + "a1"], a[pre + "g1"]],
            axis=1)),
        "w2aug": np.ascontiguousarray(np.concatenate([a[pre + "w2"], w0], 0)),
        "a2aug": np.ascontiguousarray(np.concatenate([a[pre + "a2"], a0], 0)),
        "v2aug": np.ascontiguousarray(np.concatenate([a[pre + "v2"], v0], 0)),
        "g2z": np.ascontiguousarray(np.concatenate(
            [a[pre + "g2"], np.zeros((1, C), f)], 0)),
        "xmix": np.ascontiguousarray(np.concatenate(
            [colsplit(a[pre + "x_" + s]) for s in "rkvwag"], axis=1)),
        "kkt": tile(a[pre + "k_k"]),
        "lnwc": colsplit(a[pre + "ln_w"]),
        "lnw_tile": tile(a[pre + "ln_w"]),
        "lnb_tile": tile(a[pre + "ln_b"]),
        "rk_tile": tile(a[pre + "r_k"]),
        "identb": np.eye(128, dtype=ml_dtypes.bfloat16),
        "identf": np.eye(128, dtype=f),
        "ones512": np.ones((1, 4 * TP), dtype=ml_dtypes.bfloat16),
    }
    eall = np.zeros((128, C), f)
    for k in range(NCH):
        for pp in range(128):
            eall[2 * k + pp // 64, k * 128 + pp] = 1.0
    shared["eall"] = eall
    cconst = np.zeros((128, 4), f)
    cconst[:, CC_ONE] = 1.0
    cconst[:, CC_ZERO] = 0.0
    cconst[:, CC_HALF] = 0.5
    shared["cconst"] = cconst
    return shared


def make_in_maps(x, shared, C, HP, RP, n_seq_shards=4):
    B, T, _ = x.shape
    NP = HP + RP
    per = RP * TP
    halo = HP * TP
    in_maps = []
    for core in range(B * n_seq_shards):
        b, q = core // n_seq_shards, core % n_seq_shards
        start = q * per - halo
        rows = np.zeros((NP * TP, C), x.dtype)
        lo = max(start, 0)
        rows[lo - start:, :] = x[b, lo:q * per + per, :]
        m = dict(shared)
        m["x"] = rows
        in_maps.append(m)
    return in_maps


_NC_CACHE = {}


def kernel(**inputs):
    C, HP, RP = 1024, 1, 16
    x = np.asarray(inputs["L0_attn_x"])
    B, T, _ = x.shape
    per = RP * TP
    shared = prep_shared(inputs, C)
    key = (C, HP, RP)
    if key not in _NC_CACHE:
        _NC_CACHE[key] = build_nc(C, HP, RP)
    nc = _NC_CACHE[key]
    in_maps = make_in_maps(x, shared, C, HP, RP)
    res = run_bass_kernel_spmd(nc, in_maps, core_ids=list(range(8)))
    odt = np.asarray(res.results[0]["y"]).dtype
    out = np.empty((B, T, C), np.float32 if odt == np.float32 else odt)
    for core in range(8):
        b, q = core // 4, core % 4
        out[b, q * per:(q + 1) * per, :] = res.results[core]["y"]
    return out


# version 78
# speedup vs baseline: 1.0038x; 1.0016x over previous
import numpy as np
import ml_dtypes
from contextlib import ExitStack

import concourse.bass as bass
import concourse.mybir as mybir
from concourse.bass_utils import run_bass_kernel_spmd

F32 = mybir.dt.float32
BF16 = mybir.dt.bfloat16
AF = mybir.ActivationFunctionType
ALU = mybir.AluOpType
AX = mybir.AxisListType

TP = 128
DLOW = 64
NE, NR = 8, 16
ERK = NE * NR
K2 = DLOW + 1
EPS = 1e-5
E_HALF = float(np.exp(-0.5))
CC_ONE, CC_ZERO, CC_HALF = 0, 1, 2
DEBUG_LOG = None


class _Stream:
    def __init__(self, name, sem, dma):
        self.name, self.sem, self.dma = name, sem, dma
        self.count = 0


class Buf:
    __slots__ = ("ap", "writer", "readers")

    def __init__(self, ap):
        self.ap, self.writer, self.readers = ap, None, []


class _Op:
    __slots__ = ("idx", "st", "fns", "deps", "est", "prio", "start", "fin",
                 "tok")

    def __init__(self, idx, st, fns, deps, est):
        self.idx, self.st, self.fns, self.deps, self.est = (
            idx, st, fns, deps, est)
        self.prio = 0.0
        self.start = self.fin = 0.0
        self.tok = None


SCHED_SEED = 123


class Prog:

    SEQ_OF = {"v": "v", "s": "s", "t": "t", "g": "g", "gd": "g", "gd2": "g",
              "gd3": "g",
              "syc": "sy", "sf0": "sy", "sf1": "sy", "sx0": "sy", "sx1": "sy",
              "sst": "sy"}
    XLAT = 340.0
    OPFIX = 90.0

    def __init__(self, nc, stack):
        self.nc = nc
        self.streams = {}
        for nm, dma in [("v", False), ("s", False), ("t", False), ("g", False),
                        ("gd", True), ("gd2", True), ("gd3", True),
                        ("syc", True),
                        ("sf0", True), ("sf1", True), ("sx0", True),
                        ("sx1", True), ("sst", True)]:
            sem = stack.enter_context(nc.semaphore(name="sem_" + nm))
            self.streams[nm] = _Stream(nm, sem, dma)
        self.ops = []
        self.final_stream = None

    def emit(self, st_name, fn, reads=(), writes=(), est=400.0):
        fns = tuple(fn) if isinstance(fn, (list, tuple)) else (fn,)
        deps = set()
        for b in reads:
            if b.writer is not None:
                deps.add(b.writer)
        for b in writes:
            if b.writer is not None:
                deps.add(b.writer)
            deps.update(b.readers)
        op = _Op(len(self.ops), st_name, fns, deps, float(est))
        self.ops.append(op)
        for b in reads:
            b.readers.append(op.idx)
        for b in writes:
            b.writer = op.idx
            b.readers = []
        return op

    def schedule(self):
        ops = self.ops
        n = len(ops)
        barrier_members = {}
        for op in ops:
            if op.st in ("gd", "gd2", "gd3", "syc"):
                barrier_members.setdefault(op.st, set()).add(op.idx)
        for op in ops:
            ext = None
            for d in op.deps:
                dst = ops[d].st
                if dst in barrier_members and op.st != dst:
                    ext = (ext or set()) | barrier_members[dst]
            if ext:
                op.deps = (op.deps | ext) - {op.idx}
        succs = [[] for _ in range(n)]
        for op in ops:
            for d in op.deps:
                succs[d].append(op.idx)
        if SCHED_SEED:
            rng = np.random.default_rng(SCHED_SEED)
            jit = rng.uniform(0.92, 1.08, n)
        else:
            jit = None
        for i in range(n - 1, -1, -1):
            p = ops[i].est
            if jit is not None:
                p *= jit[i]
            m = 0.0
            for s in succs[i]:
                if ops[s].prio > m:
                    m = ops[s].prio
            ops[i].prio = p + m
        for op in ops:
            if op.st == "gd3":
                op.prio = 3e12
            elif op.st == "gd":
                op.prio = 2e12
            elif op.st == "gd2":
                op.prio = 1e12
        SLACK = 400.0
        seqs = sorted(set(self.SEQ_OF.values()))
        seq_time = {q: 0.0 for q in seqs}
        ndeps = [len(op.deps) for op in ops]
        ready = {q: set() for q in seqs}
        est_ready = [0.0] * n
        for op in ops:
            if not op.deps:
                ready[self.SEQ_OF[op.st]].add(op.idx)
        seq_lists = {q: [] for q in seqs}
        scheduled = 0
        while scheduled < n:
            best = None
            for q in seqs:
                rq = ready[q]
                if not rq:
                    continue
                emin = min(est_ready[i] for i in rq)
                t0 = max(seq_time[q], emin)
                cand, cprio = None, -1.0
                for i in rq:
                    if est_ready[i] <= t0 + SLACK and ops[i].prio > cprio:
                        cand, cprio = i, ops[i].prio
                start = max(seq_time[q], est_ready[cand])
                if (best is None or start < best[0] - 1e-9 or
                        (abs(start - best[0]) < 1e-9 and cprio > best[3])):
                    best = (start, q, cand, cprio)
            start, q, i, _ = best
            op = ops[i]
            ready[q].discard(i)
            op.start = start
            op.fin = start + op.est + self.OPFIX
            seq_time[q] = op.fin
            seq_lists[q].append(i)
            scheduled += 1
            for s in succs[i]:
                ndeps[s] -= 1
                if ndeps[s] == 0:
                    est_ready[s] = max(
                        [0.0] + [ops[d].fin + self.XLAT
                                 for d in ops[s].deps])
                    ready[self.SEQ_OF[ops[s].st]].add(s)
        assert scheduled == n, (scheduled, n)
        import os
        if os.environ.get("SCHED_REPORT"):
            busy = {}
            for op in ops:
                busy[self.SEQ_OF[op.st]] = busy.get(self.SEQ_OF[op.st], 0) + op.est
            print("SCHED makespan %.1f us; busy: %s" % (
                max(o.fin for o in ops) / 1e3,
                {k: round(v / 1e3, 1) for k, v in sorted(busy.items())}))
            import traceback as _tb
            last = max(ops, key=lambda o: o.fin)
            chain = []
            cur = last
            while cur is not None:
                chain.append(cur)
                best = None
                for d in cur.deps:
                    if abs(ops[d].fin + self.XLAT - cur.start) < 1e-6:
                        best = ops[d]
                        break
                if best is None:
                    q = self.SEQ_OF[cur.st]
                    cand = [o for o in ops
                            if self.SEQ_OF[o.st] == q and o.fin <= cur.start
                            and o is not cur]
                    if cand:
                        b = max(cand, key=lambda o: o.fin)
                        if abs(b.fin - cur.start) < 1e-6:
                            best = b
                cur = best
            chain.reverse()
            import collections
            agg = collections.Counter()
            for o in chain:
                agg[(o.st, round(o.est, -1))] += 1
            print("SCHED critical chain: %d ops" % len(chain))
            for (stn, e), cnt in agg.most_common(20):
                print("   %s est=%s x%d  (%.1f us)" % (stn, e, cnt,
                                                       e * cnt / 1e3))
        for q in seqs:
            for i in seq_lists[q]:
                op = ops[i]
                st = self.streams[op.st]
                st.count += 16 if st.dma else 1
                op.tok = (op.st, st.count)
        self.seq_ops = {q: [] for q in seqs}
        for q in seqs:
            waited = {}
            for i in seq_lists[q]:
                op = ops[i]
                need = {}
                for d in op.deps:
                    snm, val = ops[d].tok
                    if snm == op.st and self.streams[snm].dma:
                        continue
                    if snm in ("gd", "gd2", "gd3", "syc"):
                        val = self.streams[snm].count
                    if need.get(snm, 0) < val:
                        need[snm] = val
                waits = []
                for snm, val in sorted(need.items()):
                    if waited.get(snm, 0) < val:
                        waited[snm] = val
                        waits.append((self.streams[snm].sem, val))
                sem = self.streams[op.st].sem
                inc = 16 if self.streams[op.st].dma else 1

                def run(eng, _w=tuple(waits), _f=op.fns, _s=sem, _i=inc):
                    for s, v in _w:
                        eng.wait_ge(s, v)
                    last = None
                    for f in _f:
                        last = f(eng)
                    last.then_inc(_s, _i)

                self.seq_ops[q].append(run)
        fw = []
        for nm in ("sst", "sx0", "sx1", "syc", "sf0", "sf1"):
            st = self.streams[nm]
            if st.count:
                fw.append((st.sem, st.count))

        def run_final(eng, _w=tuple(fw)):
            for s, v in _w:
                eng.wait_ge(s, v)

        self.seq_ops["sy"].append(run_final)


def out_chunks(C):
    return [(i, min(i + 512, C)) for i in range(0, C, 512)]


def build_nc(C, HP, RP):
    NCH = C // 128
    H = C // 64
    NP = HP + RP
    ROWS = NP * TP
    NW1 = 4 * DLOW
    UGW = ERK + NE
    OC = out_chunks(C)

    nc = bass.Bass("TRN2", target_bir_lowering=False)
    dt = nc.dram_tensor

    x_in = dt("x", [ROWS, C], F32, kind="ExternalInput").ap()
    wtT_in = dt("wtT", [C, C], F32, kind="ExternalInput").ap()
    wr_in = dt("wr", [C, C], F32, kind="ExternalInput").ap()
    wk_in = dt("wk", [C, C], F32, kind="ExternalInput").ap()
    wv_in = dt("wv", [C, C], F32, kind="ExternalInput").ap()
    wo_in = dt("wo", [C, C], F32, kind="ExternalInput").ap()
    ug_in = dt("ug", [C, UGW], F32, kind="ExternalInput").ap()
    vcat_in = dt("vcat", [ERK, C], F32, kind="ExternalInput").ap()
    w1cat_in = dt("w1cat", [C, NW1], F32, kind="ExternalInput").ap()
    w2aug_in = dt("w2aug", [K2, H], F32, kind="ExternalInput").ap()
    a2aug_in = dt("a2aug", [K2, H], F32, kind="ExternalInput").ap()
    v2aug_in = dt("v2aug", [K2, C], F32, kind="ExternalInput").ap()
    g2z_in = dt("g2z", [K2, C], F32, kind="ExternalInput").ap()
    xmix_in = dt("xmix", [128, 6 * NCH], F32, kind="ExternalInput").ap()
    kkt_in = dt("kkt", [128, C], F32, kind="ExternalInput").ap()
    lnwc_in = dt("lnwc", [128, NCH], F32, kind="ExternalInput").ap()
    lnw_tile_in = dt("lnw_tile", [128, C], F32, kind="ExternalInput").ap()
    lnb_tile_in = dt("lnb_tile", [128, C], F32, kind="ExternalInput").ap()
    rk_tile_in = dt("rk_tile", [128, C], F32, kind="ExternalInput").ap()
    identb_in = dt("identb", [128, 128], BF16, kind="ExternalInput").ap()
    identf_in = dt("identf", [128, 128], F32, kind="ExternalInput").ap()
    eall_in = dt("eall", [128, C], F32, kind="ExternalInput").ap()
    cconst_in = dt("cconst", [128, 4], F32, kind="ExternalInput").ap()
    ones512_in = dt("ones512", [1, 4 * TP], BF16, kind="ExternalInput").ap()
    y_out = dt("y", [RP * TP, C], F32, kind="ExternalOutput").ap()

    stack = ExitStack()

    _nn = [0]

    def sb(shape, dty):
        _nn[0] += 1
        return Buf(stack.enter_context(
            nc.sbuf_tensor("sb%d" % _nn[0], shape, dty))[:])

    def psum(shape):
        _nn[0] += 1
        return Buf(stack.enter_context(
            nc.psum_tensor("ps%d" % _nn[0], shape, F32))[:])

    with stack:
        wt = sb([128, NCH * C], BF16)
        wr = sb([128, NCH * C], BF16)
        wk = sb([128, NCH * C], BF16)
        wv = sb([128, NCH * C], BF16)
        wo = sb([128, NCH * C], BF16)
        ug = sb([128, NCH * UGW], BF16)
        vcat = sb([ERK, C], BF16)
        w1cat = sb([128, NCH * NW1], BF16)
        w1x = sb([128, NCH * NW1], BF16)
        w2aug = sb([K2, H], BF16)
        a2aug = sb([K2, H], BF16)
        v2aug = sb([K2, C], BF16)
        g2z = sb([K2, C], BF16)
        xmix = sb([128, 6 * NCH], BF16)
        lnwc = sb([128, NCH], F32)
        lnwh = sb([128, NCH], F32)
        lblw = sb([128, C], BF16)
        rkt = sb([128, C], F32)
        idb = sb([128, 128], BF16)
        eall = sb([128, C], BF16)
        cc = sb([128, 4], F32)
        powc = sb([128, 16], F32)

        xin = [sb([128, C], F32) for _ in range(2)]
        x2 = [sb([128, C], F32) for _ in range(2)]
        x3 = sb([128, C], F32)
        xn_tm = sb([128, C], BF16)
        xc1 = sb([128, C], BF16)
        xnT = [sb([128, C], BF16) for _ in range(2)]
        xnT_ext = [sb([128, NCH * (TP + 2)], BF16) for _ in range(2)]
        sxT = [sb([128, C], BF16) for _ in range(2)]
        mixT = [sb([128, C], BF16) for _ in range(3)]
        th = sb([128, NW1], BF16)
        thT = [sb([K2, 4 * TP], BF16) for _ in range(2)]
        xg2 = sb([128, C], BF16)
        xg2T = sb([128, C], BF16)
        xn3_tm = xg2
        xn3T = xg2T
        tw = sb([128, ERK], BF16)
        twT = sb([ERK, TP], BF16)
        tw2 = sb([128, ERK], BF16)
        twT2 = sb([ERK, TP], BF16)
        tg_sb = sb([128, C], BF16)

        kpre = [sb([128, C], BF16) for _ in range(2)]
        kk8 = [sb([128, C], F32) for _ in range(2)]
        v_sb = [sb([128, C], BF16) for _ in range(2)]
        u2 = sb([128, C], BF16)
        u3 = u2

        states = [sb([128, C], F32) for _ in range(2)]
        r_sb = [sb([128, C], F32) for _ in range(2)]
        rkk = sb([128, C], F32)
        xwkv = sb([128, C], F32)
        xout = sb([128, C], F32)
        stgf = [x2[0], x2[1]]
        lnw_t = rkk
        sc1 = x3
        vgt = xg2
        kkt = x3
        kvp = mixT[2]
        stb = xg2T
        rkkr = xwkv
        u1 = xwkv
        bon = rkk

        ln12 = [sb([128, 12], F32) for _ in range(2)]
        lnmv = [sb([128, 2], F32) for _ in range(2)]
        lnst = [[dict(sum=sb([128, 1], F32), ssq=sb([128, 1], F32),
                      mean=sb([128, 1], F32), rstd=sb([128, 1], F32),
                      veps=sb([128, 1], F32), msq=sb([128, 1], F32),
                      rstdh=sb([128, 1], F32))
                 for _ in range(2)] for _ in range(3)]
        gexp = sb([128, NE], F32)
        gsum = sb([128, 1], F32)
        grec = sb([128, 1], F32)
        gate = sb([128, NE], F32)
        gexp2 = sb([128, NE], F32)
        gsum2 = sb([128, 1], F32)
        grec2 = sb([128, 1], F32)
        gate2 = sb([128, NE], F32)
        w_fm = [sb([128, TP], BF16) for _ in range(2)]
        a_t = sb([128, H], F32)
        a8 = sb([128, H], F32)
        ssh = sb([128, H], F32)
        rna = sb([128, H], F32)
        sshe = sb([128, H], F32)
        rn = sb([128, H], F32)
        dot = sshe
        gs1 = sb([128, H], F32)
        gd2 = sb([128, H], F32)
        gm = ssh
        gaM = rna
        grs = rn
        gbe = gd2

        psT = psum([128, 1024])
        psB = psum([128, 1024])
        psW = psum([128, 1024])
        psC = psum([128, 1024])
        quarters = lambda ps: [Buf(ps.ap[:, i * 256:(i + 1) * 256])
                               for i in range(4)]
        psT_r = quarters(psT)
        psB_r = quarters(psB)
        psW_r = quarters(psW)
        psC_r = quarters(psC)

        def bq(ps_r, lo, hi):
            return ps_r[lo // 256:(hi + 255) // 256]

        P = Prog(nc, stack)
        E = P.emit

        one_c = cc.ap[:, CC_ONE:CC_ONE + 1]
        zero_c = cc.ap[:, CC_ZERO:CC_ZERO + 1]
        half_c = cc.ap[:, CC_HALF:CC_HALF + 1]
        zero16 = cc.ap[0:H, CC_ZERO:CC_ZERO + 1]
        half16 = cc.ap[0:H, CC_HALF:CC_HALF + 1]

        def bc3(small_ap, nh, wide):
            return small_ap.broadcast_to([128, nh, wide])

        def re3(ap_, nh):
            return ap_.rearrange("p (h d) -> p h d", h=nh)

        xe3v = [xnT_ext[i].ap.rearrange("p (k c) -> p k c", k=NCH)
                for i in range(2)]

        E("sx0", lambda e: e.dma_start(out=xin[0].ap, in_=x_in[0:TP, :]),
          writes=[xin[0]])
        E("sx1", lambda e: e.dma_start(out=xin[1].ap, in_=x_in[TP:2 * TP, :]),
          writes=[xin[1]])
        lnb_stg = xout
        for buf, src in [(lnb_stg, lnb_tile_in), (lnw_t, lnw_tile_in),
                         (rkt, rk_tile_in),
                         (cc, cconst_in), (kkt, kkt_in),
                         (lnwc, lnwc_in)]:
            E("syc", lambda e, b=buf, s=src: e.dma_start(out=b.ap, in_=s),
              writes=[buf])
        E("syc", lambda e: e.dma_start(out=thT[0].ap[DLOW:K2, :],
                                       in_=ones512_in), writes=[thT[0]])
        E("syc", lambda e: e.dma_start(out=thT[1].ap[DLOW:K2, :],
                                       in_=ones512_in), writes=[thT[1]])

        def cload(stream, group, buf, src, chunked=False):
            s = src.rearrange("(k p) c -> p k c", p=128) if chunked else src
            o = (buf.ap.rearrange("p (k c) -> p k c", k=NCH) if chunked
                 else buf.ap)
            E(stream, lambda e, _o=o, _s=s: e.dma_start(out=_o, in_=_s),
              writes=[buf])
            group.append(buf)

        grpA, grpB = [], []
        cload("gd", grpA, ug, ug_in, True)
        cload("gd", grpA, vcat, vcat_in)
        cload("gd", grpA, idb, identb_in)
        cload("gd", grpA, xmix, xmix_in)
        cload("gd", grpA, eall, eall_in)
        cload("gd2", grpB, w1cat, w1cat_in, True)
        cload("gd2", grpB, w2aug, w2aug_in)
        cload("gd2", grpB, a2aug, a2aug_in)
        cload("gd2", grpB, v2aug, v2aug_in)
        cload("gd2", grpB, g2z, g2z_in)
        cload("gd2", grpB, wr, wr_in, True)
        cload("gd2", grpB, wv, wv_in, True)

        E("v", lambda e: e.memset(powc.ap, -0.5), writes=[powc])
        pw1 = powc.ap[:, 0:1]
        pwH = powc.ap[:, 0:H]
        E("v", lambda e: e.tensor_scalar(lnwh.ap, lnwc.ap, 0.5, None,
                                         ALU.mult),
          reads=[lnwc], writes=[lnwh])
        E("v", lambda e: e.reciprocal(lnw_t.ap, lnw_t.ap),
          reads=[lnw_t], writes=[lnw_t])
        E("v", lambda e: e.tensor_tensor(lblw.ap, lnb_stg.ap, lnw_t.ap,
                                         ALU.mult),
          reads=[lnb_stg, lnw_t], writes=[lblw])
        E("v", lambda e: e.tensor_scalar(rkt.ap, rkt.ap, 0.125, None,
                                         ALU.mult),
          reads=[rkt], writes=[rkt])
        for i in range(2):
            E("v", lambda e, _b=xnT_ext[i]: e.memset(_b.ap, 0.0),
              writes=[xnT_ext[i]])
            E("v", lambda e, _b=w_fm[i]: e.memset(_b.ap, 0.0),
              writes=[w_fm[i]])
        mix_bc = (xmix.ap[:, 2 * NCH:6 * NCH]
                  .rearrange("p (m k) -> p k m", m=4)[:, :, :, None]
                  .broadcast_to([128, NCH, 4, DLOW]))
        E("v", lambda e: e.tensor_tensor(
            w1x.ap.rearrange("p (k m d) -> p k m d", k=NCH, m=4),
            w1cat.ap.rearrange("p (k m d) -> p k m d", k=NCH, m=4),
            mix_bc, ALU.mult),
          reads=[w1cat, xmix], writes=[w1x])

        def folded_load(dst, src, fold):
            for k in range(NCH):
                s = stgf[k % 2]
                E("sf%d" % (k % 2), lambda e, _s=s, _k=k, _src=src:
                  e.dma_start(out=_s.ap,
                              in_=_src[_k * 128:(_k + 1) * 128, :]),
                  writes=[s])
                fold(k, s, dst)

        for k in range(NCH):
            E("gd3", lambda e, _k=k: e.dma_start(
                out=wt.ap[:, _k * C:(_k + 1) * C],
                in_=wtT_in[_k * 128:(_k + 1) * 128, :]),
              writes=[wt])
        folded_load(wk, wk_in, lambda k, s, d: E(
            "g", lambda e, _k=k, _s=s, _d=d: e.tensor_tensor(
                _d.ap[:, _k * C:(_k + 1) * C], _s.ap, kkt.ap, ALU.mult),
            reads=[s, kkt], writes=[d]))
        folded_load(wo, wo_in, lambda k, s, d: E(
            "s", lambda e, _k=k, _s=s, _d=d: e.activation(
                _d.ap[:, _k * C:(_k + 1) * C], _s.ap, AF.Copy,
                bias=0.0, scale=lnwh.ap[:, _k:_k + 1]),
            reads=[s, lnwh, cc], writes=[d]))

        def stats_from_accum(st):
            E("v", [lambda e: e.tensor_scalar(st["mean"].ap, st["sum"].ap,
                                              -1.0 / C, None, ALU.mult),
                    lambda e: e.tensor_scalar(st["veps"].ap, st["ssq"].ap,
                                              1.0 / C, EPS, ALU.mult,
                                              ALU.add)],
              reads=[st["sum"], st["ssq"]],
              writes=[st["mean"], st["veps"]], est=250)
            E("v", lambda e: e.tensor_tensor(st["msq"].ap, st["mean"].ap,
                                             st["mean"].ap, ALU.mult),
              reads=[st["mean"]], writes=[st["msq"]], est=100)
            E("v", lambda e: e.tensor_tensor(st["veps"].ap, st["veps"].ap,
                                             st["msq"].ap, ALU.subtract),
              reads=[st["veps"], st["msq"]], writes=[st["veps"]], est=100)
            E("g", lambda e: e.tensor_tensor(st["rstd"].ap, st["veps"].ap,
                                             pw1, ALU.pow),
              reads=[st["veps"], powc], writes=[st["rstd"]], est=1500)
            E("v", lambda e: e.tensor_scalar(st["rstdh"].ap, st["rstd"].ap,
                                             0.5, None, ALU.mult),
              reads=[st["rstd"]], writes=[st["rstdh"]], est=100)

        def stats_from_bn(src_buf, st, par):
            wdt = C // 2
            l12, lmv = ln12[par], lnmv[par]
            E("v", [lambda e: e.bn_stats(l12.ap[:, 0:6],
                                         src_buf.ap[:, 0:wdt]),
                    lambda e: e.bn_stats(l12.ap[:, 6:12],
                                         src_buf.ap[:, wdt:C])],
              reads=[src_buf], writes=[l12])
            E("v", lambda e: e.bn_aggr(lmv.ap, l12.ap),
              reads=[l12], writes=[lmv])
            E("v", lambda e: e.tensor_scalar(st["mean"].ap, lmv.ap[:, 0:1],
                                             -1.0, None, ALU.mult),
              reads=[lmv], writes=[st["mean"]])
            E("v", lambda e: e.tensor_scalar(st["veps"].ap, lmv.ap[:, 1:2],
                                             EPS, None, ALU.add),
              reads=[lmv], writes=[st["veps"]])
            E("g", lambda e: e.tensor_tensor(st["rstd"].ap, st["veps"].ap,
                                             pw1, ALU.pow),
              reads=[st["veps"], powc], writes=[st["rstd"]], est=1500)
            E("v", lambda e: e.tensor_scalar(st["rstdh"].ap, st["rstd"].ap,
                                             0.5, None, ALU.mult),
              reads=[st["rstd"]], writes=[st["rstdh"]], est=100)

        def transpose_blocks(src_ap, n_blk, ps, col0=0):
            fns = []
            for k in range(n_blk):
                fns.append(lambda e,
                           _s=src_ap[:, k * 128:(k + 1) * 128],
                           _d=ps.ap[:, col0 + k * 128:col0 + (k + 1) * 128]:
                           e.matmul(_d, _s, idb.ap, start=True, stop=True))
            return fns

        def big_mm_group(lhs_ap_of, w_buf, ps_buf, close=True):
            fns = []
            for k in range(NCH):
                lhs = lhs_ap_of(k)
                for ci, (lo, hi) in enumerate(OC):
                    fns.append(lambda e, _l=lhs,
                               _r=w_buf.ap[:, k * C + lo:k * C + hi],
                               _d=ps_buf.ap[:, lo:hi],
                               _st=(k == 0),
                               _sp=(close and k == NCH - 1):
                               e.matmul(_d, _l, _r, start=_st, stop=_sp))
            return fns

        def ffn_fn(x_buf, out_buf, st, fb, bps, bps_r, aps, aps_r,
                   compact_aux=False):
            E("v", lambda e: e.tensor_scalar(fb["xc"].ap, x_buf.ap,
                                             st["mean"].ap, None,
                                             ALU.add),
              reads=[x_buf, st["mean"]], writes=[fb["xc"]])
            if compact_aux:
                nh = NCH // 2
                for r in range(2):
                    fns = []
                    for k in range(r * nh, (r + 1) * nh):
                        fns.append(lambda e,
                                   _s=fb["xc"].ap[:, k * 128:(k + 1) * 128],
                                   _d=aps.ap[:, (k % nh) * 128:
                                             (k % nh + 1) * 128]:
                                   e.matmul(_d, _s, idb.ap, start=True,
                                            stop=True))
                    E("t", fns, reads=[fb["xc"], idb], writes=aps_r[0:2],
                      est=nh * 95)
                    E("s", lambda e, _r=r, _w=nh * 128: e.copy(
                        fb["xT"].ap[:, _r * _w:(_r + 1) * _w],
                        aps.ap[:, 0:_w]),
                      reads=aps_r[0:2], writes=[fb["xT"]], est=500)
            else:
                E("t", transpose_blocks(fb["xc"].ap, NCH, aps),
                  reads=[fb["xc"], idb], writes=aps_r, est=NCH * 95)
                E("s", lambda e: e.copy(fb["xT"].ap, aps.ap[:, 0:C]),
                  reads=aps_r, writes=[fb["xT"]], est=950)
            twc = 384 if compact_aux else 512
            twq = aps_r[1] if compact_aux else aps_r[2]
            fns = []
            for k in range(NCH):
                fns.append(lambda e,
                           _l=fb["xT"].ap[:, k * 128:(k + 1) * 128],
                           _r=ug.ap[:, k * UGW:(k + 1) * UGW],
                           _st=(k == 0), _sp=(k == NCH - 1):
                           e.matmul(aps.ap[:, 0:UGW], _l, _r,
                                    start=_st, stop=_sp))
            E("t", fns, reads=[fb["xT"], ug], writes=[aps_r[0]], est=800)
            for lo, hi in OC:
                fns = []
                for k in range(NCH):
                    fns.append(lambda e,
                               _l=fb["xT"].ap[:, k * 128:(k + 1) * 128],
                               _r=wt.ap[:, k * C + lo:k * C + hi],
                               _d=bps.ap[:, lo:hi], _st=(k == 0):
                               e.matmul(_d, _l, _r, start=_st, stop=False))
                E("t", fns, reads=[fb["xT"], wt],
                  writes=bq(bps_r, lo, hi), est=NCH * (hi - lo) * 0.42 + 150)
            E("s", lambda e: e.activation(fb["gexp"].ap, aps.ap[:, ERK:UGW],
                                          AF.Exp, bias=zero_c,
                                          scale=st["rstd"].ap,
                                          accum_out=fb["gsum"].ap),
              reads=[aps_r[0], st["rstd"], cc],
              writes=[fb["gexp"], fb["gsum"]])
            E("v", lambda e: e.reciprocal(fb["grec"].ap, fb["gsum"].ap),
              reads=[fb["gsum"]], writes=[fb["grec"]])
            E("v", lambda e: e.tensor_scalar(fb["gate"].ap, fb["gexp"].ap,
                                             fb["grec"].ap, 1.0 / NR,
                                             ALU.mult, ALU.mult),
              reads=[fb["gexp"], fb["grec"]], writes=[fb["gate"]])
            E("v", lambda e: e.tensor_tensor(
                re3(fb["tw"].ap, NE), re3(aps.ap[:, 0:ERK], NE),
                bc3(fb["gate"].ap, NE, NR), ALU.mult),
              reads=[aps_r[0], fb["gate"]], writes=[fb["tw"]])
            E("t", lambda e: e.matmul(aps.ap[:, twc:twc + 128], fb["tw"].ap,
                                      idb.ap, start=True, stop=True),
              reads=[fb["tw"], idb], writes=[twq], est=200)
            E("s", lambda e: e.copy(fb["twT"].ap, aps.ap[:, twc:twc + 128]),
              reads=[twq], writes=[fb["twT"]], est=300)
            for lo, hi in OC:
                E("t", lambda e, _r=vcat.ap[:, lo:hi], _d=bps.ap[:, lo:hi]:
                  e.matmul(_d, fb["twT"].ap, _r, start=False, stop=True),
                  reads=[fb["twT"], vcat], writes=bq(bps_r, lo, hi),
                  est=900)
            E("v", lambda e: e.scalar_tensor_tensor(
                out_buf.ap, bps.ap[:, 0:C], st["rstdh"].ap, x_buf.ap,
                ALU.mult, ALU.add),
              reads=bps_r + [x_buf, st["rstdh"]], writes=[out_buf], est=1300)

        def mix_stream(mslot, mixcol, par, eng="v"):
            mb = mixT[mslot]
            xe3_ = xe3v[par]
            m3 = mb.ap.rearrange("p (k c) -> p k c", k=NCH)
            mc = (xmix.ap[:, mixcol * NCH:(mixcol + 1) * NCH]
                  [:, :, None].broadcast_to([128, NCH, 128]))
            est = 1150 if eng == "v" else 2500
            E(eng, lambda e, _sx=sxT[par]: e.tensor_tensor(
                m3, _sx.ap.rearrange("p (k c) -> p k c", k=NCH), mc,
                ALU.mult),
              reads=[sxT[par], xmix], writes=[mb], est=est)
            E(eng, lambda e: e.tensor_tensor(m3, m3, xe3_[:, :, 2:TP + 2],
                                             ALU.add),
              reads=[mb, xnT_ext[par]], writes=[mb], est=est)
            return mb

        FB1 = [dict(xc=xc1, xT=xnT[i], tw=tw, twT=twT, gexp=gexp,
                    gsum=gsum, grec=grec, gate=gate) for i in range(2)]
        FB2 = dict(xc=xn3_tm, xT=xn3T, tw=tw2, twT=twT2, gexp=gexp2,
                   gsum=gsum2, grec=grec2, gate=gate2)


        def proj_stream(mslot, wbuf, dst, est_mm):
            for lo, hi in OC:
                w = hi - lo
                qs = psT_r[2:2 + (w + 255) // 256]
                fns = []
                for k in range(NCH):
                    fns.append(lambda e,
                               _l=mixT[mslot].ap[:, k * 128:(k + 1) * 128],
                               _r=wbuf.ap[:, k * C + lo:k * C + hi],
                               _d=psT.ap[:, 512:512 + w],
                               _st=(k == 0), _sp=(k == NCH - 1):
                               e.matmul(_d, _l, _r, start=_st, stop=_sp))
                E("t", fns, reads=[mixT[mslot], wbuf],
                  writes=qs, est=est_mm)
                E("s", lambda e, _lo=lo, _hi=hi, _w=w, _d=dst: e.copy(
                    _d.ap[:, _lo:_hi], psT.ap[:, 512:512 + _w]),
                  reads=qs, writes=[dst], est=550)

        def stage_a(p):
            real = p >= HP
            par = p % 2
            xb, x2b = xin[par], x2[par]
            kk8b, vsb, rsb, thTb = (kk8[par], v_sb[par], r_sb[par],
                                    thT[par])
            scur, sprev = states[par], states[1 - par]
            kpreb = kpre[par]
            wfmb = w_fm[par]
            sxTb = sxT[par]
            xe, xep = xe3v[par], xe3v[1 - par]
            lnA, lnB = lnst[0][par], lnst[1][par]
            E("s", lambda e: e.activation(xc1.ap, xb.ap, AF.Copy,
                                          bias=0.0, scale=one_c,
                                          accum_out=lnA["sum"].ap),
              reads=[xb, cc], writes=[xc1, lnA["sum"]], est=1150)
            E("s", lambda e: e.activation(xn_tm.ap, xb.ap, AF.Square,
                                          bias=zero_c, scale=one_c,
                                          accum_out=lnA["ssq"].ap),
              reads=[xb, cc], writes=[xn_tm, lnA["ssq"]], est=1300)
            stats_from_accum(lnA)
            ffn_fn(xb, x2b, lnA, FB1[par], psB, psB_r, psT, psT_r,
                   compact_aux=True)
            stats_from_bn(x2b, lnB, par)
            if p + 2 < NP:
                E("sx%d" % par, lambda e, _r=(p + 2) * TP, _b=xb:
                  e.dma_start(out=_b.ap, in_=x_in[_r:_r + TP, :]),
                  writes=[xb])
            E("v", lambda e: e.tensor_tensor(lnB["msq"].ap,
                                             lnB["mean"].ap,
                                             lnB["rstd"].ap, ALU.mult),
              reads=[lnB["mean"], lnB["rstd"]], writes=[lnB["msq"]],
              est=100)
            E("s", lambda e: e.activation(xn_tm.ap, x2b.ap, AF.Identity,
                                          bias=lnB["msq"].ap,
                                          scale=lnB["rstd"].ap),
              reads=[x2b, lnB["msq"], lnB["rstd"]], writes=[xn_tm],
              est=1150)
            if p > 0:
                E("v", lambda e: e.tensor_copy(xe[:, :, 1:2],
                                               xep[:, :, TP + 1:TP + 2]),
                  reads=[xnT_ext[1 - par]], writes=[xnT_ext[par]])
            nh2 = NCH // 2
            for r in range(2):
                fns = []
                for k in range(r * nh2, (r + 1) * nh2):
                    fns.append(lambda e,
                               _s=xn_tm.ap[:, k * 128:(k + 1) * 128],
                               _d=psT.ap[:, (k % nh2) * 128:
                                         (k % nh2 + 1) * 128]:
                               e.matmul(_d, _s, idb.ap, start=True,
                                        stop=True))
                E("t", fns, reads=[xn_tm, idb], writes=psT_r[0:2],
                  est=nh2 * 95)
                E("s", lambda e, _r=r, _n=nh2: e.copy(
                    xe[:, _r * _n:(_r + 1) * _n, 2:TP + 2],
                    psT.ap[:, 0:_n * 128].rearrange("p (k c) -> p k c",
                                                    k=_n)),
                  reads=psT_r[0:2], writes=[xnT_ext[par]], est=500)
            E("v", lambda e: e.tensor_tensor(
                sxTb.ap.rearrange("p (k c) -> p k c", k=NCH),
                xe[:, :, 2:TP + 2], xe[:, :, 1:TP + 1], ALU.subtract),
              reads=[xnT_ext[par]], writes=[sxTb], est=1250)
            fns = []
            for k in range(NCH):
                fns.append(lambda e, _l=xe[:, k, 2:TP + 2],
                           _r=w1cat.ap[:, k * NW1:(k + 1) * NW1],
                           _st=(k == 0):
                           e.matmul(psT.ap[:, 0:NW1], _l, _r, start=_st,
                                    stop=False))
                fns.append(lambda e, _l=sxTb.ap[:, k * 128:(k + 1) * 128],
                           _r=w1x.ap[:, k * NW1:(k + 1) * NW1],
                           _sp=(k == NCH - 1):
                           e.matmul(psT.ap[:, 0:NW1], _l, _r, start=False,
                                    stop=_sp))
            E("t", fns, reads=[xnT_ext[par], sxTb, w1cat, w1x],
              writes=[psT_r[0]], est=1300)
            E("s", lambda e: e.activation(th.ap, psT.ap[:, 0:NW1], AF.Tanh,
                                          bias=zero_c, scale=one_c),
              reads=[psT_r[0], cc], writes=[th], est=350)
            fns = []
            for m in range(4):
                fns.append(lambda e, _s=th.ap[:, m * DLOW:(m + 1) * DLOW],
                           _d=psT.ap[0:DLOW, m * TP:(m + 1) * TP]:
                           e.matmul(_d, _s, idb.ap, start=True, stop=True))
            E("t", fns, reads=[th, idb], writes=psT_r[0:2], est=450)
            E("s", lambda e, _t=thTb: e.copy(
                _t.ap[0:DLOW, :], psT.ap[0:DLOW, 0:4 * TP]),
              reads=psT_r[0:2], writes=[thTb], est=400)
            E("t", lambda e, _t=thTb: e.matmul(
                psT.ap[0:H, 0:TP], w2aug.ap, _t.ap[:, TP:2 * TP],
                start=True, stop=True),
              reads=[w2aug, thTb], writes=[psT_r[0]], est=200)
            E("s", lambda e, _w=wfmb: e.activation(_w.ap[0:H, :],
                                                   psT.ap[0:H, 0:TP],
                                                   AF.Tanh, bias=zero16,
                                                   scale=half16),
              reads=[psT_r[0], cc], writes=[wfmb], est=250)
            E("v", lambda e, _w=wfmb: e.tensor_scalar(_w.ap[0:H, :],
                                                      _w.ap[0:H, :],
                                                      1.0, 0.5 * E_HALF,
                                                      ALU.add, ALU.mult),
              reads=[wfmb], writes=[wfmb], est=200)
            E("t", lambda e, _t=thTb: e.matmul(
                psT.ap[:, 256:256 + H], _t.ap[:, 2 * TP:3 * TP], a2aug.ap,
                start=True, stop=True),
              reads=[thTb, a2aug], writes=[psT_r[1]], est=200)
            E("s", lambda e: e.activation(a_t.ap, psT.ap[:, 256:256 + H],
                                          AF.Tanh, bias=zero_c,
                                          scale=half_c),
              reads=[psT_r[1], cc], writes=[a_t], est=250)
            E("v", lambda e: e.tensor_scalar(a8.ap, a_t.ap, 1.0 / 16,
                                             1.0 / 16, ALU.mult, ALU.add),
              reads=[a_t], writes=[a8], est=170)
            mix_stream(0, 1, par)
            proj_stream(0, wk, kpreb, NCH * 512 * 0.42 + 150)
            mix_stream(1, 2, par)
            proj_stream(1, wv, vsb, NCH * 512 * 0.42 + 150)
            if real:
                mix_stream(2, 0, par)
                proj_stream(2, wr, rsb, NCH * 512 * 0.42 + 150)
            E("s", lambda e, _kk=kk8b: e.activation(_kk.ap, kpreb.ap,
                                                    AF.Square,
                                                    bias=zero_c,
                                                    scale=one_c),
              reads=[kpreb, cc], writes=[kk8b], est=1300)
            E("v", lambda e, _kk=kk8b: e.tensor_reduce(ssh.ap,
                                                       re3(_kk.ap, H), AX.X,
                                                       ALU.add),
              reads=[kk8b], writes=[ssh], est=1300)
            E("v", lambda e: e.tensor_scalar(sshe.ap, ssh.ap, 1e-24,
                                             1.0 / 64, ALU.max, ALU.mult),
              reads=[ssh], writes=[sshe], est=170)
            E("g", lambda e: e.tensor_tensor(rn.ap, sshe.ap, pwH, ALU.pow),
              reads=[sshe, powc], writes=[rn], est=3600)
            E("v", lambda e, _kk=kk8b, _kp=kpreb: e.tensor_tensor(
                re3(_kk.ap, H), re3(_kp.ap, H), bc3(rn.ap, H, 64),
                ALU.mult),
              reads=[kpreb, rn], writes=[kk8b], est=1250)
            E("v", lambda e: e.tensor_tensor(rna.ap, rn.ap, a8.ap,
                                             ALU.mult),
              reads=[rn, a8], writes=[rna], est=170)
            E("g", lambda e, _kp=kpreb: e.tensor_tensor(
                re3(kvp.ap, H), re3(_kp.ap, H), bc3(rna.ap, H, 64),
                ALU.mult),
              reads=[kpreb, rna], writes=[kvp], est=2500)
            E("v", lambda e, _v=vsb, _kv=kpreb: e.tensor_tensor(
                _kv.ap, kvp.ap, _v.ap, ALU.mult),
              reads=[kvp, vsb], writes=[kpreb], est=700)
            E("t", transpose_blocks(kpreb.ap, NCH, psC),
              reads=[kpreb, idb], writes=psC_r, est=NCH * 110)
            for lo, hi in OC:
                E("s", lambda e, _lo=lo, _hi=hi, _d=kpreb: e.copy(
                    _d.ap[:, _lo:_hi], psC.ap[:, _lo:_hi]),
                  reads=bq(psC_r, lo, hi), writes=[kpreb], est=550)
            for lo, hi in OC:
                fns = []
                for k in range(lo // 128, hi // 128):
                    fns.append(lambda e,
                               _l=eall.ap[:, k * 128:(k + 1) * 128],
                               _d=psC.ap[:, k * 128:(k + 1) * 128],
                               _w=wfmb:
                               e.matmul(_d, _l, _w.ap, start=True,
                                        stop=True))
                E("t", fns, reads=[eall, wfmb],
                  writes=bq(psC_r, lo, hi), est=(hi - lo) // 128 * 240)
            for k in range(NCH):
                lo = k * 128
                init = 0.0 if p == 0 else sprev.ap[:, lo + TP - 1:lo + TP]
                E("v", lambda e, _lo=lo, _i=init, _sc=scur, _kf=kpreb:
                  e.tensor_tensor_scan(
                    _sc.ap[:, _lo:_lo + TP], psC.ap[:, _lo:_lo + TP],
                    _kf.ap[:, _lo:_lo + TP], _i, ALU.mult, ALU.add),
                  reads=[psC_r[k * 128 // 256], kpreb] +
                        ([] if p == 0 else [sprev]),
                  writes=[scur], est=475)

        def stage_b(p):
            par = p % 2
            x2b = x2[par]
            kk8b, vsb, rsb, thTb = (kk8[par], v_sb[par], r_sb[par],
                                    thT[par])
            scur = states[par]
            lnC = lnst[2][par]
            fns = []
            for ci, (lo, hi) in enumerate(OC):
                fns.append(lambda e, _t=thTb, _r=g2z.ap[:, lo:hi],
                           _d=psW.ap[:, lo:hi]:
                           e.matmul(_d, _t.ap[:, 3 * TP:4 * TP], _r,
                                    start=True, stop=True))
            E("t", fns, reads=[thTb, g2z], writes=psW_r, est=500)
            E("s", lambda e: e.activation(tg_sb.ap, psW.ap[:, 0:C],
                                          AF.Tanh, bias=zero_c,
                                          scale=half_c),
              reads=psW_r + [cc], writes=[tg_sb], est=1100)
            fns = []
            for ci, (lo, hi) in enumerate(OC):
                fns.append(lambda e, _t=thTb, _r=v2aug.ap[:, lo:hi],
                           _d=psW.ap[:, lo:hi]:
                           e.matmul(_d, _t.ap[:, 0:TP], _r,
                                    start=True, stop=True))
            E("t", fns, reads=[thTb, v2aug], writes=psW_r, est=500)
            E("s", lambda e: e.activation(vgt.ap, psW.ap[:, 0:C], AF.Tanh,
                                          bias=zero_c, scale=half_c),
              reads=psW_r + [cc], writes=[vgt], est=1100)
            E("s", lambda e, _s=scur: e.copy(stb.ap, _s.ap),
              reads=[scur], writes=[stb], est=950)
            E("t", transpose_blocks(stb.ap, NCH, psC),
              reads=[stb, idb], writes=psC_r, est=NCH * 110)
            E("v", lambda e, _r=rsb: e.tensor_tensor(
                sc1.ap, psC.ap[:, 0:C], _r.ap, ALU.mult),
              reads=psC_r + [rsb], writes=[sc1], est=1350)
            E("g", lambda e, _r=rsb, _kk=kk8b: e.tensor_tensor(
                rkk.ap, _r.ap, _kk.ap, ALU.mult),
              reads=[rsb, kk8b], writes=[rkk], est=2300)
            E("g", lambda e: e.tensor_tensor(rkkr.ap, rkk.ap, rkt.ap,
                                             ALU.mult),
              reads=[rkk, rkt], writes=[rkkr], est=2300)
            E("v", lambda e: e.tensor_reduce(dot.ap, re3(rkkr.ap, H), AX.X,
                                             ALU.add),
              reads=[rkkr], writes=[dot], est=1300)
            E("g", lambda e, _v=vsb: e.tensor_tensor(
                re3(bon.ap, H), re3(_v.ap, H), bc3(dot.ap, H, 64), ALU.mult),
              reads=[vsb, dot], writes=[bon], est=2500)
            E("v", lambda e: e.tensor_tensor(xwkv.ap, sc1.ap, bon.ap,
                                             ALU.add),
              reads=[sc1, bon], writes=[xwkv], est=1250)
            x3v = re3(xwkv.ap, H)
            E("s", lambda e, _kk=kk8b: e.activation(_kk.ap, xwkv.ap,
                                                    AF.Square, bias=zero_c,
                                                    scale=one_c),
              reads=[xwkv, cc], writes=[kk8b], est=1300)
            E("v", lambda e: e.tensor_reduce(gs1.ap, x3v, AX.X, ALU.add),
              reads=[xwkv], writes=[gs1], est=1300)
            E("v", lambda e, _kk=kk8b: e.tensor_reduce(
                gaM.ap, re3(_kk.ap, H), AX.X, ALU.add),
              reads=[kk8b], writes=[gaM], est=1300)
            E("v", lambda e: e.tensor_scalar(gm.ap, gs1.ap, 1.0 / 64, None,
                                             ALU.mult),
              reads=[gs1], writes=[gm], est=170)
            E("v", lambda e: e.tensor_tensor(gd2.ap, gm.ap, gm.ap,
                                             ALU.mult),
              reads=[gm], writes=[gd2], est=170)
            E("v", lambda e: e.scalar_tensor_tensor(gbe.ap, gaM.ap,
                                                    1.0 / 64, gd2.ap,
                                                    ALU.mult, ALU.subtract),
              reads=[gaM, gd2], writes=[gbe], est=170)
            E("v", lambda e: e.tensor_scalar(gbe.ap, gbe.ap, EPS, None,
                                             ALU.add),
              reads=[gbe], writes=[gbe], est=170)
            E("g", lambda e: e.tensor_tensor(grs.ap, gbe.ap, pwH, ALU.pow),
              reads=[gbe, powc], writes=[grs], est=3600)
            E("v", lambda e: e.tensor_tensor(re3(u1.ap, H), x3v,
                                             bc3(gm.ap, H, 64),
                                             ALU.subtract),
              reads=[xwkv, gm], writes=[u1], est=1250)
            E("g", lambda e: e.tensor_tensor(re3(u2.ap, H), re3(u1.ap, H),
                                             bc3(grs.ap, H, 64), ALU.mult),
              reads=[u1, grs], writes=[u2], est=2500)
            E("v", lambda e: e.tensor_tensor(u3.ap, u2.ap, lblw.ap, ALU.add),
              reads=[u2, lblw], writes=[u3], est=700)
            E("v", lambda e: e.scalar_tensor_tensor(xg2.ap, tg_sb.ap, 1.0,
                                                    u3.ap, ALU.add,
                                                    ALU.mult),
              reads=[tg_sb, u3], writes=[xg2], est=900)
            E("t", transpose_blocks(xg2.ap, NCH, psC),
              reads=[xg2, idb], writes=psC_r, est=NCH * 95)
            E("s", lambda e: e.copy(xg2T.ap, psC.ap[:, 0:C]),
              reads=psC_r, writes=[xg2T], est=620)
            for lo, hi in OC:
                fns = []
                for k in range(NCH):
                    fns.append(lambda e,
                               _l=xg2T.ap[:, k * 128:(k + 1) * 128],
                               _r=wo.ap[:, k * C + lo:k * C + hi],
                               _d=psW.ap[:, lo:hi],
                               _st=(k == 0), _sp=(k == NCH - 1):
                               e.matmul(_d, _l, _r, start=_st, stop=_sp))
                E("t", fns, reads=[xg2T, wo], writes=bq(psW_r, lo, hi),
                  est=NCH * (hi - lo) * 0.42 + 150)
            E("v", lambda e, _x2=x2b: e.scalar_tensor_tensor(
                x3.ap, psW.ap[:, 0:C], 1.0, _x2.ap, ALU.mult, ALU.add),
              reads=psW_r + [x2b], writes=[x3], est=1300)
            stats_from_bn(x3, lnC, par)
            ffn_fn(x3, xout, lnC, FB2, psW, psW_r, psC, psC_r)
            E("sst", lambda e, _r=(p - HP) * TP, _x=xout:
              e.dma_start(out=y_out[_r:_r + TP, :], in_=_x.ap),
              reads=[xout], est=2500)

        for p in range(NP):
            stage_a(p)
            if p >= HP:
                stage_b(p)

        P.schedule()
        with nc.Block() as block:
            @block.vector
            def _(v):
                for op in P.seq_ops["v"]:
                    op(v)

            @block.scalar
            def _(s):
                for op in P.seq_ops["s"]:
                    op(s)

            @block.tensor
            def _(t):
                for op in P.seq_ops["t"]:
                    op(t)

            @block.gpsimd
            def _(g):
                for op in P.seq_ops["g"]:
                    op(g)

            @block.sync
            def _(sy):
                for op in P.seq_ops["sy"]:
                    op(sy)

    return nc


def prep_shared(inp, C, pre="L0_attn_", fpre="L0_ffn_"):
    NCH = C // 128
    H = C // 64
    a = {k: np.asarray(v) for k, v in inp.items()}
    f = np.float64 if any(v.dtype == np.float64 for v in a.values()) \
        else np.float32
    w0 = a[pre + "w0"].reshape(1, H)
    a0 = a[pre + "a0"].reshape(1, H)
    v0 = a[pre + "v0"].reshape(1, C)
    colsplit = lambda v: np.ascontiguousarray(v.reshape(NCH, 128).T)
    tile = lambda v: np.ascontiguousarray(np.tile(v.reshape(1, C), (128, 1)))
    shared = {
        "wtT": np.ascontiguousarray(a[fpre + "weight"].T),
        "wr": np.ascontiguousarray(a[pre + "Wr"]),
        "wk": np.ascontiguousarray(a[pre + "Wk"]),
        "wv": np.ascontiguousarray(a[pre + "Wv"]),
        "wo": np.ascontiguousarray(a[pre + "Wo"]),
        "ug": np.ascontiguousarray(np.concatenate(
            [a[fpre + "lora_U"].reshape(ERK, C).T, a[fpre + "gate_w"].T],
            axis=1)),
        "vcat": np.ascontiguousarray(
            a[fpre + "lora_V"].transpose(0, 2, 1).reshape(ERK, C)),
        "w1cat": np.ascontiguousarray(np.concatenate(
            [a[pre + "v1"], a[pre + "w1"], a[pre + "a1"], a[pre + "g1"]],
            axis=1)),
        "w2aug": np.ascontiguousarray(np.concatenate([a[pre + "w2"], w0], 0)),
        "a2aug": np.ascontiguousarray(np.concatenate([a[pre + "a2"], a0], 0)),
        "v2aug": np.ascontiguousarray(np.concatenate([a[pre + "v2"], v0], 0)),
        "g2z": np.ascontiguousarray(np.concatenate(
            [a[pre + "g2"], np.zeros((1, C), f)], 0)),
        "xmix": np.ascontiguousarray(np.concatenate(
            [colsplit(a[pre + "x_" + s]) for s in "rkvwag"], axis=1)),
        "kkt": tile(a[pre + "k_k"]),
        "lnwc": colsplit(a[pre + "ln_w"]),
        "lnw_tile": tile(a[pre + "ln_w"]),
        "lnb_tile": tile(a[pre + "ln_b"]),
        "rk_tile": tile(a[pre + "r_k"]),
        "identb": np.eye(128, dtype=ml_dtypes.bfloat16),
        "identf": np.eye(128, dtype=f),
        "ones512": np.ones((1, 4 * TP), dtype=ml_dtypes.bfloat16),
    }
    eall = np.zeros((128, C), f)
    for k in range(NCH):
        for pp in range(128):
            eall[2 * k + pp // 64, k * 128 + pp] = 1.0
    shared["eall"] = eall
    cconst = np.zeros((128, 4), f)
    cconst[:, CC_ONE] = 1.0
    cconst[:, CC_ZERO] = 0.0
    cconst[:, CC_HALF] = 0.5
    shared["cconst"] = cconst
    return shared


def make_in_maps(x, shared, C, HP, RP, n_seq_shards=4):
    B, T, _ = x.shape
    NP = HP + RP
    per = RP * TP
    halo = HP * TP
    in_maps = []
    for core in range(B * n_seq_shards):
        b, q = core // n_seq_shards, core % n_seq_shards
        start = q * per - halo
        rows = np.zeros((NP * TP, C), x.dtype)
        lo = max(start, 0)
        rows[lo - start:, :] = x[b, lo:q * per + per, :]
        m = dict(shared)
        m["x"] = rows
        in_maps.append(m)
    return in_maps


_NC_CACHE = {}


def kernel(**inputs):
    C, HP, RP = 1024, 1, 16
    x = np.asarray(inputs["L0_attn_x"])
    B, T, _ = x.shape
    per = RP * TP
    shared = prep_shared(inputs, C)
    key = (C, HP, RP)
    if key not in _NC_CACHE:
        _NC_CACHE[key] = build_nc(C, HP, RP)
    nc = _NC_CACHE[key]
    in_maps = make_in_maps(x, shared, C, HP, RP)
    res = run_bass_kernel_spmd(nc, in_maps, core_ids=list(range(8)))
    odt = np.asarray(res.results[0]["y"]).dtype
    out = np.empty((B, T, C), np.float32 if odt == np.float32 else odt)
    for core in range(8):
        b, q = core // 4, core % 4
        out[b, q * per:(q + 1) * per, :] = res.results[core]["y"]
    return out


# version 79
# speedup vs baseline: 1.0170x; 1.0131x over previous
import numpy as np
import ml_dtypes
from contextlib import ExitStack

import concourse.bass as bass
import concourse.mybir as mybir
from concourse.bass_utils import run_bass_kernel_spmd

F32 = mybir.dt.float32
BF16 = mybir.dt.bfloat16
AF = mybir.ActivationFunctionType
ALU = mybir.AluOpType
AX = mybir.AxisListType

TP = 128
DLOW = 64
NE, NR = 8, 16
ERK = NE * NR
K2 = DLOW + 1
EPS = 1e-5
E_HALF = float(np.exp(-0.5))
CC_ONE, CC_ZERO, CC_HALF = 0, 1, 2
DEBUG_LOG = None


class _Stream:
    def __init__(self, name, sem, dma):
        self.name, self.sem, self.dma = name, sem, dma
        self.count = 0


class Buf:
    __slots__ = ("ap", "writer", "readers")

    def __init__(self, ap):
        self.ap, self.writer, self.readers = ap, None, []


class _Op:
    __slots__ = ("idx", "st", "fns", "deps", "est", "prio", "start", "fin",
                 "tok")

    def __init__(self, idx, st, fns, deps, est):
        self.idx, self.st, self.fns, self.deps, self.est = (
            idx, st, fns, deps, est)
        self.prio = 0.0
        self.start = self.fin = 0.0
        self.tok = None


SCHED_SEED = 1234


class Prog:

    SEQ_OF = {"v": "v", "s": "s", "t": "t", "g": "g", "gd": "g", "gd2": "g",
              "gd3": "g",
              "syc": "sy", "sf0": "sy", "sf1": "sy", "sx0": "sy", "sx1": "sy",
              "sst": "sy"}
    XLAT = 340.0
    OPFIX = 90.0

    def __init__(self, nc, stack):
        self.nc = nc
        self.streams = {}
        for nm, dma in [("v", False), ("s", False), ("t", False), ("g", False),
                        ("gd", True), ("gd2", True), ("gd3", True),
                        ("syc", True),
                        ("sf0", True), ("sf1", True), ("sx0", True),
                        ("sx1", True), ("sst", True)]:
            sem = stack.enter_context(nc.semaphore(name="sem_" + nm))
            self.streams[nm] = _Stream(nm, sem, dma)
        self.ops = []
        self.final_stream = None

    def emit(self, st_name, fn, reads=(), writes=(), est=400.0):
        fns = tuple(fn) if isinstance(fn, (list, tuple)) else (fn,)
        deps = set()
        for b in reads:
            if b.writer is not None:
                deps.add(b.writer)
        for b in writes:
            if b.writer is not None:
                deps.add(b.writer)
            deps.update(b.readers)
        op = _Op(len(self.ops), st_name, fns, deps, float(est))
        self.ops.append(op)
        for b in reads:
            b.readers.append(op.idx)
        for b in writes:
            b.writer = op.idx
            b.readers = []
        return op

    def schedule(self):
        ops = self.ops
        n = len(ops)
        barrier_members = {}
        for op in ops:
            if op.st in ("gd", "gd2", "gd3", "syc"):
                barrier_members.setdefault(op.st, set()).add(op.idx)
        for op in ops:
            ext = None
            for d in op.deps:
                dst = ops[d].st
                if dst in barrier_members and op.st != dst:
                    ext = (ext or set()) | barrier_members[dst]
            if ext:
                op.deps = (op.deps | ext) - {op.idx}
        succs = [[] for _ in range(n)]
        for op in ops:
            for d in op.deps:
                succs[d].append(op.idx)
        if SCHED_SEED:
            rng = np.random.default_rng(SCHED_SEED)
            jit = rng.uniform(0.92, 1.08, n)
        else:
            jit = None
        for i in range(n - 1, -1, -1):
            p = ops[i].est
            if jit is not None:
                p *= jit[i]
            m = 0.0
            for s in succs[i]:
                if ops[s].prio > m:
                    m = ops[s].prio
            ops[i].prio = p + m
        for op in ops:
            if op.st == "gd3":
                op.prio = 3e12
            elif op.st == "gd":
                op.prio = 2e12
            elif op.st == "gd2":
                op.prio = 1e12
        SLACK = 400.0
        seqs = sorted(set(self.SEQ_OF.values()))
        seq_time = {q: 0.0 for q in seqs}
        ndeps = [len(op.deps) for op in ops]
        ready = {q: set() for q in seqs}
        est_ready = [0.0] * n
        for op in ops:
            if not op.deps:
                ready[self.SEQ_OF[op.st]].add(op.idx)
        seq_lists = {q: [] for q in seqs}
        scheduled = 0
        while scheduled < n:
            best = None
            for q in seqs:
                rq = ready[q]
                if not rq:
                    continue
                emin = min(est_ready[i] for i in rq)
                t0 = max(seq_time[q], emin)
                cand, cprio = None, -1.0
                for i in rq:
                    if est_ready[i] <= t0 + SLACK and ops[i].prio > cprio:
                        cand, cprio = i, ops[i].prio
                start = max(seq_time[q], est_ready[cand])
                if (best is None or start < best[0] - 1e-9 or
                        (abs(start - best[0]) < 1e-9 and cprio > best[3])):
                    best = (start, q, cand, cprio)
            start, q, i, _ = best
            op = ops[i]
            ready[q].discard(i)
            op.start = start
            op.fin = start + op.est + self.OPFIX
            seq_time[q] = op.fin
            seq_lists[q].append(i)
            scheduled += 1
            for s in succs[i]:
                ndeps[s] -= 1
                if ndeps[s] == 0:
                    est_ready[s] = max(
                        [0.0] + [ops[d].fin + self.XLAT
                                 for d in ops[s].deps])
                    ready[self.SEQ_OF[ops[s].st]].add(s)
        assert scheduled == n, (scheduled, n)
        import os
        if os.environ.get("SCHED_REPORT"):
            busy = {}
            for op in ops:
                busy[self.SEQ_OF[op.st]] = busy.get(self.SEQ_OF[op.st], 0) + op.est
            print("SCHED makespan %.1f us; busy: %s" % (
                max(o.fin for o in ops) / 1e3,
                {k: round(v / 1e3, 1) for k, v in sorted(busy.items())}))
            import traceback as _tb
            last = max(ops, key=lambda o: o.fin)
            chain = []
            cur = last
            while cur is not None:
                chain.append(cur)
                best = None
                for d in cur.deps:
                    if abs(ops[d].fin + self.XLAT - cur.start) < 1e-6:
                        best = ops[d]
                        break
                if best is None:
                    q = self.SEQ_OF[cur.st]
                    cand = [o for o in ops
                            if self.SEQ_OF[o.st] == q and o.fin <= cur.start
                            and o is not cur]
                    if cand:
                        b = max(cand, key=lambda o: o.fin)
                        if abs(b.fin - cur.start) < 1e-6:
                            best = b
                cur = best
            chain.reverse()
            import collections
            agg = collections.Counter()
            for o in chain:
                agg[(o.st, round(o.est, -1))] += 1
            print("SCHED critical chain: %d ops" % len(chain))
            for (stn, e), cnt in agg.most_common(20):
                print("   %s est=%s x%d  (%.1f us)" % (stn, e, cnt,
                                                       e * cnt / 1e3))
        for q in seqs:
            for i in seq_lists[q]:
                op = ops[i]
                st = self.streams[op.st]
                st.count += 16 if st.dma else 1
                op.tok = (op.st, st.count)
        self.seq_ops = {q: [] for q in seqs}
        for q in seqs:
            waited = {}
            for i in seq_lists[q]:
                op = ops[i]
                need = {}
                for d in op.deps:
                    snm, val = ops[d].tok
                    if snm == op.st and self.streams[snm].dma:
                        continue
                    if snm in ("gd", "gd2", "gd3", "syc"):
                        val = self.streams[snm].count
                    if need.get(snm, 0) < val:
                        need[snm] = val
                waits = []
                for snm, val in sorted(need.items()):
                    if waited.get(snm, 0) < val:
                        waited[snm] = val
                        waits.append((self.streams[snm].sem, val))
                sem = self.streams[op.st].sem
                inc = 16 if self.streams[op.st].dma else 1

                def run(eng, _w=tuple(waits), _f=op.fns, _s=sem, _i=inc):
                    for s, v in _w:
                        eng.wait_ge(s, v)
                    last = None
                    for f in _f:
                        last = f(eng)
                    last.then_inc(_s, _i)

                self.seq_ops[q].append(run)
        fw = []
        for nm in ("sst", "sx0", "sx1", "syc", "sf0", "sf1"):
            st = self.streams[nm]
            if st.count:
                fw.append((st.sem, st.count))

        def run_final(eng, _w=tuple(fw)):
            for s, v in _w:
                eng.wait_ge(s, v)

        self.seq_ops["sy"].append(run_final)


def out_chunks(C):
    return [(i, min(i + 512, C)) for i in range(0, C, 512)]


def build_nc(C, HP, RP):
    NCH = C // 128
    H = C // 64
    NP = HP + RP
    ROWS = NP * TP
    NW1 = 4 * DLOW
    UGW = ERK + NE
    OC = out_chunks(C)

    nc = bass.Bass("TRN2", target_bir_lowering=False)
    dt = nc.dram_tensor

    x_in = dt("x", [ROWS, C], F32, kind="ExternalInput").ap()
    wtT_in = dt("wtT", [C, C], F32, kind="ExternalInput").ap()
    wr_in = dt("wr", [C, C], F32, kind="ExternalInput").ap()
    wk_in = dt("wk", [C, C], F32, kind="ExternalInput").ap()
    wv_in = dt("wv", [C, C], F32, kind="ExternalInput").ap()
    wo_in = dt("wo", [C, C], F32, kind="ExternalInput").ap()
    ug_in = dt("ug", [C, UGW], F32, kind="ExternalInput").ap()
    vcat_in = dt("vcat", [ERK, C], F32, kind="ExternalInput").ap()
    w1cat_in = dt("w1cat", [C, NW1], F32, kind="ExternalInput").ap()
    w2aug_in = dt("w2aug", [K2, H], F32, kind="ExternalInput").ap()
    a2aug_in = dt("a2aug", [K2, H], F32, kind="ExternalInput").ap()
    v2aug_in = dt("v2aug", [K2, C], F32, kind="ExternalInput").ap()
    g2z_in = dt("g2z", [K2, C], F32, kind="ExternalInput").ap()
    xmix_in = dt("xmix", [128, 6 * NCH], F32, kind="ExternalInput").ap()
    kkt_in = dt("kkt", [128, C], F32, kind="ExternalInput").ap()
    lnwc_in = dt("lnwc", [128, NCH], F32, kind="ExternalInput").ap()
    lnw_tile_in = dt("lnw_tile", [128, C], F32, kind="ExternalInput").ap()
    lnb_tile_in = dt("lnb_tile", [128, C], F32, kind="ExternalInput").ap()
    rk_tile_in = dt("rk_tile", [128, C], F32, kind="ExternalInput").ap()
    identb_in = dt("identb", [128, 128], BF16, kind="ExternalInput").ap()
    identf_in = dt("identf", [128, 128], F32, kind="ExternalInput").ap()
    eall_in = dt("eall", [128, C], F32, kind="ExternalInput").ap()
    cconst_in = dt("cconst", [128, 4], F32, kind="ExternalInput").ap()
    ones512_in = dt("ones512", [1, 4 * TP], BF16, kind="ExternalInput").ap()
    y_out = dt("y", [RP * TP, C], F32, kind="ExternalOutput").ap()

    stack = ExitStack()

    _nn = [0]

    def sb(shape, dty):
        _nn[0] += 1
        return Buf(stack.enter_context(
            nc.sbuf_tensor("sb%d" % _nn[0], shape, dty))[:])

    def psum(shape):
        _nn[0] += 1
        return Buf(stack.enter_context(
            nc.psum_tensor("ps%d" % _nn[0], shape, F32))[:])

    with stack:
        wt = sb([128, NCH * C], BF16)
        wr = sb([128, NCH * C], BF16)
        wk = sb([128, NCH * C], BF16)
        wv = sb([128, NCH * C], BF16)
        wo = sb([128, NCH * C], BF16)
        ug = sb([128, NCH * UGW], BF16)
        vcat = sb([ERK, C], BF16)
        w1cat = sb([128, NCH * NW1], BF16)
        w1x = sb([128, NCH * NW1], BF16)
        w2aug = sb([K2, H], BF16)
        a2aug = sb([K2, H], BF16)
        v2aug = sb([K2, C], BF16)
        g2z = sb([K2, C], BF16)
        xmix = sb([128, 6 * NCH], BF16)
        lnwc = sb([128, NCH], F32)
        lnwh = sb([128, NCH], F32)
        lblw = sb([128, C], BF16)
        rkt = sb([128, C], F32)
        idb = sb([128, 128], BF16)
        eall = sb([128, C], BF16)
        cc = sb([128, 4], F32)
        powc = sb([128, 16], F32)

        xin = [sb([128, C], F32) for _ in range(2)]
        x2 = [sb([128, C], F32) for _ in range(2)]
        x3 = sb([128, C], F32)
        xn_tm = sb([128, C], BF16)
        xc1 = sb([128, C], BF16)
        xnT = [sb([128, C], BF16) for _ in range(2)]
        xnT_ext = [sb([128, NCH * (TP + 2)], BF16) for _ in range(2)]
        sxT = [sb([128, C], BF16) for _ in range(2)]
        mixT = [sb([128, C], BF16) for _ in range(3)]
        th = sb([128, NW1], BF16)
        thT = [sb([K2, 4 * TP], BF16) for _ in range(2)]
        xg2 = sb([128, C], BF16)
        xg2T = sb([128, C], BF16)
        xn3_tm = xg2
        xn3T = xg2T
        tw = sb([128, ERK], BF16)
        twT = sb([ERK, TP], BF16)
        tw2 = sb([128, ERK], BF16)
        twT2 = sb([ERK, TP], BF16)
        tg_sb = sb([128, C], BF16)

        kpre = [sb([128, C], BF16) for _ in range(2)]
        kk8 = [sb([128, C], F32) for _ in range(2)]
        v_sb = [sb([128, C], BF16) for _ in range(2)]
        u2 = sb([128, C], BF16)
        u3 = u2

        states = [sb([128, C], F32) for _ in range(2)]
        r_sb = [sb([128, C], F32) for _ in range(2)]
        rkk = sb([128, C], F32)
        xwkv = sb([128, C], F32)
        xout = sb([128, C], F32)
        stgf = [x2[0], x2[1]]
        lnw_t = rkk
        sc1 = x3
        vgt = xg2
        kkt = x3
        kvp = mixT[2]
        stb = xg2T
        rkkr = xwkv
        u1 = xwkv
        bon = rkk

        ln12 = [sb([128, 12], F32) for _ in range(2)]
        lnmv = [sb([128, 2], F32) for _ in range(2)]
        lnst = [[dict(sum=sb([128, 1], F32), ssq=sb([128, 1], F32),
                      mean=sb([128, 1], F32), rstd=sb([128, 1], F32),
                      veps=sb([128, 1], F32), msq=sb([128, 1], F32),
                      rstdh=sb([128, 1], F32))
                 for _ in range(2)] for _ in range(3)]
        gexp = sb([128, NE], F32)
        gsum = sb([128, 1], F32)
        grec = sb([128, 1], F32)
        gate = sb([128, NE], F32)
        gexp2 = sb([128, NE], F32)
        gsum2 = sb([128, 1], F32)
        grec2 = sb([128, 1], F32)
        gate2 = sb([128, NE], F32)
        w_fm = [sb([128, TP], BF16) for _ in range(2)]
        a_t = sb([128, H], F32)
        a8 = sb([128, H], F32)
        ssh = sb([128, H], F32)
        rna = sb([128, H], F32)
        sshe = sb([128, H], F32)
        rn = sb([128, H], F32)
        dot = sshe
        gs1 = sb([128, H], F32)
        gd2 = sb([128, H], F32)
        gm = ssh
        gaM = rna
        grs = rn
        gbe = gd2

        psT = psum([128, 1024])
        psB = psum([128, 1024])
        psW = psum([128, 1024])
        psC = psum([128, 1024])
        quarters = lambda ps: [Buf(ps.ap[:, i * 256:(i + 1) * 256])
                               for i in range(4)]
        psT_r = quarters(psT)
        psB_r = quarters(psB)
        psW_r = quarters(psW)
        psC_r = quarters(psC)

        def bq(ps_r, lo, hi):
            return ps_r[lo // 256:(hi + 255) // 256]

        P = Prog(nc, stack)
        E = P.emit

        one_c = cc.ap[:, CC_ONE:CC_ONE + 1]
        zero_c = cc.ap[:, CC_ZERO:CC_ZERO + 1]
        half_c = cc.ap[:, CC_HALF:CC_HALF + 1]
        zero16 = cc.ap[0:H, CC_ZERO:CC_ZERO + 1]
        half16 = cc.ap[0:H, CC_HALF:CC_HALF + 1]

        def bc3(small_ap, nh, wide):
            return small_ap.broadcast_to([128, nh, wide])

        def re3(ap_, nh):
            return ap_.rearrange("p (h d) -> p h d", h=nh)

        xe3v = [xnT_ext[i].ap.rearrange("p (k c) -> p k c", k=NCH)
                for i in range(2)]

        E("sx0", lambda e: e.dma_start(out=xin[0].ap, in_=x_in[0:TP, :]),
          writes=[xin[0]])
        E("sx1", lambda e: e.dma_start(out=xin[1].ap, in_=x_in[TP:2 * TP, :]),
          writes=[xin[1]])
        lnb_stg = xout
        for buf, src in [(lnb_stg, lnb_tile_in), (lnw_t, lnw_tile_in),
                         (rkt, rk_tile_in),
                         (cc, cconst_in), (kkt, kkt_in),
                         (lnwc, lnwc_in)]:
            E("syc", lambda e, b=buf, s=src: e.dma_start(out=b.ap, in_=s),
              writes=[buf])
        E("syc", lambda e: e.dma_start(out=thT[0].ap[DLOW:K2, :],
                                       in_=ones512_in), writes=[thT[0]])
        E("syc", lambda e: e.dma_start(out=thT[1].ap[DLOW:K2, :],
                                       in_=ones512_in), writes=[thT[1]])

        def cload(stream, group, buf, src, chunked=False):
            s = src.rearrange("(k p) c -> p k c", p=128) if chunked else src
            o = (buf.ap.rearrange("p (k c) -> p k c", k=NCH) if chunked
                 else buf.ap)
            E(stream, lambda e, _o=o, _s=s: e.dma_start(out=_o, in_=_s),
              writes=[buf])
            group.append(buf)

        grpA, grpB = [], []
        cload("gd", grpA, ug, ug_in, True)
        cload("gd", grpA, vcat, vcat_in)
        cload("gd", grpA, idb, identb_in)
        cload("gd", grpA, xmix, xmix_in)
        cload("gd", grpA, eall, eall_in)
        cload("gd2", grpB, w1cat, w1cat_in, True)
        cload("gd2", grpB, w2aug, w2aug_in)
        cload("gd2", grpB, a2aug, a2aug_in)
        cload("gd2", grpB, v2aug, v2aug_in)
        cload("gd2", grpB, g2z, g2z_in)
        cload("gd2", grpB, wr, wr_in, True)
        cload("gd2", grpB, wv, wv_in, True)

        E("v", lambda e: e.memset(powc.ap, -0.5), writes=[powc])
        pw1 = powc.ap[:, 0:1]
        pwH = powc.ap[:, 0:H]
        E("v", lambda e: e.tensor_scalar(lnwh.ap, lnwc.ap, 0.5, None,
                                         ALU.mult),
          reads=[lnwc], writes=[lnwh])
        E("v", lambda e: e.reciprocal(lnw_t.ap, lnw_t.ap),
          reads=[lnw_t], writes=[lnw_t])
        E("v", lambda e: e.tensor_tensor(lblw.ap, lnb_stg.ap, lnw_t.ap,
                                         ALU.mult),
          reads=[lnb_stg, lnw_t], writes=[lblw])
        E("v", lambda e: e.tensor_scalar(rkt.ap, rkt.ap, 0.125, None,
                                         ALU.mult),
          reads=[rkt], writes=[rkt])
        for i in range(2):
            E("v", lambda e, _b=xnT_ext[i]: e.memset(_b.ap, 0.0),
              writes=[xnT_ext[i]])
            E("v", lambda e, _b=w_fm[i]: e.memset(_b.ap, 0.0),
              writes=[w_fm[i]])
        mix_bc = (xmix.ap[:, 2 * NCH:6 * NCH]
                  .rearrange("p (m k) -> p k m", m=4)[:, :, :, None]
                  .broadcast_to([128, NCH, 4, DLOW]))
        E("v", lambda e: e.tensor_tensor(
            w1x.ap.rearrange("p (k m d) -> p k m d", k=NCH, m=4),
            w1cat.ap.rearrange("p (k m d) -> p k m d", k=NCH, m=4),
            mix_bc, ALU.mult),
          reads=[w1cat, xmix], writes=[w1x])

        def folded_load(dst, src, fold):
            for k in range(NCH):
                s = stgf[k % 2]
                E("sf%d" % (k % 2), lambda e, _s=s, _k=k, _src=src:
                  e.dma_start(out=_s.ap,
                              in_=_src[_k * 128:(_k + 1) * 128, :]),
                  writes=[s])
                fold(k, s, dst)

        for k in range(NCH):
            E("gd3", lambda e, _k=k: e.dma_start(
                out=wt.ap[:, _k * C:(_k + 1) * C],
                in_=wtT_in[_k * 128:(_k + 1) * 128, :]),
              writes=[wt])
        folded_load(wk, wk_in, lambda k, s, d: E(
            "g", lambda e, _k=k, _s=s, _d=d: e.tensor_tensor(
                _d.ap[:, _k * C:(_k + 1) * C], _s.ap, kkt.ap, ALU.mult),
            reads=[s, kkt], writes=[d]))
        folded_load(wo, wo_in, lambda k, s, d: E(
            "s", lambda e, _k=k, _s=s, _d=d: e.activation(
                _d.ap[:, _k * C:(_k + 1) * C], _s.ap, AF.Copy,
                bias=0.0, scale=lnwh.ap[:, _k:_k + 1]),
            reads=[s, lnwh, cc], writes=[d]))

        def stats_from_accum(st):
            E("v", [lambda e: e.tensor_scalar(st["mean"].ap, st["sum"].ap,
                                              -1.0 / C, None, ALU.mult),
                    lambda e: e.tensor_scalar(st["veps"].ap, st["ssq"].ap,
                                              1.0 / C, EPS, ALU.mult,
                                              ALU.add)],
              reads=[st["sum"], st["ssq"]],
              writes=[st["mean"], st["veps"]], est=250)
            E("v", lambda e: e.tensor_tensor(st["msq"].ap, st["mean"].ap,
                                             st["mean"].ap, ALU.mult),
              reads=[st["mean"]], writes=[st["msq"]], est=100)
            E("v", lambda e: e.tensor_tensor(st["veps"].ap, st["veps"].ap,
                                             st["msq"].ap, ALU.subtract),
              reads=[st["veps"], st["msq"]], writes=[st["veps"]], est=100)
            E("g", lambda e: e.tensor_tensor(st["rstd"].ap, st["veps"].ap,
                                             pw1, ALU.pow),
              reads=[st["veps"], powc], writes=[st["rstd"]], est=1500)
            E("v", lambda e: e.tensor_scalar(st["rstdh"].ap, st["rstd"].ap,
                                             0.5, None, ALU.mult),
              reads=[st["rstd"]], writes=[st["rstdh"]], est=100)

        def stats_from_bn(src_buf, st, par):
            wdt = C // 2
            l12, lmv = ln12[par], lnmv[par]
            E("v", [lambda e: e.bn_stats(l12.ap[:, 0:6],
                                         src_buf.ap[:, 0:wdt]),
                    lambda e: e.bn_stats(l12.ap[:, 6:12],
                                         src_buf.ap[:, wdt:C])],
              reads=[src_buf], writes=[l12])
            E("v", lambda e: e.bn_aggr(lmv.ap, l12.ap),
              reads=[l12], writes=[lmv])
            E("v", lambda e: e.tensor_scalar(st["mean"].ap, lmv.ap[:, 0:1],
                                             -1.0, None, ALU.mult),
              reads=[lmv], writes=[st["mean"]])
            E("v", lambda e: e.tensor_scalar(st["veps"].ap, lmv.ap[:, 1:2],
                                             EPS, None, ALU.add),
              reads=[lmv], writes=[st["veps"]])
            E("g", lambda e: e.tensor_tensor(st["rstd"].ap, st["veps"].ap,
                                             pw1, ALU.pow),
              reads=[st["veps"], powc], writes=[st["rstd"]], est=1500)
            E("v", lambda e: e.tensor_scalar(st["rstdh"].ap, st["rstd"].ap,
                                             0.5, None, ALU.mult),
              reads=[st["rstd"]], writes=[st["rstdh"]], est=100)

        def transpose_blocks(src_ap, n_blk, ps, col0=0):
            fns = []
            for k in range(n_blk):
                fns.append(lambda e,
                           _s=src_ap[:, k * 128:(k + 1) * 128],
                           _d=ps.ap[:, col0 + k * 128:col0 + (k + 1) * 128]:
                           e.matmul(_d, _s, idb.ap, start=True, stop=True))
            return fns

        def big_mm_group(lhs_ap_of, w_buf, ps_buf, close=True):
            fns = []
            for k in range(NCH):
                lhs = lhs_ap_of(k)
                for ci, (lo, hi) in enumerate(OC):
                    fns.append(lambda e, _l=lhs,
                               _r=w_buf.ap[:, k * C + lo:k * C + hi],
                               _d=ps_buf.ap[:, lo:hi],
                               _st=(k == 0),
                               _sp=(close and k == NCH - 1):
                               e.matmul(_d, _l, _r, start=_st, stop=_sp))
            return fns

        def ffn_fn(x_buf, out_buf, st, fb, bps, bps_r, aps, aps_r,
                   compact_aux=False):
            E("v", lambda e: e.tensor_scalar(fb["xc"].ap, x_buf.ap,
                                             st["mean"].ap, None,
                                             ALU.add),
              reads=[x_buf, st["mean"]], writes=[fb["xc"]])
            if compact_aux:
                nh = NCH // 2
                for r in range(2):
                    fns = []
                    for k in range(r * nh, (r + 1) * nh):
                        fns.append(lambda e,
                                   _s=fb["xc"].ap[:, k * 128:(k + 1) * 128],
                                   _d=aps.ap[:, (k % nh) * 128:
                                             (k % nh + 1) * 128]:
                                   e.matmul(_d, _s, idb.ap, start=True,
                                            stop=True))
                    E("t", fns, reads=[fb["xc"], idb], writes=aps_r[0:2],
                      est=nh * 95)
                    E("s", lambda e, _r=r, _w=nh * 128: e.copy(
                        fb["xT"].ap[:, _r * _w:(_r + 1) * _w],
                        aps.ap[:, 0:_w]),
                      reads=aps_r[0:2], writes=[fb["xT"]], est=500)
            else:
                E("t", transpose_blocks(fb["xc"].ap, NCH, aps),
                  reads=[fb["xc"], idb], writes=aps_r, est=NCH * 95)
                E("s", lambda e: e.copy(fb["xT"].ap, aps.ap[:, 0:C]),
                  reads=aps_r, writes=[fb["xT"]], est=950)
            twc = 384 if compact_aux else 512
            twq = aps_r[1] if compact_aux else aps_r[2]
            fns = []
            for k in range(NCH):
                fns.append(lambda e,
                           _l=fb["xT"].ap[:, k * 128:(k + 1) * 128],
                           _r=ug.ap[:, k * UGW:(k + 1) * UGW],
                           _st=(k == 0), _sp=(k == NCH - 1):
                           e.matmul(aps.ap[:, 0:UGW], _l, _r,
                                    start=_st, stop=_sp))
            E("t", fns, reads=[fb["xT"], ug], writes=[aps_r[0]], est=800)
            for lo, hi in OC:
                fns = []
                for k in range(NCH):
                    fns.append(lambda e,
                               _l=fb["xT"].ap[:, k * 128:(k + 1) * 128],
                               _r=wt.ap[:, k * C + lo:k * C + hi],
                               _d=bps.ap[:, lo:hi], _st=(k == 0):
                               e.matmul(_d, _l, _r, start=_st, stop=False))
                E("t", fns, reads=[fb["xT"], wt],
                  writes=bq(bps_r, lo, hi), est=NCH * (hi - lo) * 0.42 + 150)
            E("s", lambda e: e.activation(fb["gexp"].ap, aps.ap[:, ERK:UGW],
                                          AF.Exp, bias=zero_c,
                                          scale=st["rstd"].ap,
                                          accum_out=fb["gsum"].ap),
              reads=[aps_r[0], st["rstd"], cc],
              writes=[fb["gexp"], fb["gsum"]])
            E("v", lambda e: e.reciprocal(fb["grec"].ap, fb["gsum"].ap),
              reads=[fb["gsum"]], writes=[fb["grec"]])
            E("v", lambda e: e.tensor_scalar(fb["gate"].ap, fb["gexp"].ap,
                                             fb["grec"].ap, 1.0 / NR,
                                             ALU.mult, ALU.mult),
              reads=[fb["gexp"], fb["grec"]], writes=[fb["gate"]])
            E("v", lambda e: e.tensor_tensor(
                re3(fb["tw"].ap, NE), re3(aps.ap[:, 0:ERK], NE),
                bc3(fb["gate"].ap, NE, NR), ALU.mult),
              reads=[aps_r[0], fb["gate"]], writes=[fb["tw"]])
            E("t", lambda e: e.matmul(aps.ap[:, twc:twc + 128], fb["tw"].ap,
                                      idb.ap, start=True, stop=True),
              reads=[fb["tw"], idb], writes=[twq], est=200)
            E("s", lambda e: e.copy(fb["twT"].ap, aps.ap[:, twc:twc + 128]),
              reads=[twq], writes=[fb["twT"]], est=300)
            for lo, hi in OC:
                E("t", lambda e, _r=vcat.ap[:, lo:hi], _d=bps.ap[:, lo:hi]:
                  e.matmul(_d, fb["twT"].ap, _r, start=False, stop=True),
                  reads=[fb["twT"], vcat], writes=bq(bps_r, lo, hi),
                  est=900)
            E("v", lambda e: e.scalar_tensor_tensor(
                out_buf.ap, bps.ap[:, 0:C], st["rstdh"].ap, x_buf.ap,
                ALU.mult, ALU.add),
              reads=bps_r + [x_buf, st["rstdh"]], writes=[out_buf], est=1300)

        def mix_stream(mslot, mixcol, par, eng="v"):
            mb = mixT[mslot]
            xe3_ = xe3v[par]
            m3 = mb.ap.rearrange("p (k c) -> p k c", k=NCH)
            mc = (xmix.ap[:, mixcol * NCH:(mixcol + 1) * NCH]
                  [:, :, None].broadcast_to([128, NCH, 128]))
            est = 1150 if eng == "v" else 2500
            E(eng, lambda e, _sx=sxT[par]: e.tensor_tensor(
                m3, _sx.ap.rearrange("p (k c) -> p k c", k=NCH), mc,
                ALU.mult),
              reads=[sxT[par], xmix], writes=[mb], est=est)
            E(eng, lambda e: e.tensor_tensor(m3, m3, xe3_[:, :, 2:TP + 2],
                                             ALU.add),
              reads=[mb, xnT_ext[par]], writes=[mb], est=est)
            return mb

        FB1 = [dict(xc=xc1, xT=xnT[i], tw=tw, twT=twT, gexp=gexp,
                    gsum=gsum, grec=grec, gate=gate) for i in range(2)]
        FB2 = dict(xc=xn3_tm, xT=xn3T, tw=tw2, twT=twT2, gexp=gexp2,
                   gsum=gsum2, grec=grec2, gate=gate2)


        def proj_stream(mslot, wbuf, dst, est_mm):
            for lo, hi in OC:
                w = hi - lo
                qs = psT_r[2:2 + (w + 255) // 256]
                fns = []
                for k in range(NCH):
                    fns.append(lambda e,
                               _l=mixT[mslot].ap[:, k * 128:(k + 1) * 128],
                               _r=wbuf.ap[:, k * C + lo:k * C + hi],
                               _d=psT.ap[:, 512:512 + w],
                               _st=(k == 0), _sp=(k == NCH - 1):
                               e.matmul(_d, _l, _r, start=_st, stop=_sp))
                E("t", fns, reads=[mixT[mslot], wbuf],
                  writes=qs, est=est_mm)
                E("s", lambda e, _lo=lo, _hi=hi, _w=w, _d=dst: e.copy(
                    _d.ap[:, _lo:_hi], psT.ap[:, 512:512 + _w]),
                  reads=qs, writes=[dst], est=550)

        def stage_a(p):
            real = p >= HP
            par = p % 2
            xb, x2b = xin[par], x2[par]
            kk8b, vsb, rsb, thTb = (kk8[par], v_sb[par], r_sb[par],
                                    thT[par])
            scur, sprev = states[par], states[1 - par]
            kpreb = kpre[par]
            wfmb = w_fm[par]
            sxTb = sxT[par]
            xe, xep = xe3v[par], xe3v[1 - par]
            lnA, lnB = lnst[0][par], lnst[1][par]
            E("s", lambda e: e.activation(xc1.ap, xb.ap, AF.Copy,
                                          bias=0.0, scale=one_c,
                                          accum_out=lnA["sum"].ap),
              reads=[xb, cc], writes=[xc1, lnA["sum"]], est=1150)
            E("s", lambda e: e.activation(xn_tm.ap, xb.ap, AF.Square,
                                          bias=zero_c, scale=one_c,
                                          accum_out=lnA["ssq"].ap),
              reads=[xb, cc], writes=[xn_tm, lnA["ssq"]], est=1300)
            stats_from_accum(lnA)
            ffn_fn(xb, x2b, lnA, FB1[par], psB, psB_r, psT, psT_r,
                   compact_aux=True)
            stats_from_bn(x2b, lnB, par)
            if p + 2 < NP:
                E("sx%d" % par, lambda e, _r=(p + 2) * TP, _b=xb:
                  e.dma_start(out=_b.ap, in_=x_in[_r:_r + TP, :]),
                  writes=[xb])
            E("v", lambda e: e.tensor_tensor(lnB["msq"].ap,
                                             lnB["mean"].ap,
                                             lnB["rstd"].ap, ALU.mult),
              reads=[lnB["mean"], lnB["rstd"]], writes=[lnB["msq"]],
              est=100)
            E("s", lambda e: e.activation(xn_tm.ap, x2b.ap, AF.Identity,
                                          bias=lnB["msq"].ap,
                                          scale=lnB["rstd"].ap),
              reads=[x2b, lnB["msq"], lnB["rstd"]], writes=[xn_tm],
              est=1150)
            if p > 0:
                E("v", lambda e: e.tensor_copy(xe[:, :, 1:2],
                                               xep[:, :, TP + 1:TP + 2]),
                  reads=[xnT_ext[1 - par]], writes=[xnT_ext[par]])
            nh2 = NCH // 2
            for r in range(2):
                fns = []
                for k in range(r * nh2, (r + 1) * nh2):
                    fns.append(lambda e,
                               _s=xn_tm.ap[:, k * 128:(k + 1) * 128],
                               _d=psT.ap[:, (k % nh2) * 128:
                                         (k % nh2 + 1) * 128]:
                               e.matmul(_d, _s, idb.ap, start=True,
                                        stop=True))
                E("t", fns, reads=[xn_tm, idb], writes=psT_r[0:2],
                  est=nh2 * 95)
                E("s", lambda e, _r=r, _n=nh2: e.copy(
                    xe[:, _r * _n:(_r + 1) * _n, 2:TP + 2],
                    psT.ap[:, 0:_n * 128].rearrange("p (k c) -> p k c",
                                                    k=_n)),
                  reads=psT_r[0:2], writes=[xnT_ext[par]], est=500)
            E("v", lambda e: e.tensor_tensor(
                sxTb.ap.rearrange("p (k c) -> p k c", k=NCH),
                xe[:, :, 2:TP + 2], xe[:, :, 1:TP + 1], ALU.subtract),
              reads=[xnT_ext[par]], writes=[sxTb], est=1250)
            fns = []
            for k in range(NCH):
                fns.append(lambda e, _l=xe[:, k, 2:TP + 2],
                           _r=w1cat.ap[:, k * NW1:(k + 1) * NW1],
                           _st=(k == 0):
                           e.matmul(psT.ap[:, 0:NW1], _l, _r, start=_st,
                                    stop=False))
                fns.append(lambda e, _l=sxTb.ap[:, k * 128:(k + 1) * 128],
                           _r=w1x.ap[:, k * NW1:(k + 1) * NW1],
                           _sp=(k == NCH - 1):
                           e.matmul(psT.ap[:, 0:NW1], _l, _r, start=False,
                                    stop=_sp))
            E("t", fns, reads=[xnT_ext[par], sxTb, w1cat, w1x],
              writes=[psT_r[0]], est=1300)
            E("s", lambda e: e.activation(th.ap, psT.ap[:, 0:NW1], AF.Tanh,
                                          bias=zero_c, scale=one_c),
              reads=[psT_r[0], cc], writes=[th], est=350)
            fns = []
            for m in range(4):
                fns.append(lambda e, _s=th.ap[:, m * DLOW:(m + 1) * DLOW],
                           _d=psT.ap[0:DLOW, m * TP:(m + 1) * TP]:
                           e.matmul(_d, _s, idb.ap, start=True, stop=True))
            E("t", fns, reads=[th, idb], writes=psT_r[0:2], est=450)
            E("s", lambda e, _t=thTb: e.copy(
                _t.ap[0:DLOW, :], psT.ap[0:DLOW, 0:4 * TP]),
              reads=psT_r[0:2], writes=[thTb], est=400)
            E("t", lambda e, _t=thTb: e.matmul(
                psT.ap[0:H, 0:TP], w2aug.ap, _t.ap[:, TP:2 * TP],
                start=True, stop=True),
              reads=[w2aug, thTb], writes=[psT_r[0]], est=200)
            E("s", lambda e, _w=wfmb: e.activation(_w.ap[0:H, :],
                                                   psT.ap[0:H, 0:TP],
                                                   AF.Tanh, bias=zero16,
                                                   scale=half16),
              reads=[psT_r[0], cc], writes=[wfmb], est=250)
            E("v", lambda e, _w=wfmb: e.tensor_scalar(_w.ap[0:H, :],
                                                      _w.ap[0:H, :],
                                                      1.0, 0.5 * E_HALF,
                                                      ALU.add, ALU.mult),
              reads=[wfmb], writes=[wfmb], est=200)
            E("t", lambda e, _t=thTb: e.matmul(
                psT.ap[:, 256:256 + H], _t.ap[:, 2 * TP:3 * TP], a2aug.ap,
                start=True, stop=True),
              reads=[thTb, a2aug], writes=[psT_r[1]], est=200)
            E("s", lambda e: e.activation(a_t.ap, psT.ap[:, 256:256 + H],
                                          AF.Tanh, bias=zero_c,
                                          scale=half_c),
              reads=[psT_r[1], cc], writes=[a_t], est=250)
            E("v", lambda e: e.tensor_scalar(a8.ap, a_t.ap, 1.0 / 16,
                                             1.0 / 16, ALU.mult, ALU.add),
              reads=[a_t], writes=[a8], est=170)
            mix_stream(0, 1, par)
            proj_stream(0, wk, kpreb, NCH * 512 * 0.42 + 150)
            mix_stream(1, 2, par)
            proj_stream(1, wv, vsb, NCH * 512 * 0.42 + 150)
            if real:
                mix_stream(2, 0, par)
                proj_stream(2, wr, rsb, NCH * 512 * 0.42 + 150)
            E("s", lambda e, _kk=kk8b: e.activation(_kk.ap, kpreb.ap,
                                                    AF.Square,
                                                    bias=zero_c,
                                                    scale=one_c),
              reads=[kpreb, cc], writes=[kk8b], est=1300)
            E("v", lambda e, _kk=kk8b: e.tensor_reduce(ssh.ap,
                                                       re3(_kk.ap, H), AX.X,
                                                       ALU.add),
              reads=[kk8b], writes=[ssh], est=1300)
            E("v", lambda e: e.tensor_scalar(sshe.ap, ssh.ap, 1e-24,
                                             1.0 / 64, ALU.max, ALU.mult),
              reads=[ssh], writes=[sshe], est=170)
            E("g", lambda e: e.tensor_tensor(rn.ap, sshe.ap, pwH, ALU.pow),
              reads=[sshe, powc], writes=[rn], est=3600)
            E("v", lambda e, _kk=kk8b, _kp=kpreb: e.tensor_tensor(
                re3(_kk.ap, H), re3(_kp.ap, H), bc3(rn.ap, H, 64),
                ALU.mult),
              reads=[kpreb, rn], writes=[kk8b], est=1250)
            E("v", lambda e: e.tensor_tensor(rna.ap, rn.ap, a8.ap,
                                             ALU.mult),
              reads=[rn, a8], writes=[rna], est=170)
            E("g", lambda e, _kp=kpreb: e.tensor_tensor(
                re3(kvp.ap, H), re3(_kp.ap, H), bc3(rna.ap, H, 64),
                ALU.mult),
              reads=[kpreb, rna], writes=[kvp], est=2500)
            E("v", lambda e, _v=vsb, _kv=kpreb: e.tensor_tensor(
                _kv.ap, kvp.ap, _v.ap, ALU.mult),
              reads=[kvp, vsb], writes=[kpreb], est=700)
            E("t", transpose_blocks(kpreb.ap, NCH, psC),
              reads=[kpreb, idb], writes=psC_r, est=NCH * 110)
            for lo, hi in OC:
                E("s", lambda e, _lo=lo, _hi=hi, _d=kpreb: e.copy(
                    _d.ap[:, _lo:_hi], psC.ap[:, _lo:_hi]),
                  reads=bq(psC_r, lo, hi), writes=[kpreb], est=550)
            for lo, hi in OC:
                fns = []
                for k in range(lo // 128, hi // 128):
                    fns.append(lambda e,
                               _l=eall.ap[:, k * 128:(k + 1) * 128],
                               _d=psC.ap[:, k * 128:(k + 1) * 128],
                               _w=wfmb:
                               e.matmul(_d, _l, _w.ap, start=True,
                                        stop=True))
                E("t", fns, reads=[eall, wfmb],
                  writes=bq(psC_r, lo, hi), est=(hi - lo) // 128 * 240)
            for k in range(NCH):
                lo = k * 128
                init = 0.0 if p == 0 else sprev.ap[:, lo + TP - 1:lo + TP]
                E("v", lambda e, _lo=lo, _i=init, _sc=scur, _kf=kpreb:
                  e.tensor_tensor_scan(
                    _sc.ap[:, _lo:_lo + TP], psC.ap[:, _lo:_lo + TP],
                    _kf.ap[:, _lo:_lo + TP], _i, ALU.mult, ALU.add),
                  reads=[psC_r[k * 128 // 256], kpreb] +
                        ([] if p == 0 else [sprev]),
                  writes=[scur], est=475)

        def stage_b(p):
            par = p % 2
            x2b = x2[par]
            kk8b, vsb, rsb, thTb = (kk8[par], v_sb[par], r_sb[par],
                                    thT[par])
            scur = states[par]
            lnC = lnst[2][par]
            fns = []
            for ci, (lo, hi) in enumerate(OC):
                fns.append(lambda e, _t=thTb, _r=g2z.ap[:, lo:hi],
                           _d=psW.ap[:, lo:hi]:
                           e.matmul(_d, _t.ap[:, 3 * TP:4 * TP], _r,
                                    start=True, stop=True))
            E("t", fns, reads=[thTb, g2z], writes=psW_r, est=500)
            E("s", lambda e: e.activation(tg_sb.ap, psW.ap[:, 0:C],
                                          AF.Tanh, bias=zero_c,
                                          scale=half_c),
              reads=psW_r + [cc], writes=[tg_sb], est=1100)
            fns = []
            for ci, (lo, hi) in enumerate(OC):
                fns.append(lambda e, _t=thTb, _r=v2aug.ap[:, lo:hi],
                           _d=psW.ap[:, lo:hi]:
                           e.matmul(_d, _t.ap[:, 0:TP], _r,
                                    start=True, stop=True))
            E("t", fns, reads=[thTb, v2aug], writes=psW_r, est=500)
            E("s", lambda e: e.activation(vgt.ap, psW.ap[:, 0:C], AF.Tanh,
                                          bias=zero_c, scale=half_c),
              reads=psW_r + [cc], writes=[vgt], est=1100)
            E("s", lambda e, _s=scur: e.copy(stb.ap, _s.ap),
              reads=[scur], writes=[stb], est=950)
            E("t", transpose_blocks(stb.ap, NCH, psC),
              reads=[stb, idb], writes=psC_r, est=NCH * 110)
            E("v", lambda e, _r=rsb: e.tensor_tensor(
                sc1.ap, psC.ap[:, 0:C], _r.ap, ALU.mult),
              reads=psC_r + [rsb], writes=[sc1], est=1350)
            E("g", lambda e, _r=rsb, _kk=kk8b: e.tensor_tensor(
                rkk.ap, _r.ap, _kk.ap, ALU.mult),
              reads=[rsb, kk8b], writes=[rkk], est=2300)
            E("g", lambda e: e.tensor_tensor(rkkr.ap, rkk.ap, rkt.ap,
                                             ALU.mult),
              reads=[rkk, rkt], writes=[rkkr], est=2300)
            E("v", lambda e: e.tensor_reduce(dot.ap, re3(rkkr.ap, H), AX.X,
                                             ALU.add),
              reads=[rkkr], writes=[dot], est=1300)
            E("g", lambda e, _v=vsb: e.tensor_tensor(
                re3(bon.ap, H), re3(_v.ap, H), bc3(dot.ap, H, 64), ALU.mult),
              reads=[vsb, dot], writes=[bon], est=2500)
            E("v", lambda e: e.tensor_tensor(xwkv.ap, sc1.ap, bon.ap,
                                             ALU.add),
              reads=[sc1, bon], writes=[xwkv], est=1250)
            x3v = re3(xwkv.ap, H)
            E("s", lambda e, _kk=kk8b: e.activation(_kk.ap, xwkv.ap,
                                                    AF.Square, bias=zero_c,
                                                    scale=one_c),
              reads=[xwkv, cc], writes=[kk8b], est=1300)
            E("v", lambda e: e.tensor_reduce(gs1.ap, x3v, AX.X, ALU.add),
              reads=[xwkv], writes=[gs1], est=1300)
            E("v", lambda e, _kk=kk8b: e.tensor_reduce(
                gaM.ap, re3(_kk.ap, H), AX.X, ALU.add),
              reads=[kk8b], writes=[gaM], est=1300)
            E("v", lambda e: e.tensor_scalar(gm.ap, gs1.ap, 1.0 / 64, None,
                                             ALU.mult),
              reads=[gs1], writes=[gm], est=170)
            E("v", lambda e: e.tensor_tensor(gd2.ap, gm.ap, gm.ap,
                                             ALU.mult),
              reads=[gm], writes=[gd2], est=170)
            E("v", lambda e: e.scalar_tensor_tensor(gbe.ap, gaM.ap,
                                                    1.0 / 64, gd2.ap,
                                                    ALU.mult, ALU.subtract),
              reads=[gaM, gd2], writes=[gbe], est=170)
            E("v", lambda e: e.tensor_scalar(gbe.ap, gbe.ap, EPS, None,
                                             ALU.add),
              reads=[gbe], writes=[gbe], est=170)
            E("g", lambda e: e.tensor_tensor(grs.ap, gbe.ap, pwH, ALU.pow),
              reads=[gbe, powc], writes=[grs], est=3600)
            E("v", lambda e: e.tensor_tensor(re3(u1.ap, H), x3v,
                                             bc3(gm.ap, H, 64),
                                             ALU.subtract),
              reads=[xwkv, gm], writes=[u1], est=1250)
            E("g", lambda e: e.tensor_tensor(re3(u2.ap, H), re3(u1.ap, H),
                                             bc3(grs.ap, H, 64), ALU.mult),
              reads=[u1, grs], writes=[u2], est=2500)
            E("v", lambda e: e.tensor_tensor(u3.ap, u2.ap, lblw.ap, ALU.add),
              reads=[u2, lblw], writes=[u3], est=700)
            E("v", lambda e: e.scalar_tensor_tensor(xg2.ap, tg_sb.ap, 1.0,
                                                    u3.ap, ALU.add,
                                                    ALU.mult),
              reads=[tg_sb, u3], writes=[xg2], est=900)
            E("t", transpose_blocks(xg2.ap, NCH, psC),
              reads=[xg2, idb], writes=psC_r, est=NCH * 95)
            E("s", lambda e: e.copy(xg2T.ap, psC.ap[:, 0:C]),
              reads=psC_r, writes=[xg2T], est=620)
            for lo, hi in OC:
                fns = []
                for k in range(NCH):
                    fns.append(lambda e,
                               _l=xg2T.ap[:, k * 128:(k + 1) * 128],
                               _r=wo.ap[:, k * C + lo:k * C + hi],
                               _d=psW.ap[:, lo:hi],
                               _st=(k == 0), _sp=(k == NCH - 1):
                               e.matmul(_d, _l, _r, start=_st, stop=_sp))
                E("t", fns, reads=[xg2T, wo], writes=bq(psW_r, lo, hi),
                  est=NCH * (hi - lo) * 0.42 + 150)
            E("v", lambda e, _x2=x2b: e.scalar_tensor_tensor(
                x3.ap, psW.ap[:, 0:C], 1.0, _x2.ap, ALU.mult, ALU.add),
              reads=psW_r + [x2b], writes=[x3], est=1300)
            stats_from_bn(x3, lnC, par)
            ffn_fn(x3, xout, lnC, FB2, psW, psW_r, psC, psC_r)
            E("sst", lambda e, _r=(p - HP) * TP, _x=xout:
              e.dma_start(out=y_out[_r:_r + TP, :], in_=_x.ap),
              reads=[xout], est=2500)

        for p in range(NP):
            stage_a(p)
            if p >= HP:
                stage_b(p)

        P.schedule()
        with nc.Block() as block:
            @block.vector
            def _(v):
                for op in P.seq_ops["v"]:
                    op(v)

            @block.scalar
            def _(s):
                for op in P.seq_ops["s"]:
                    op(s)

            @block.tensor
            def _(t):
                for op in P.seq_ops["t"]:
                    op(t)

            @block.gpsimd
            def _(g):
                for op in P.seq_ops["g"]:
                    op(g)

            @block.sync
            def _(sy):
                for op in P.seq_ops["sy"]:
                    op(sy)

    return nc


def prep_shared(inp, C, pre="L0_attn_", fpre="L0_ffn_"):
    NCH = C // 128
    H = C // 64
    a = {k: np.asarray(v) for k, v in inp.items()}
    f = np.float64 if any(v.dtype == np.float64 for v in a.values()) \
        else np.float32
    w0 = a[pre + "w0"].reshape(1, H)
    a0 = a[pre + "a0"].reshape(1, H)
    v0 = a[pre + "v0"].reshape(1, C)
    colsplit = lambda v: np.ascontiguousarray(v.reshape(NCH, 128).T)
    tile = lambda v: np.ascontiguousarray(np.tile(v.reshape(1, C), (128, 1)))
    shared = {
        "wtT": np.ascontiguousarray(a[fpre + "weight"].T),
        "wr": np.ascontiguousarray(a[pre + "Wr"]),
        "wk": np.ascontiguousarray(a[pre + "Wk"]),
        "wv": np.ascontiguousarray(a[pre + "Wv"]),
        "wo": np.ascontiguousarray(a[pre + "Wo"]),
        "ug": np.ascontiguousarray(np.concatenate(
            [a[fpre + "lora_U"].reshape(ERK, C).T, a[fpre + "gate_w"].T],
            axis=1)),
        "vcat": np.ascontiguousarray(
            a[fpre + "lora_V"].transpose(0, 2, 1).reshape(ERK, C)),
        "w1cat": np.ascontiguousarray(np.concatenate(
            [a[pre + "v1"], a[pre + "w1"], a[pre + "a1"], a[pre + "g1"]],
            axis=1)),
        "w2aug": np.ascontiguousarray(np.concatenate([a[pre + "w2"], w0], 0)),
        "a2aug": np.ascontiguousarray(np.concatenate([a[pre + "a2"], a0], 0)),
        "v2aug": np.ascontiguousarray(np.concatenate([a[pre + "v2"], v0], 0)),
        "g2z": np.ascontiguousarray(np.concatenate(
            [a[pre + "g2"], np.zeros((1, C), f)], 0)),
        "xmix": np.ascontiguousarray(np.concatenate(
            [colsplit(a[pre + "x_" + s]) for s in "rkvwag"], axis=1)),
        "kkt": tile(a[pre + "k_k"]),
        "lnwc": colsplit(a[pre + "ln_w"]),
        "lnw_tile": tile(a[pre + "ln_w"]),
        "lnb_tile": tile(a[pre + "ln_b"]),
        "rk_tile": tile(a[pre + "r_k"]),
        "identb": np.eye(128, dtype=ml_dtypes.bfloat16),
        "identf": np.eye(128, dtype=f),
        "ones512": np.ones((1, 4 * TP), dtype=ml_dtypes.bfloat16),
    }
    eall = np.zeros((128, C), f)
    for k in range(NCH):
        for pp in range(128):
            eall[2 * k + pp // 64, k * 128 + pp] = 1.0
    shared["eall"] = eall
    cconst = np.zeros((128, 4), f)
    cconst[:, CC_ONE] = 1.0
    cconst[:, CC_ZERO] = 0.0
    cconst[:, CC_HALF] = 0.5
    shared["cconst"] = cconst
    return shared


def make_in_maps(x, shared, C, HP, RP, n_seq_shards=4):
    B, T, _ = x.shape
    NP = HP + RP
    per = RP * TP
    halo = HP * TP
    in_maps = []
    for core in range(B * n_seq_shards):
        b, q = core // n_seq_shards, core % n_seq_shards
        start = q * per - halo
        rows = np.zeros((NP * TP, C), x.dtype)
        lo = max(start, 0)
        rows[lo - start:, :] = x[b, lo:q * per + per, :]
        m = dict(shared)
        m["x"] = rows
        in_maps.append(m)
    return in_maps


_NC_CACHE = {}


def kernel(**inputs):
    C, HP, RP = 1024, 1, 16
    x = np.asarray(inputs["L0_attn_x"])
    B, T, _ = x.shape
    per = RP * TP
    shared = prep_shared(inputs, C)
    key = (C, HP, RP)
    if key not in _NC_CACHE:
        _NC_CACHE[key] = build_nc(C, HP, RP)
    nc = _NC_CACHE[key]
    in_maps = make_in_maps(x, shared, C, HP, RP)
    res = run_bass_kernel_spmd(nc, in_maps, core_ids=list(range(8)))
    odt = np.asarray(res.results[0]["y"]).dtype
    out = np.empty((B, T, C), np.float32 if odt == np.float32 else odt)
    for core in range(8):
        b, q = core // 4, core % 4
        out[b, q * per:(q + 1) * per, :] = res.results[core]["y"]
    return out
